# version 28
# baseline (speedup 1.0000x reference)
import contextlib
import numpy as np

import concourse.bass as bass
import concourse.mybir as mybir
from concourse.bass_utils import run_bass_kernel_spmd

F32 = mybir.dt.float32
BF16 = mybir.dt.bfloat16
I32 = mybir.dt.int32
AF = mybir.ActivationFunctionType
ALU = mybir.AluOpType

NCORES = 8
SEQ = 2048
NSEQ = 2
D = 1024
NJ = 8
H = 16
HD = 64
FF = 3072
NM = 24
TOKENS = NSEQ * SEQ
TILE = 512
NTILE = TOKENS // TILE
CPT = TILE // 128
INV127 = 1.0 / 127.0
EPS_ATTN = 1e-6
EPS_LN = 1e-5
PRELOAD = False
WKEEP = 3
PRELOAD_SLABS = True
BULKQ = "sync"
WQ = "sync"
XQ = "sync"


def _merge(d, tok):
    if tok is None:
        return
    sem, v = tok
    if d.get(sem, 0) < v:
        d[sem] = v


class Track:
    def __init__(self):
        self.ready = {}
        self.readers = {}

    def begin_write(self):
        w = dict(self.readers)
        for s, v in self.ready.items():
            if w.get(s, 0) < v:
                w[s] = v
        return w

    def end_write(self, tok):
        self.ready = {}
        self.readers = {}
        _merge(self.ready, tok)

    def add_write(self, tok):
        _merge(self.ready, tok)

    def reads(self):
        return dict(self.ready)

    def add_reader(self, tok):
        _merge(self.readers, tok)


class Prog:
    ENGS = ("sync", "tensor", "vector", "scalar", "gpsimd")

    def __init__(self, nc, stack):
        self.nc = nc
        self.sems = {}
        self.count = {}
        for e in ("tensor", "vector", "scalar", "gpsimd"):
            self.sems[e] = stack.enter_context(nc.semaphore(f"sem_{e}"))
            self.count[e] = 0
        self.dma_sems = {}
        self.dma_count = {}
        self.stack = stack
        self._reset_block()

    def _reset_block(self):
        self.ops = {e: [] for e in self.ENGS}
        self.waited = {e: {} for e in self.ENGS}

    def dma_slot(self, name):
        if name not in self.dma_sems:
            self.dma_sems[name] = self.stack.enter_context(
                self.nc.semaphore(f"dsl_{name}"))
            self.dma_count[name] = 0
        return name

    def _gather(self, waits):
        wd = {}
        for w in waits:
            if w is None:
                continue
            if isinstance(w, dict):
                for s, v in w.items():
                    _merge(wd, (s, v))
            else:
                _merge(wd, w)
        return wd

    def _emit_waits(self, eng_name, wd):
        out = []
        seen = self.waited[eng_name]
        for sem, val in sorted(wd.items(), key=lambda kv: kv[0].num):
            if seen.get(sem, 0) < val:
                seen[sem] = val
                out.append((sem, val))
        return out

    def op(self, eng_name, fn, waits=(), inc=True):
        real_waits = self._emit_waits(eng_name, self._gather(waits))
        tok = None
        sem = self.sems[eng_name]
        if inc:
            self.count[eng_name] += 1
            tok = (sem, self.count[eng_name])

        def closure(eng, fn=fn, real_waits=real_waits, tok=tok, sem=sem):
            for s, v in real_waits:
                eng.wait_ge(s, v)
            inst = fn(eng)
            if tok is not None:
                inst.then_inc(sem, 1)

        self.ops[eng_name].append(closure)
        return tok

    def dma(self, slot, fn, waits=(), eng="sync"):
        slot = self.dma_slot(slot)
        wd = self._gather(waits)
        sem = self.dma_sems[slot]
        prev = self.dma_count[slot]
        if prev:
            _merge(wd, (sem, prev))
        real_waits = self._emit_waits(eng, wd)
        self.dma_count[slot] = prev + 16
        tok = (sem, self.dma_count[slot])

        def closure(e, fn=fn, real_waits=real_waits, sem=sem):
            for s, v in real_waits:
                e.wait_ge(s, v)
            fn(e).then_inc(sem, 16)

        self.ops[eng].append(closure)
        return tok

    def sync_wait(self, waits):
        real_waits = self._emit_waits("sync", self._gather(waits))

        def closure(eng, real_waits=real_waits):
            for s, v in real_waits:
                eng.wait_ge(s, v)

        self.ops["sync"].append(closure)

    def flush_block(self):
        nc = self.nc
        ops = self.ops
        with nc.Block() as blk:
            if ops["sync"]:
                blk.sync(lambda e, fns=ops["sync"]: [f(e) for f in fns])
            if ops["tensor"]:
                blk.tensor(lambda e, fns=ops["tensor"]: [f(e) for f in fns])
            if ops["vector"]:
                blk.vector(lambda e, fns=ops["vector"]: [f(e) for f in fns])
            if ops["scalar"]:
                blk.scalar(lambda e, fns=ops["scalar"]: [f(e) for f in fns])
            if ops["gpsimd"]:
                blk.gpsimd(lambda e, fns=ops["gpsimd"]: [f(e) for f in fns])
        self._reset_block()


def build_program(n_seq=NSEQ, n_tiles=NTILE // NSEQ, stop_after=None):
    nc = bass.Bass("TRN2", target_bir_lowering=False,
                   detect_race_conditions=False, dynamic_dma_scratch_size=2048)

    x_hbm = nc.dram_tensor("x", [TOKENS, D], F32, kind="ExternalInput")
    w_imgs = {n: nc.dram_tensor(n, [128, NJ * D], F32, kind="ExternalInput")
              for n in ("wq", "wk", "wv", "wo")}
    wi_imgs = {n: nc.dram_tensor(n, [NM, 128, D], I32, kind="ExternalInput")
               for n in ("wgi", "wui", "wdi")}
    out_hbm = nc.dram_tensor("out", [TOKENS, D], F32, kind="ExternalOutput")
    scr = {n: nc.dram_tensor(f"scr_{n}", [NM, 128, D], BF16)
           for n in ("wg", "wu", "wd")}
    scr_qkvo = {n: nc.dram_tensor(f"scr_{n}", [128, NJ * D], BF16)
                for n in ("wq", "wk", "wv", "wo")}
    dbg_hbm = None
    if stop_after is not None:
        dbg_hbm = nc.dram_tensor("dbg", [128, 8192], BF16, kind="ExternalOutput")

    with contextlib.ExitStack() as ctx:
        E = ctx.enter_context
        wpair = E(nc.sbuf_tensor("wpair", [128, 2, NJ, D], BF16))
        wd_buf = E(nc.sbuf_tensor("wd_buf", [128, NM, D], BF16))
        wgu = E(nc.sbuf_tensor("wgu", [128, 2, 4, D], BF16))
        h_fm = E(nc.sbuf_tensor("h_fm", [128, NM, TILE], BF16))
        ln1_bf = E(nc.sbuf_tensor("ln1_bf", [128, CPT + 1, D], BF16))
        ln1T = E(nc.sbuf_tensor("ln1T", [128, NJ, TILE], BF16))
        x_f32 = E(nc.sbuf_tensor("x_f32", [128, 2, D], F32))
        x_bf = E(nc.sbuf_tensor("x_bf", [128, CPT, D], BF16))
        xT = E(nc.sbuf_tensor("xT", [128, NJ, TILE], BF16))
        q_fm = E(nc.sbuf_tensor("q_fm", [128, NJ, TILE], BF16))
        k_tm = E(nc.sbuf_tensor("k_tm", [128, 2, D], BF16))
        v_tm = E(nc.sbuf_tensor("v_tm", [128, 2, H, HD + 1], BF16))
        kv_acc = E(nc.sbuf_tensor("kv_acc", [128, NJ, HD + 1], F32))
        kvbd = E(nc.sbuf_tensor("kvbd", [128, NJ, 2 * (HD + 1)], BF16))
        attn_bf = E(nc.sbuf_tensor("attn_bf", [128, 2, D], BF16))
        attnT = E(nc.sbuf_tensor("attnT", [128, CPT, NJ, 128], BF16))
        t1 = E(nc.sbuf_tensor("t1", [128, D], BF16))
        texp = E(nc.sbuf_tensor("texp", [128, D], BF16))
        res = E(nc.sbuf_tensor("res", [128, 2, D], F32))
        stats = E(nc.sbuf_tensor("stats", [128, 2 * CPT, 8], F32))
        zr = E(nc.sbuf_tensor("zr", [128, CPT, 2, NJ, 2], F32))
        outb = E(nc.sbuf_tensor("outb", [128, 2, D], F32))
        silu_b = E(nc.sbuf_tensor("silu_b", [128, 2, TILE], BF16))
        consts = E(nc.sbuf_tensor("consts", [128, 4], F32))
        ps_a = E(nc.psum_tensor("ps_a", [128, 1024], F32))
        ps_b = E(nc.psum_tensor("ps_b", [128, 1024], F32))
        ps_c = E(nc.psum_tensor("ps_c", [128, 2048], F32))

        P = Prog(nc, ctx)

        BK = {k: Track() for k in ("a0", "a1", "b0", "b1", "c0", "c1", "c2", "c3")}

        def bw(keys):
            return [BK[k].begin_write() for k in keys]

        def brd(keys):
            return [BK[k].reads() for k in keys]

        def bend(keys, tok):
            for k in keys:
                BK[k].end_write(tok)

        def bread(keys, tok):
            for k in keys:
                BK[k].add_reader(tok)

        kv_view = ps_c[:].rearrange("p (r c e) -> p r c e", r=2, c=NJ)
        out_view = ps_c[:].rearrange("p (c e) -> p c e", c=NJ)
        outa_view = ps_a[:].rearrange("p (c e) -> p c e", c=CPT)
        d_view = ps_c[:].rearrange("p (r e) -> p r e", r=2)

        tr_x = [Track() for _ in range(2)]
        tr_xbf = [Track() for _ in range(CPT)]
        tr_xT = [Track() for _ in range(CPT)]
        tr_wpair = [Track() for _ in range(2)]
        tr_ktm = [Track() for _ in range(2)]
        tr_vtm = [Track() for _ in range(2)]
        tr_kvacc = Track()
        tr_kvbd = Track()
        tr_q = Track()
        tr_attn = [Track() for _ in range(2)]
        tr_attnT = [Track() for _ in range(CPT)]
        tr_t1 = Track()
        tr_texp = Track()
        tr_sqs = tr_texp
        tr_res = [Track() for _ in range(2)]
        tr_stats = [Track() for _ in range(2 * CPT)]
        tr_zr = [Track() for _ in range(CPT)]
        tr_outb = [Track() for _ in range(2)]
        tr_ln1 = [Track() for _ in range(CPT + 1)]
        tr_ln1T = Track()
        tr_h = [Track() for _ in range(NM)]
        tr_wgu = [[Track() for _ in range(4)] for _ in range(2)]
        tr_wd = [Track() for _ in range(NM)]
        tr_silu = [Track() for _ in range(2)]
        tr_stgi = [Track() for _ in range(4)]
        tr_stgb = [Track() for _ in range(4)]
        tr_scr = {n: [Track() for _ in range(NM)] for n in ("wg", "wu")}
        tr_scr.update({n: [Track() for _ in range(NJ)]
                       for n in ("wq", "wk", "wv", "wo")})
        tr_consts = Track()

        tok = P.op("vector", lambda v: v.memset(consts[:, 0:1], EPS_LN))
        tr_consts.add_write(tok)
        tok = P.op("vector", lambda v: v.memset(consts[:, 1:2], -0.5))
        tr_consts.add_write(tok)
        tok = P.op("vector", lambda v: v.memset(consts[:, 2:3], INV127))
        tr_consts.add_write(tok)
        for r in range(2):
            tok = P.op("vector", lambda v, r=r: v.memset(v_tm[:, r, :, HD:HD + 1], 1.0))
            tr_vtm[r].add_write(tok)

        stg_i32 = [outb[:, 0, :].bitcast(I32), outb[:, 1, :].bitcast(I32),
                   wgu[:, 0, :, :].rearrange("p a b -> p (a b)").bitcast(I32)[:, 0:D],
                   wgu[:, 1, :, :].rearrange("p a b -> p (a b)").bitcast(I32)[:, 0:D]]
        stg_bf = [h_fm[:, 2 * i:2 * i + 2, :].rearrange("p a b -> p (a b)")
                  for i in range(4)]

        w_jobs = []
        for mat in ("wq", "wo"):
            for j in range(NJ):
                w_jobs.append(("f32", mat, w_imgs[mat], j))
        for m in range(NM):
            w_jobs.append(("wdd", "wd", wi_imgs["wdi"], m))
        for mat, si in (("wk", 0), ("wv", 1)):
            w_jobs.append(("pair_st", mat, si, 0))
        w_pos = [0]
        w_stores = []

        def flush_w_stores(keep=0):
            while len(w_stores) > keep:
                w_stores.pop(0)()

        def drain_w_jobs(k):
            for _ in range(k):
                if w_pos[0] >= len(w_jobs):
                    flush_w_stores()
                    return
                kind, mat, src_t, m = w_jobs[w_pos[0]]
                r = w_pos[0] % 4
                w_pos[0] += 1
                if kind == "pair_st":
                    si = src_t
                    st = P.dma(f"pst{si}",
                               lambda s, mat=mat, si=si:
                               s.dma_start(scr_qkvo[mat][:],
                                           wpair[:, si, :, :].rearrange(
                                               "p a b -> p (a b)")),
                               waits=[tr_wpair[si].reads()], eng=BULKQ)
                    tr_wpair[si].add_reader(st)
                    for t in tr_scr[mat]:
                        t.add_write(st)
                    continue
                if kind == "int" or kind == "wdd":
                    extra = [tr_outb[r].begin_write()] if r < 2 else []
                    ld = P.dma(f"wi{r}",
                               lambda s, src_t=src_t, m=m, r=r:
                               s.dma_start(stg_i32[r], src_t[m]),
                               waits=[tr_stgi[r].begin_write()] + extra,
                               eng=WQ)
                    tr_stgi[r].end_write(ld)
                    if r < 2:
                        tr_outb[r].end_write(ld)
                    if kind == "wdd":
                        ct = P.op("vector" if r % 2 == 0 else "gpsimd",
                                  lambda e, r=r, m=m:
                                  e.tensor_copy(wd_buf[:, m, :], stg_i32[r]),
                                  waits=[ld, tr_wd[m].begin_write()])
                        tr_stgi[r].add_reader(ct)
                        if r < 2:
                            tr_outb[r].add_reader(ct)
                        tr_wd[m].end_write(ct)
                        continue
                    ct = P.op("vector",
                              lambda e, r=r: e.tensor_copy(stg_bf[r], stg_i32[r]),
                              waits=[ld, tr_stgb[r].begin_write()])
                    tr_stgi[r].add_reader(ct)
                    if r < 2:
                        tr_outb[r].add_reader(ct)
                    tr_stgb[r].end_write(ct)
                    def mk_store(mat=mat, m=m, r=r, ct=ct):
                        st = P.dma(f"scw{r}",
                                   lambda s, mat=mat, m=m, r=r:
                                   s.dma_start(scr[mat][m], stg_bf[r]),
                                   waits=[ct], eng=BULKQ)
                        tr_stgb[r].add_reader(st)
                        tr_scr[mat][m].add_write(st)
                    w_stores.append(mk_store)
                    flush_w_stores(keep=WKEEP)
                else:
                    rr2 = w_pos[0] % 2
                    ld = P.dma(f"wf{rr2}",
                               lambda s, src_t=src_t, m=m, rr2=rr2:
                               s.dma_start(res[:, rr2, :], src_t[:, m * D:(m + 1) * D]),
                               waits=[tr_res[rr2].begin_write()], eng=WQ)
                    tr_res[rr2].end_write(ld)
                    ct = P.op("scalar",
                              lambda e, r=r, rr2=rr2:
                              e.activation(stg_bf[r], res[:, rr2, :], AF.Copy),
                              waits=[ld, tr_stgb[r].begin_write()])
                    tr_res[rr2].add_reader(ct)
                    tr_stgb[r].end_write(ct)
                    def mk_store2(mat=mat, m=m, r=r, ct=ct):
                        st = P.dma(f"scw{r}",
                                   lambda s, mat=mat, m=m, r=r:
                                   s.dma_start(scr_qkvo[mat][:, m * D:(m + 1) * D],
                                               stg_bf[r]),
                                   waits=[ct], eng=BULKQ)
                        tr_stgb[r].add_reader(st)
                        tr_scr[mat][m].add_write(st)
                    w_stores.append(mk_store2)
                    flush_w_stores(keep=WKEEP)

        def load_pair_direct(m0, m1):
            for si, mat in enumerate((m0, m1)):
                fsrc = w_imgs[mat]
                for j in range(NJ):
                    r = j % 2
                    ld = P.dma(f"wp{si}",
                               lambda s, fsrc=fsrc, j=j, r=r:
                               s.dma_start(x_f32[:, r, :], fsrc[:, j * D:(j + 1) * D]),
                               waits=[tr_x[r].begin_write()], eng=BULKQ)
                    tr_x[r].end_write(ld)
                    ct = P.op("scalar",
                              lambda e, si=si, j=j, r=r:
                              e.activation(wpair[:, si, j, :], x_f32[:, r, :], AF.Copy),
                              waits=[ld, tr_wpair[si].begin_write() if j == 0 else None])
                    tr_x[r].add_reader(ct)
                    if j == 0:
                        tr_wpair[si].end_write(ct)
                    else:
                        tr_wpair[si].add_write(ct)

        def load_pair(m0, m1):
            for si, mat in enumerate((m0, m1)):
                ld = P.dma(f"wp{si}",
                           lambda s, si=si, mat=mat:
                           s.dma_start(wpair[:, si, :, :].rearrange("p a b -> p (a b)"),
                                       scr_qkvo[mat][:]),
                           waits=[tr_wpair[si].begin_write()]
                                 + [t.reads() for t in tr_scr[mat]],
                           eng=BULKQ)
                tr_wpair[si].end_write(ld)

        xring = [0]

        def load_x_ld(tok0):
            xr = xring[0] % 2
            xring[0] += 1
            ld = P.dma(f"x{xr}",
                       lambda s, tok0=tok0, xr=xr:
                       s.dma_start(x_f32[:, xr, :], x_hbm[tok0:tok0 + 128, :]),
                       waits=[tr_x[xr].begin_write()], eng=XQ)
            tr_x[xr].end_write(ld)
            return (ld, xr)

        def load_x_cast(ent, tq):
            ld, xr = ent
            ct = P.op("scalar",
                      lambda e, xr=xr, tq=tq:
                      e.activation(x_bf[:, tq, :], x_f32[:, xr, :], AF.Copy),
                      waits=[ld, tr_xbf[tq].begin_write()])
            tr_x[xr].add_reader(ct)
            tr_xbf[tq].end_write(ct)
            return ct

        def load_x_tp(ct, tq):
            tp = P.dma(f"xt{tq % 2}",
                       lambda s, tq=tq:
                       s.dma_start(xT[:, :, tq * 128:(tq + 1) * 128],
                                   x_bf[:, tq, :], transpose=True),
                       waits=[ct, tr_xT[tq].begin_write()])
            tr_xbf[tq].add_reader(tp)
            tr_xT[tq].end_write(tp)

        def load_x_fin(ent, tq):
            load_x_tp(load_x_cast(ent, tq), tq)

        def dbg_dump(src_ap, n, waits=()):
            tk = P.dma("dbg", lambda sy, src_ap=src_ap, n=n:
                       sy.dma_start(dbg_hbm[:, 0:n], src_ap), waits=waits)
            P.sync_wait([tk])

        def phase1(s, first):
            if first:
                load_pair_direct("wk", "wv")
            NCH = SEQ // 128
            pend = {}

            def emit_kv(u):
                kr, kvr, t_k, t_v = pend.pop(u)
                cbk = ("c0", "c1") if kvr == 0 else ("c2", "c3")
                for h in range(H):
                    last = (h == H - 1)
                    p0 = (h % 2) * 64
                    tok = P.op(
                        "tensor",
                        lambda t, h=h, kr=kr, p0=p0, kvr=kvr:
                        t.matmul(kv_view[p0:p0 + 64, kvr, h // 2, 0:HD + 1],
                                 k_tm[:, kr, h * HD:(h + 1) * HD],
                                 v_tm[:, kr, h, :],
                                 start=True, stop=True,
                                 tile_position=(0, p0)),
                        waits=[t_k, t_v] + (bw(cbk) if h == 0 else []),
                        inc=last)
                bend(cbk, tok)
                tr_ktm[kr].add_reader(tok)
                tr_vtm[kr].add_reader(tok)
                if u == 0:
                    acc = P.op("vector",
                               lambda v, kvr=kvr:
                               v.tensor_copy(kv_acc[:], kv_view[:, kvr, :, 0:HD + 1]),
                               waits=[tok, tr_kvacc.begin_write()])
                else:
                    acc = P.op("vector",
                               lambda v, kvr=kvr:
                               v.tensor_tensor(kv_acc[:], kv_acc[:],
                                               kv_view[:, kvr, :, 0:HD + 1], ALU.add),
                               waits=[tok, tr_kvacc.reads()])
                bread(cbk, acc)
                tr_kvacc.end_write(acc)

            order = [(u + CPT) % NCH for u in range(NCH)]
            cts = {0: load_x_cast(load_x_ld(s * SEQ + order[0] * 128),
                                  order[0] % CPT)}
            load_x_tp(cts.pop(0), order[0] % CPT)
            for u in range(NCH):
                if u + 1 < NCH:
                    cts[u + 1] = load_x_cast(
                        load_x_ld(s * SEQ + order[u + 1] * 128),
                        order[u + 1] % CPT)
                tq = order[u] % CPT
                kr = u % 2
                kvr = u % 2
                if first:
                    drain_w_jobs(4)

                kps = ps_a if u % 2 == 0 else ps_b
                vps = ps_b if u % 2 == 0 else ps_a
                kbk = ("a0", "a1") if u % 2 == 0 else ("b0", "b1")
                vbk = ("b0", "b1") if u % 2 == 0 else ("a0", "a1")
                for j in range(NJ):
                    for hl in range(2):
                        last = (j == NJ - 1 and hl == 1)
                        tok = P.op(
                            "tensor",
                            lambda t, j=j, hl=hl, tq=tq, kps=kps:
                            t.matmul(kps[:, hl * 512:(hl + 1) * 512],
                                     xT[:, j, tq * 128:(tq + 1) * 128],
                                     wpair[:, 0, j, hl * 512:(hl + 1) * 512],
                                     start=(j == 0), stop=(j == NJ - 1)),
                            waits=[tr_xT[tq].reads(), tr_wpair[0].reads()]
                                  + (bw(kbk) if (j == 0 and hl == 0) else []),
                            inc=last)
                bend(kbk, tok)
                for j in range(NJ):
                    for hl in range(2):
                        last = (j == NJ - 1 and hl == 1)
                        tok = P.op(
                            "tensor",
                            lambda t, j=j, hl=hl, tq=tq, vps=vps:
                            t.matmul(vps[:, hl * 512:(hl + 1) * 512],
                                     xT[:, j, tq * 128:(tq + 1) * 128],
                                     wpair[:, 1, j, hl * 512:(hl + 1) * 512],
                                     start=(j == 0), stop=(j == NJ - 1)),
                            waits=[tr_xT[tq].reads(), tr_wpair[1].reads()]
                                  + (bw(vbk) if (j == 0 and hl == 0) else []),
                            inc=last)
                bend(vbk, tok)
                tr_xT[tq].add_reader(tok)
                tr_wpair[0].add_reader(tok)
                tr_wpair[1].add_reader(tok)

                if u > 0:
                    emit_kv(u - 1)

                t_min = P.op("vector",
                             lambda v, kps=kps: v.tensor_scalar_min(t1[:], kps[:], 0.0),
                             waits=brd(kbk) + [tr_t1.begin_write()])
                bread(kbk, t_min)
                tr_t1.end_write(t_min)
                t_v = P.op("scalar",
                           lambda e, kr=kr, vps=vps:
                           e.activation(v_tm[:, kr, :, 0:HD],
                                        vps[:].rearrange("p (h e) -> p h e", h=H),
                                        AF.Copy),
                           waits=brd(vbk) + [tr_vtm[kr].begin_write()])
                bread(vbk, t_v)
                tr_vtm[kr].add_write(t_v)
                t_exp = P.op("scalar",
                             lambda e: e.activation(texp[:], t1[:], AF.Exp),
                             waits=[t_min, tr_texp.begin_write()])
                tr_t1.add_reader(t_exp)
                tr_texp.end_write(t_exp)
                t_k = P.op("vector",
                           lambda v, kr=kr, kps=kps:
                           v.scalar_tensor_tensor(k_tm[:, kr, :], kps[:], 0.0,
                                                  texp[:], ALU.max, ALU.add),
                           waits=[t_exp] + brd(kbk)
                                 + [tr_ktm[kr].begin_write()])
                bread(kbk, t_k)
                tr_texp.add_reader(t_k)
                tr_ktm[kr].end_write(t_k)
                pend[u] = (kr, kvr, t_k, t_v)
                if u + 1 < NCH:
                    load_x_tp(cts.pop(u + 1), order[u + 1] % CPT)

            emit_kv(NCH - 1)
            if first:
                drain_w_jobs(len(w_jobs))
                flush_w_stores()

            z0 = P.op("vector", lambda v: v.memset(kvbd[:], 0.0),
                      waits=[tr_kvbd.begin_write()])
            c0 = P.op("vector",
                      lambda v: v.tensor_copy(kvbd[0:64, :, 0:HD + 1],
                                              kv_acc[0:64, :, :]),
                      waits=[z0, tr_kvacc.reads()])
            c1 = P.op("vector",
                      lambda v: v.tensor_copy(kvbd[64:128, :, HD + 1:2 * (HD + 1)],
                                              kv_acc[64:128, :, :]),
                      waits=[z0, tr_kvacc.reads()])
            tr_kvacc.add_reader(c0)
            tr_kvacc.add_reader(c1)
            tr_kvbd.end_write(c0)
            tr_kvbd.add_write(c1)
            load_pair("wq", "wo")

        def ln_chunk(tc, rr, ln1_out, out_tok0=None, slot=None, lnslot=None):
            sl = tc if slot is None else slot
            lnslot = tc if lnslot is None else lnslot
            t_sq = P.op("vector",
                        lambda v, rr=rr, sl=sl:
                        v.scalar_tensor_tensor(texp[:], res[:, rr, :], 1.0,
                                               res[:, rr, :], ALU.mult, ALU.mult,
                                               accum_out=stats[:, sl, 1:2]),
                        waits=[tr_res[rr].reads(), tr_sqs.begin_write(),
                               tr_stats[sl].reads()])
            tr_res[rr].add_reader(t_sq)
            tr_sqs.end_write(t_sq)
            tr_stats[sl].add_write(t_sq)
            t_nm = P.op("vector",
                        lambda v, sl=sl:
                        v.tensor_scalar(stats[:, sl, 2:3], stats[:, sl, 0:1],
                                        -1.0 / D, None, ALU.mult),
                        waits=[tr_stats[sl].reads()])
            tr_stats[sl].add_write(t_nm)
            t_m2 = P.op("vector",
                        lambda v, sl=sl:
                        v.scalar_tensor_tensor(stats[:, sl, 3:4], stats[:, sl, 0:1],
                                               1.0 / (D * D), stats[:, sl, 0:1],
                                               ALU.mult, ALU.mult),
                        waits=[t_nm])
            tr_stats[sl].add_write(t_m2)
            t_var = P.op("vector",
                         lambda v, sl=sl:
                         v.scalar_tensor_tensor(stats[:, sl, 4:5], stats[:, sl, 1:2],
                                                1.0 / D, stats[:, sl, 3:4],
                                                ALU.mult, ALU.subtract),
                         waits=[t_m2, t_sq])
            tr_stats[sl].add_write(t_var)
            t_ln = P.op("scalar",
                        lambda e, sl=sl:
                        e.activation(stats[:, sl, 5:6], stats[:, sl, 4:5], AF.Ln,
                                     bias=consts[:, 0:1]),
                        waits=[t_var, tr_consts.reads()])
            tr_stats[sl].add_write(t_ln)
            t_rs = P.op("scalar",
                        lambda e, sl=sl:
                        e.activation(stats[:, sl, 6:7], stats[:, sl, 5:6], AF.Exp,
                                     scale=consts[:, 1:2]),
                        waits=[t_ln])
            tr_stats[sl].add_write(t_rs)
            t_b = P.op("scalar",
                       lambda e, sl=sl:
                       e.activation(stats[:, sl, 7:8], stats[:, sl, 2:3], AF.Copy,
                                    scale=stats[:, sl, 6:7]),
                       waits=[t_rs, t_nm])
            tr_stats[sl].add_write(t_b)
            if ln1_out:
                t_ap = P.op("scalar",
                            lambda e, rr=rr, sl=sl, lnslot=lnslot:
                            e.activation(ln1_bf[:, lnslot, :], res[:, rr, :],
                                         AF.Identity,
                                         scale=stats[:, sl, 6:7],
                                         bias=stats[:, sl, 7:8]),
                            waits=[t_b, tr_res[rr].reads(),
                                   tr_ln1[lnslot].begin_write()])
                tr_res[rr].add_reader(t_ap)
                tr_stats[sl].add_reader(t_ap)
                tr_ln1[lnslot].end_write(t_ap)
                t_lt = P.dma(f"lt{tc % 2}",
                             lambda s, tc=tc, lnslot=lnslot:
                             s.dma_start(ln1T[:, :, tc * 128:(tc + 1) * 128],
                                         ln1_bf[:, lnslot, :], transpose=True),
                             waits=[t_ap, tr_ln1T.begin_write() if tc == 0 else None])
                tr_ln1[lnslot].add_reader(t_lt)
                if tc == 0:
                    tr_ln1T.end_write(t_lt)
                else:
                    tr_ln1T.add_write(t_lt)
            else:
                orr = tc % 2
                t_ap = P.op("scalar",
                            lambda e, rr=rr, sl=sl, orr=orr:
                            e.activation(outb[:, orr, :], res[:, rr, :], AF.Identity,
                                         scale=stats[:, sl, 6:7],
                                         bias=stats[:, sl, 7:8]),
                            waits=[t_b, tr_res[rr].reads(),
                                   tr_outb[orr].begin_write()])
                tr_res[rr].add_reader(t_ap)
                tr_stats[sl].add_reader(t_ap)
                tr_outb[orr].end_write(t_ap)
                st = P.dma(f"out{orr}",
                           lambda s, orr=orr, out_tok0=out_tok0:
                           s.dma_start(out_hbm[out_tok0:out_tok0 + 128, :],
                                       outb[:, orr, :]),
                           waits=[t_ap], eng=XQ)
                tr_outb[orr].add_reader(st)

        down_toks = {}

        def down_half(sc, hl):
            pr = sc % 2
            bank = ("c0", "c1", "c2", "c3")[pr * 2 + hl]
            for hm in range(NM):
                last = (hm == NM - 1)
                tok = P.op(
                    "tensor",
                    lambda t, hm=hm, hl=hl, sc=sc, pr=pr:
                    t.matmul(d_view[:, pr, hl * 512:(hl + 1) * 512],
                             h_fm[:, hm, sc * 128:(sc + 1) * 128],
                             wd_buf[:, hm, hl * 512:(hl + 1) * 512],
                             start=(hm == 0), stop=(hm == NM - 1)),
                    waits=[tr_h[hm].reads(), tr_wd[hm].reads()]
                          + (bw((bank,)) if hm == 0 else []),
                    inc=last)
            bend((bank,), tok)
            for hm in range(NM):
                tr_h[hm].add_reader(tok)
                tr_wd[hm].add_reader(tok)
            down_toks[(sc, hl)] = tok

        def down_fin(s, g, sc, lnslot):
            base = s * SEQ + g * TILE
            pr = sc % 2
            rr = (sc + 1) % 2
            cbk = ("c0", "c1") if pr == 0 else ("c2", "c3")
            tk0 = down_toks.pop((sc, 0))
            tk1 = down_toks.pop((sc, 1))
            t_res = P.op(
                "vector",
                lambda v, pr=pr, rr=rr, sc=sc, lnslot=lnslot:
                v.scalar_tensor_tensor(res[:, rr, :], d_view[:, pr, :], INV127,
                                       ln1_bf[:, lnslot, :], ALU.mult, ALU.add,
                                       accum_out=stats[:, 4 + sc, 0:1]),
                waits=[tk0, tk1, tr_ln1[lnslot].reads(),
                       tr_res[rr].begin_write(), tr_stats[4 + sc].begin_write()])
            bread(cbk, t_res)
            tr_ln1[lnslot].add_reader(t_res)
            tr_res[rr].end_write(t_res)
            tr_stats[4 + sc].end_write(t_res)
            ln_chunk(sc, rr, ln1_out=False, out_tok0=base + sc * 128, slot=4 + sc)

        def phase2_a(s, g):
            base = s * SEQ + g * TILE
            e0 = load_x_ld(base)
            e1 = load_x_ld(base + 128)
            load_x_fin(e0, 0)
            e2 = load_x_ld(base + 256)
            load_x_fin(e1, 1)
            e3 = load_x_ld(base + 384)
            load_x_fin(e2, 2)
            load_x_fin(e3, 3)

        def phase2(s, g, stop2=None, skip_a=False, pend=None, slot3=3,
                   next_x=None, pre_pend=False, fill_jobs=None):
            base = s * SEQ + g * TILE
            if not skip_a:
                phase2_a(s, g)
            nxt = {}

            def px(step):
                if next_x is None:
                    return
                if step == 0:
                    nxt[0] = load_x_ld(next_x)
                    nxt[1] = load_x_ld(next_x + 128)
                elif step == 1:
                    load_x_fin(nxt[0], 0)
                elif step == 2:
                    nxt[2] = load_x_ld(next_x + 256)
                elif step == 3:
                    load_x_fin(nxt[1], 1)
                elif step == 4:
                    nxt[3] = load_x_ld(next_x + 384)
                elif step == 5:
                    load_x_fin(nxt[2], 2)
                    load_x_fin(nxt[3], 3)

            if pre_pend and pend is not None:
                down_half(0, 0)
                down_half(0, 1)
            for i in range(NJ):
                abk = "a0" if i % 2 == 0 else "a1"
                for j in range(NJ):
                    last = (j == NJ - 1)
                    tok = P.op(
                        "tensor",
                        lambda t, i=i, j=j:
                        t.matmul(ps_a[:, (i % 2) * 512:(i % 2 + 1) * 512],
                                 wpair[:, 0, j, i * 128:(i + 1) * 128],
                                 xT[:, j, :], start=(j == 0), stop=(j == NJ - 1)),
                        waits=[tt.reads() for tt in tr_xT] + [tr_wpair[0].reads()]
                              + (bw((abk,)) if j == 0 else []),
                        inc=last)
                bend((abk,), tok)
                tr_wpair[0].add_reader(tok)
                for tt in tr_xT:
                    tt.add_reader(tok)
                t_min = P.op("vector",
                             lambda v, i=i:
                             v.tensor_scalar_min(t1[:, 0:TILE],
                                                 ps_a[:, (i % 2) * 512:(i % 2 + 1) * 512],
                                                 0.0),
                             waits=[tok, tr_t1.begin_write()])
                bread((abk,), t_min)
                tr_t1.end_write(t_min)
                t_exp = P.op("scalar",
                             lambda e: e.activation(texp[:, 0:TILE], t1[:, 0:TILE],
                                                    AF.Exp),
                             waits=[t_min, tr_texp.begin_write()])
                tr_t1.add_reader(t_exp)
                tr_texp.end_write(t_exp)
                t_q = P.op("vector",
                           lambda v, i=i:
                           v.scalar_tensor_tensor(q_fm[:, i, :],
                                                  ps_a[:, (i % 2) * 512:(i % 2 + 1) * 512],
                                                  0.0, texp[:, 0:TILE], ALU.max, ALU.add),
                           waits=[t_exp, tok,
                                  tr_q.begin_write() if i == 0 else None])
                bread((abk,), t_q)
                tr_texp.add_reader(t_q)
                if i == 0:
                    tr_q.end_write(t_q)
                else:
                    tr_q.add_write(t_q)

            def emit_c1h(tc, half):
                ar = tc % 2
                for p in range(4):
                    c = half * 4 + p
                    last = (p == 3)
                    tok = P.op(
                        "tensor",
                        lambda t, c=c, p=p, tc=tc:
                        t.matmul(outa_view[:, p, 0:2 * (HD + 1)],
                                 q_fm[:, c, tc * 128:(tc + 1) * 128],
                                 kvbd[:, c, :],
                                 start=True, stop=True),
                        waits=[tr_q.reads(), tr_kvbd.reads()]
                              + (bw(("a0", "a1")) if p == 0 else []),
                        inc=last)
                bend(("a0", "a1"), tok)
                tr_q.add_reader(tok)
                tr_kvbd.add_reader(tok)
                if stop2 == "P2o" and tc == 0 and half == 1:
                    cp = P.op("vector",
                              lambda v: v.tensor_copy(
                                  h_fm[:, 0:2, :].rearrange("p a b -> p (a b)"),
                                  ps_a[:]),
                              waits=[tok])
                    dbg_dump(h_fm[:, 0:2, :].rearrange("p a b -> p (a b)"), D,
                             waits=[cp])
                    return True
                pair = outa_view[:, :, 0:2 * (HD + 1)].rearrange(
                    "p c (q e) -> p c q e", q=2)
                hs = slice(half * 4, half * 4 + 4)
                t_z = P.op("vector",
                           lambda v, pair=pair, tc=tc, hs=hs:
                           v.tensor_scalar_add(zr[:, tc, 0, hs, :],
                                               pair[:, :, :, HD], EPS_ATTN),
                           waits=brd(("a0", "a1"))
                                 + [tr_zr[tc].begin_write() if half == 0 else None])
                bread(("a0", "a1"), t_z)
                if half == 0:
                    tr_zr[tc].end_write(t_z)
                else:
                    tr_zr[tc].add_write(t_z)
                t_r = P.op("vector",
                           lambda v, tc=tc, hs=hs:
                           v.reciprocal(zr[:, tc, 1, hs, :], zr[:, tc, 0, hs, :]),
                           waits=[t_z])
                tr_zr[tc].add_write(t_r)
                t_dv = P.op(
                    "vector",
                    lambda v, ar=ar, pair=pair, tc=tc, hs=hs, half=half:
                    v.tensor_tensor(attn_bf[:, ar, half * 512:(half + 1) * 512]
                                    .rearrange("p (c q e) -> p c q e", c=4, q=2),
                                    pair[:, :, :, 0:HD],
                                    zr[:, tc, 1, hs, :].unsqueeze(3).broadcast_to(
                                        [128, 4, 2, HD]),
                                    ALU.mult),
                    waits=[t_r] + brd(("a0", "a1"))
                          + [tr_attn[ar].begin_write() if half == 0 else None])
                bread(("a0", "a1"), t_dv)
                tr_zr[tc].add_reader(t_dv)
                if half == 0:
                    tr_attn[ar].end_write(t_dv)
                    return False
                tr_attn[ar].add_write(t_dv)
                if stop2 == "P2d" and tc == 0:
                    dbg_dump(attn_bf[:, ar, :], D, waits=[t_dv])
                    return True
                t_at = P.dma(f"at{ar}",
                             lambda s, ar=ar, tc=tc:
                             s.dma_start(attnT[:, tc, :, :], attn_bf[:, ar, :],
                                         transpose=True),
                             waits=[t_dv, tr_attnT[tc].begin_write()])
                tr_attn[ar].add_reader(t_at)
                tr_attnT[tc].end_write(t_at)

                return False

            def emit_c2(tc):
                lnsl = tc if tc < CPT - 1 else slot3
                for e_ in range(NJ):
                    for hl in range(2):
                        last = (e_ == NJ - 1 and hl == 1)
                        tok = P.op(
                            "tensor",
                            lambda t, e_=e_, hl=hl, tc=tc:
                            t.matmul(ps_b[:, hl * 512:(hl + 1) * 512],
                                     attnT[:, tc, e_, :],
                                     wpair[:, 1, e_, hl * 512:(hl + 1) * 512],
                                     start=(e_ == 0), stop=(e_ == NJ - 1)),
                            waits=[tr_attnT[tc].reads(), tr_wpair[1].reads()]
                                  + (bw(("b0", "b1")) if (e_ == 0 and hl == 0) else []),
                            inc=last)
                bend(("b0", "b1"), tok)
                tr_attnT[tc].add_reader(tok)
                tr_wpair[1].add_reader(tok)
                if stop2 == "P2w" and tc == 0:
                    cp = P.op("vector",
                              lambda v: v.tensor_copy(
                                  h_fm[:, 0:2, :].rearrange("p a b -> p (a b)"),
                                  ps_b[:]),
                              waits=[tok])
                    dbg_dump(h_fm[:, 0:2, :].rearrange("p a b -> p (a b)"), D,
                             waits=[cp])
                    return True
                rr = tc % 2
                t_res = P.op(
                    "vector",
                    lambda v, rr=rr, tc=tc:
                    v.scalar_tensor_tensor(res[:, rr, :], ps_b[:], 1.0, x_bf[:, tc, :],
                                           ALU.mult, ALU.add,
                                           accum_out=stats[:, tc, 0:1]),
                    waits=brd(("b0", "b1")) + [tr_xbf[tc].reads(),
                                               tr_res[rr].begin_write(),
                                               tr_stats[tc].begin_write()])
                bread(("b0", "b1"), t_res)
                tr_xbf[tc].add_reader(t_res)
                tr_res[rr].end_write(t_res)
                tr_stats[tc].end_write(t_res)
                ln_chunk(tc, rr, ln1_out=True, lnslot=lnsl)
                return False

            def pend_half(sc, hl):
                if pend is not None:
                    down_half(sc, hl)
                elif fill_jobs:
                    fill_jobs.pop(0)()

            def pend_fin(sc):
                if pend is not None:
                    ps_, pg_, psl3 = pend
                    down_fin(ps_, pg_, sc, sc if sc < CPT - 1 else psl3)

            if emit_c1h(0, 0):
                return True
            if emit_c1h(0, 1):
                return True
            for tc in range(1, CPT):
                px((tc - 1) * 2)
                if pre_pend:
                    if tc < CPT - 1:
                        pend_fin(tc - 1)
                        pend_half(tc, 0)
                else:
                    pend_half(tc - 1, 0)
                if emit_c1h(tc, 0):
                    return True
                if pre_pend:
                    if tc < CPT - 1:
                        pend_half(tc, 1)
                else:
                    pend_half(tc - 1, 1)
                if emit_c1h(tc, 1):
                    return True
                if not pre_pend or tc == CPT - 1:
                    pend_fin(tc - 1)
                if emit_c2(tc - 1):
                    return True
                if tc < CPT - 1:
                    px(tc * 2 - 1)
            if emit_c2(CPT - 1):
                return True
            px(5)
            pend_half(CPT - 1, 0)
            pend_half(CPT - 1, 1)
            pend_fin(CPT - 1)
            return False

        slab_state = {"lg": [None] * NM, "lu": [None] * NM, "st": [],
                      "first": True, "pre4": False}

        def load_slab(m, first_tile=False, both_gp=False, defer_vec=None):
            r = m % 4
            lg_toks = slab_state["lg"]
            lu_toks = slab_state["lu"]
            w3_stores = slab_state["st"]
            if first_tile:
                for gi, (mat, img) in enumerate((("wg", wi_imgs["wgi"]),
                                                 ("wu", wi_imgs["wui"]))):
                    sl = (2 * m + gi) % 2
                    ld = P.dma(f"w3i{sl}",
                               lambda sy, img=img, m=m, sl=sl:
                               sy.dma_start(stg_i32[sl], img[m]),
                               waits=[tr_stgi[sl].begin_write(),
                                      tr_outb[sl].begin_write()],
                               eng=BULKQ)
                    tr_stgi[sl].end_write(ld)
                    tr_outb[sl].end_write(ld)

                    def mk_ct(gi=gi, r=r, sl=sl, ld=ld, m=m, mat=mat,
                              both_gp=both_gp):
                        eng = "gpsimd" if (both_gp or gi == 1) else "vector"
                        ct = P.op(eng,
                                  lambda e, gi=gi, r=r, sl=sl:
                                  e.tensor_copy(wgu[:, gi, r, :], stg_i32[sl]),
                                  waits=[ld, tr_wgu[gi][r].begin_write()]
                                        + ([tr_stgi[2 + gi].begin_write()]
                                           if m < 2 else []))
                        tr_stgi[sl].add_reader(ct)
                        tr_outb[sl].add_reader(ct)
                        tr_wgu[gi][r].end_write(ct)
                        if gi == 0:
                            slab_state["lg"][m] = ct
                        else:
                            slab_state["lu"][m] = ct

                        def mk_st(mat=mat, m=m, gi=gi, r=r, ct=ct):
                            st = P.dma(f"scw{(2 * m + gi) % 2}",
                                       lambda sy, mat=mat, m=m, gi=gi, r=r:
                                       sy.dma_start(scr[mat][m], wgu[:, gi, r, :]),
                                       waits=[ct], eng=BULKQ)
                            tr_wgu[gi][r].add_reader(st)
                            tr_scr[mat][m].add_write(st)
                        slab_state["st"].append(mk_st)
                        while len(slab_state["st"]) > 2:
                            slab_state["st"].pop(0)()
                    if gi == 0 and defer_vec is not None:
                        defer_vec.append(mk_ct)
                    else:
                        mk_ct()
                return
            lg = P.dma(f"wg{r % 2}",
                       lambda sy, m=m, r=r:
                       sy.dma_start(wgu[:, 0, r, :], scr["wg"][m]),
                       waits=[tr_wgu[0][r].begin_write(), tr_scr["wg"][m].reads()]
                             + ([tr_stgi[2].begin_write()] if m < 2 else []),
                       eng=BULKQ)
            tr_wgu[0][r].end_write(lg)
            lg_toks[m] = lg
            lu = P.dma(f"wu{r % 2}",
                       lambda sy, m=m, r=r:
                       sy.dma_start(wgu[:, 1, r, :], scr["wu"][m]),
                       waits=[tr_wgu[1][r].begin_write(), tr_scr["wu"][m].reads()]
                             + ([tr_stgi[3].begin_write()] if m < 2 else []),
                       eng=BULKQ)
            tr_wgu[1][r].end_write(lu)
            lu_toks[m] = lu

        def phase3(s, g, next_a=None, next_kv_pair=False, preload_next=False):
            base = s * SEQ + g * TILE
            first_tile = slab_state["first"]
            slab_state["first"] = False
            if next_kv_pair:
                load_pair("wk", "wv")
            lg_toks = slab_state["lg"]
            lu_toks = slab_state["lu"]
            w3_stores = slab_state["st"]

            if first_tile:
                if not slab_state["pre4"]:
                    for m in range(4):
                        load_slab(m, True)
            elif not PRELOAD_SLABS:
                for m in range(4):
                    load_slab(m)
            vc = slab_state.setdefault("vc", [])
            for m in range(NM):
                if vc:
                    vc.pop(0)()
                r = m % 4
                abk = "a0" if m % 2 == 0 else "a1"
                bbk = "b0" if m % 2 == 0 else "b1"
                lg = lg_toks[m]
                lu = lu_toks[m]
                if next_a is not None and m == 12:
                    next_a()
                    next_a = None

                for j in range(NJ):
                    last = (j == NJ - 1)
                    tokg = P.op(
                        "tensor",
                        lambda t, j=j, r=r, m=m:
                        t.matmul(ps_a[:, (m % 2) * 512:(m % 2 + 1) * 512],
                                 wgu[:, 0, r, j * 128:(j + 1) * 128],
                                 ln1T[:, j, :], start=(j == 0), stop=(j == NJ - 1)),
                        waits=[lg, tr_ln1T.reads()] + (bw((abk,)) if j == 0 else []),
                        inc=last)
                bend((abk,), tokg)
                tr_wgu[0][r].add_reader(tokg)
                tr_ln1T.add_reader(tokg)
                for j in range(NJ):
                    last = (j == NJ - 1)
                    toku = P.op(
                        "tensor",
                        lambda t, j=j, r=r, m=m:
                        t.matmul(ps_b[:, (m % 2) * 512:(m % 2 + 1) * 512],
                                 wgu[:, 1, r, j * 128:(j + 1) * 128],
                                 ln1T[:, j, :], start=(j == 0), stop=(j == NJ - 1)),
                        waits=[lu, tr_ln1T.reads()] + (bw((bbk,)) if j == 0 else []),
                        inc=last)
                bend((bbk,), toku)
                tr_wgu[1][r].add_reader(toku)
                tr_ln1T.add_reader(toku)
                if m + 4 < NM:
                    load_slab(m + 4, first_tile,
                              defer_vec=(vc if first_tile else None))
                elif preload_next and m == NM - 1:
                    while w3_stores:
                        w3_stores.pop(0)()
                    for mm in range(4):
                        load_slab(mm)

                t_si = P.op("scalar",
                            lambda e, m=m:
                            e.activation(silu_b[:, m % 2, :],
                                         ps_a[:, (m % 2) * 512:(m % 2 + 1) * 512],
                                         AF.Silu, scale=consts[:, 2:3]),
                            waits=[tokg, tr_silu[m % 2].begin_write(), tr_consts.reads()])
                bread((abk,), t_si)
                tr_silu[m % 2].end_write(t_si)
                t_h = P.op("vector",
                           lambda v, m=m:
                           v.scalar_tensor_tensor(h_fm[:, m, :],
                                                  ps_b[:, (m % 2) * 512:(m % 2 + 1) * 512],
                                                  INV127, silu_b[:, m % 2, :],
                                                  ALU.mult, ALU.mult),
                           waits=[t_si, toku, tr_h[m].begin_write()]
                                 + ([tr_stgb[m // 2].begin_write()] if m < 8 else []))
                bread((bbk,), t_h)
                tr_silu[m % 2].add_reader(t_h)
                tr_h[m].end_write(t_h)
                if m == NM - 1 and PRELOAD:
                    P.op("scalar", lambda e: e.preload_act_table(AF.Ln), waits=[t_si])

            while vc:
                vc.pop(0)()
            while w3_stores:
                w3_stores.pop(0)()

        pend_tile = [None]
        for s in range(n_seq):
            phase1(s, first=(s == 0))
            if stop_after == "P1":
                dbg_dump(kvbd[:].rearrange("p a b -> p (a b)"), NJ * 2 * (HD + 1),
                         waits=[tr_kvbd.reads()])
                P.flush_block()
                return nc
            prefetched = [True]
            for g in range(n_tiles):
                tidx = s * n_tiles + g
                sl3 = CPT - 1 + (tidx % 2)
                nx = (s * SEQ + (g + 1) * TILE) if g + 1 < n_tiles else None
                fj = None
                if tidx == 0:
                    fj = [lambda m=m: load_slab(m, True, both_gp=True)
                          for m in range(4)]
                    slab_state["pre4"] = True
                r2 = phase2(s, g, stop2=stop_after, skip_a=prefetched[0],
                            pend=pend_tile[0], slot3=sl3, next_x=nx,
                            pre_pend=(g == 0), fill_jobs=fj)
                pend_tile[0] = (s, g, sl3)
                prefetched[0] = False
                if r2:
                    P.flush_block()
                    return nc
                if stop_after == "P2":
                    dbg_dump(ln1_bf[:, 0:CPT, :].rearrange("p a b -> p (a b)"),
                             CPT * D, waits=[t.reads() for t in tr_ln1])
                    P.flush_block()
                    return nc
                if g + 1 < n_tiles:
                    prefetched[0] = True
                    phase3(s, g, preload_next=PRELOAD_SLABS)
                else:
                    phase3(s, g, next_kv_pair=(s + 1 < n_seq),
                           preload_next=(PRELOAD_SLABS and s + 1 < n_seq))
        fs, fg, fsl3 = pend_tile[0]
        for sc in range(CPT):
            down_half(sc, 0)
            down_half(sc, 1)
            down_fin(fs, fg, sc, sc if sc < CPT - 1 else fsl3)

        P.sync_wait([(P.dma_sems["out0"], P.dma_count["out0"]),
                     (P.dma_sems["out1"], P.dma_count["out1"])])
        P.flush_block()

    return nc


def _w_img(w_t):
    return np.ascontiguousarray(
        w_t.reshape(NJ, 128, D).transpose(1, 0, 2).reshape(128, NJ * D))


def kernel(L0_attn_x, L0_attn_Wq, L0_attn_Wk, L0_attn_Wv, L0_attn_Wo,
           L0_ffn_w_gate, L0_ffn_w_up, L0_ffn_w_down):
    X = np.ascontiguousarray(np.asarray(L0_attn_x))
    Wq = np.asarray(L0_attn_Wq)
    Wk = np.asarray(L0_attn_Wk)
    Wv = np.asarray(L0_attn_Wv)
    Wo = np.asarray(L0_attn_Wo)
    wg = np.asarray(L0_ffn_w_gate)
    wu = np.asarray(L0_ffn_w_up)
    wd = np.asarray(L0_ffn_w_down)

    imgs = {
        "wq": _w_img(np.ascontiguousarray(Wq.T)),
        "wk": _w_img(np.ascontiguousarray(Wk.T)),
        "wv": _w_img(np.ascontiguousarray(Wv.T)),
        "wo": _w_img(np.ascontiguousarray(Wo.T)),
        "wgi": np.ascontiguousarray(
            wg.T.reshape(NJ, 128, NM, 128).transpose(2, 1, 0, 3).reshape(NM, 128, D)),
        "wui": np.ascontiguousarray(
            wu.T.reshape(NJ, 128, NM, 128).transpose(2, 1, 0, 3).reshape(NM, 128, D)),
        "wdi": np.ascontiguousarray(wd.T.reshape(NM, 128, D)),
    }

    nc = build_program()
    in_maps = []
    for c in range(NCORES):
        im = {"x": np.ascontiguousarray(
            X[NSEQ * c:NSEQ * (c + 1)].reshape(TOKENS, D))}
        im.update(imgs)
        in_maps.append(im)

    resu = run_bass_kernel_spmd(nc, in_maps, core_ids=list(range(NCORES)))
    out = np.empty((16, SEQ, D), dtype=np.float32)
    for c in range(NCORES):
        out[NSEQ * c:NSEQ * (c + 1)] = np.asarray(
            resu.results[c]["out"]).reshape(NSEQ, SEQ, D)
    return out


# version 29
# speedup vs baseline: 1.0138x; 1.0138x over previous
import contextlib
import numpy as np

import concourse.bass as bass
import concourse.mybir as mybir
from concourse.bass_utils import run_bass_kernel_spmd

F32 = mybir.dt.float32
BF16 = mybir.dt.bfloat16
I32 = mybir.dt.int32
AF = mybir.ActivationFunctionType
ALU = mybir.AluOpType

NCORES = 8
SEQ = 2048
NSEQ = 2
D = 1024
NJ = 8
H = 16
HD = 64
FF = 3072
NM = 24
TOKENS = NSEQ * SEQ
TILE = 512
NTILE = TOKENS // TILE
CPT = TILE // 128
INV127 = 1.0 / 127.0
EPS_ATTN = 1e-6
EPS_LN = 1e-5
PRELOAD = False
WKEEP = 3
PRELOAD_SLABS = True
BULKQ = "sync"
WQ = "sync"
XQ = "sync"


def _merge(d, tok):
    if tok is None:
        return
    sem, v = tok
    if d.get(sem, 0) < v:
        d[sem] = v


class Track:
    def __init__(self):
        self.ready = {}
        self.readers = {}

    def begin_write(self):
        w = dict(self.readers)
        for s, v in self.ready.items():
            if w.get(s, 0) < v:
                w[s] = v
        return w

    def end_write(self, tok):
        self.ready = {}
        self.readers = {}
        _merge(self.ready, tok)

    def add_write(self, tok):
        _merge(self.ready, tok)

    def reads(self):
        return dict(self.ready)

    def add_reader(self, tok):
        _merge(self.readers, tok)


class Prog:
    ENGS = ("sync", "tensor", "vector", "scalar", "gpsimd")

    def __init__(self, nc, stack):
        self.nc = nc
        self.sems = {}
        self.count = {}
        for e in ("tensor", "vector", "scalar", "gpsimd"):
            self.sems[e] = stack.enter_context(nc.semaphore(f"sem_{e}"))
            self.count[e] = 0
        self.dma_sems = {}
        self.dma_count = {}
        self.stack = stack
        self._reset_block()

    def _reset_block(self):
        self.ops = {e: [] for e in self.ENGS}
        self.waited = {e: {} for e in self.ENGS}

    def dma_slot(self, name):
        if name not in self.dma_sems:
            self.dma_sems[name] = self.stack.enter_context(
                self.nc.semaphore(f"dsl_{name}"))
            self.dma_count[name] = 0
        return name

    def _gather(self, waits):
        wd = {}
        for w in waits:
            if w is None:
                continue
            if isinstance(w, dict):
                for s, v in w.items():
                    _merge(wd, (s, v))
            else:
                _merge(wd, w)
        return wd

    def _emit_waits(self, eng_name, wd):
        out = []
        seen = self.waited[eng_name]
        for sem, val in sorted(wd.items(), key=lambda kv: kv[0].num):
            if seen.get(sem, 0) < val:
                seen[sem] = val
                out.append((sem, val))
        return out

    def op(self, eng_name, fn, waits=(), inc=True):
        real_waits = self._emit_waits(eng_name, self._gather(waits))
        tok = None
        sem = self.sems[eng_name]
        if inc:
            self.count[eng_name] += 1
            tok = (sem, self.count[eng_name])

        def closure(eng, fn=fn, real_waits=real_waits, tok=tok, sem=sem):
            for s, v in real_waits:
                eng.wait_ge(s, v)
            inst = fn(eng)
            if tok is not None:
                inst.then_inc(sem, 1)

        self.ops[eng_name].append(closure)
        return tok

    def dma(self, slot, fn, waits=(), eng="sync"):
        slot = self.dma_slot(slot)
        wd = self._gather(waits)
        sem = self.dma_sems[slot]
        prev = self.dma_count[slot]
        if prev:
            _merge(wd, (sem, prev))
        real_waits = self._emit_waits(eng, wd)
        self.dma_count[slot] = prev + 16
        tok = (sem, self.dma_count[slot])

        def closure(e, fn=fn, real_waits=real_waits, sem=sem):
            for s, v in real_waits:
                e.wait_ge(s, v)
            fn(e).then_inc(sem, 16)

        self.ops[eng].append(closure)
        return tok

    def sync_wait(self, waits):
        real_waits = self._emit_waits("sync", self._gather(waits))

        def closure(eng, real_waits=real_waits):
            for s, v in real_waits:
                eng.wait_ge(s, v)

        self.ops["sync"].append(closure)

    def flush_block(self):
        nc = self.nc
        ops = self.ops
        with nc.Block() as blk:
            if ops["sync"]:
                blk.sync(lambda e, fns=ops["sync"]: [f(e) for f in fns])
            if ops["tensor"]:
                blk.tensor(lambda e, fns=ops["tensor"]: [f(e) for f in fns])
            if ops["vector"]:
                blk.vector(lambda e, fns=ops["vector"]: [f(e) for f in fns])
            if ops["scalar"]:
                blk.scalar(lambda e, fns=ops["scalar"]: [f(e) for f in fns])
            if ops["gpsimd"]:
                blk.gpsimd(lambda e, fns=ops["gpsimd"]: [f(e) for f in fns])
        self._reset_block()


def build_program(n_seq=NSEQ, n_tiles=NTILE // NSEQ, stop_after=None):
    nc = bass.Bass("TRN2", target_bir_lowering=False,
                   detect_race_conditions=False, dynamic_dma_scratch_size=2048)

    x_hbm = nc.dram_tensor("x", [TOKENS, D], F32, kind="ExternalInput")
    w_imgs = {n: nc.dram_tensor(n, [128, NJ * D], F32, kind="ExternalInput")
              for n in ("wq", "wk", "wv", "wo")}
    wi_imgs = {n: nc.dram_tensor(n, [NM, 128, D], I32, kind="ExternalInput")
               for n in ("wgi", "wui", "wdi")}
    out_hbm = nc.dram_tensor("out", [TOKENS, D], F32, kind="ExternalOutput")
    scr = {n: nc.dram_tensor(f"scr_{n}", [NM, 128, D], BF16)
           for n in ("wg", "wu", "wd")}
    scr_qkvo = {n: nc.dram_tensor(f"scr_{n}", [128, NJ * D], BF16)
                for n in ("wq", "wk", "wv", "wo")}
    dbg_hbm = None
    if stop_after is not None:
        dbg_hbm = nc.dram_tensor("dbg", [128, 8192], BF16, kind="ExternalOutput")

    with contextlib.ExitStack() as ctx:
        E = ctx.enter_context
        wpair = E(nc.sbuf_tensor("wpair", [128, 2, NJ, D], BF16))
        wd_buf = E(nc.sbuf_tensor("wd_buf", [128, NM, D], BF16))
        wgu = E(nc.sbuf_tensor("wgu", [128, 2, 4, D], BF16))
        h_fm = E(nc.sbuf_tensor("h_fm", [128, NM, TILE], BF16))
        ln1_bf = E(nc.sbuf_tensor("ln1_bf", [128, CPT + 1, D], BF16))
        ln1T = E(nc.sbuf_tensor("ln1T", [128, NJ, TILE], BF16))
        x_f32 = E(nc.sbuf_tensor("x_f32", [128, 2, D], F32))
        x_bf = E(nc.sbuf_tensor("x_bf", [128, CPT, D], BF16))
        xT = E(nc.sbuf_tensor("xT", [128, NJ, TILE], BF16))
        q_fm = E(nc.sbuf_tensor("q_fm", [128, NJ, TILE], BF16))
        k_tm = E(nc.sbuf_tensor("k_tm", [128, 2, D], BF16))
        v_tm = E(nc.sbuf_tensor("v_tm", [128, 2, H, HD + 1], BF16))
        kv_acc = E(nc.sbuf_tensor("kv_acc", [128, NJ, HD + 1], F32))
        kvbd = E(nc.sbuf_tensor("kvbd", [128, NJ, 2 * (HD + 1)], BF16))
        attn_bf = E(nc.sbuf_tensor("attn_bf", [128, 2, D], BF16))
        attnT = E(nc.sbuf_tensor("attnT", [128, CPT, NJ, 128], BF16))
        t1 = E(nc.sbuf_tensor("t1", [128, D], BF16))
        texp = E(nc.sbuf_tensor("texp", [128, D], BF16))
        res = E(nc.sbuf_tensor("res", [128, 2, D], F32))
        stats = E(nc.sbuf_tensor("stats", [128, 2 * CPT, 8], F32))
        zr = E(nc.sbuf_tensor("zr", [128, CPT, 2, NJ, 2], F32))
        outb = E(nc.sbuf_tensor("outb", [128, 2, D], F32))
        silu_b = E(nc.sbuf_tensor("silu_b", [128, 2, TILE], BF16))
        consts = E(nc.sbuf_tensor("consts", [128, 4], F32))
        ps_a = E(nc.psum_tensor("ps_a", [128, 1024], F32))
        ps_b = E(nc.psum_tensor("ps_b", [128, 1024], F32))
        ps_c = E(nc.psum_tensor("ps_c", [128, 2048], F32))

        P = Prog(nc, ctx)

        BK = {k: Track() for k in ("a0", "a1", "b0", "b1", "c0", "c1", "c2", "c3")}

        def bw(keys):
            return [BK[k].begin_write() for k in keys]

        def brd(keys):
            return [BK[k].reads() for k in keys]

        def bend(keys, tok):
            for k in keys:
                BK[k].end_write(tok)

        def bread(keys, tok):
            for k in keys:
                BK[k].add_reader(tok)

        kv_view = ps_c[:].rearrange("p (r c e) -> p r c e", r=2, c=NJ)
        out_view = ps_c[:].rearrange("p (c e) -> p c e", c=NJ)
        outa_view = ps_a[:].rearrange("p (c e) -> p c e", c=CPT)
        d_view = ps_c[:].rearrange("p (r e) -> p r e", r=2)

        tr_x = [Track() for _ in range(2)]
        tr_xbf = [Track() for _ in range(CPT)]
        tr_xT = [Track() for _ in range(CPT)]
        tr_wpair = [Track() for _ in range(2)]
        tr_ktm = [Track() for _ in range(2)]
        tr_vtm = [Track() for _ in range(2)]
        tr_kvacc = Track()
        tr_kvbd = Track()
        tr_q = Track()
        tr_attn = [Track() for _ in range(2)]
        tr_attnT = [Track() for _ in range(CPT)]
        tr_t1 = Track()
        tr_texp = Track()
        tr_sqs = tr_texp
        tr_res = [Track() for _ in range(2)]
        tr_stats = [Track() for _ in range(2 * CPT)]
        tr_zr = [Track() for _ in range(CPT)]
        tr_outb = [Track() for _ in range(2)]
        tr_ln1 = [Track() for _ in range(CPT + 1)]
        tr_ln1T = Track()
        tr_h = [Track() for _ in range(NM)]
        tr_wgu = [[Track() for _ in range(4)] for _ in range(2)]
        tr_wd = [Track() for _ in range(NM)]
        tr_silu = [Track() for _ in range(2)]
        tr_stgi = [Track() for _ in range(4)]
        tr_stgb = [Track() for _ in range(4)]
        tr_scr = {n: [Track() for _ in range(NM)] for n in ("wg", "wu")}
        tr_scr.update({n: [Track() for _ in range(NJ)]
                       for n in ("wq", "wk", "wv", "wo")})
        tr_consts = Track()

        tok = P.op("vector", lambda v: v.memset(consts[:, 0:1], EPS_LN))
        tr_consts.add_write(tok)
        tok = P.op("vector", lambda v: v.memset(consts[:, 1:2], -0.5))
        tr_consts.add_write(tok)
        tok = P.op("vector", lambda v: v.memset(consts[:, 2:3], INV127))
        tr_consts.add_write(tok)
        for r in range(2):
            tok = P.op("vector", lambda v, r=r: v.memset(v_tm[:, r, :, HD:HD + 1], 1.0))
            tr_vtm[r].add_write(tok)

        stg_i32 = [outb[:, 0, :].bitcast(I32), outb[:, 1, :].bitcast(I32),
                   wgu[:, 0, :, :].rearrange("p a b -> p (a b)").bitcast(I32)[:, 0:D],
                   wgu[:, 1, :, :].rearrange("p a b -> p (a b)").bitcast(I32)[:, 0:D]]
        stg_bf = [h_fm[:, 2 * i:2 * i + 2, :].rearrange("p a b -> p (a b)")
                  for i in range(4)]

        w_jobs = []
        for mat in ("wq", "wo"):
            for j in range(NJ):
                w_jobs.append(("f32", mat, w_imgs[mat], j))
        for m in range(NM):
            w_jobs.append(("wdd", "wd", wi_imgs["wdi"], m))
        for mat, si in (("wk", 0), ("wv", 1)):
            w_jobs.append(("pair_st", mat, si, 0))
        w_pos = [0]
        w_stores = []

        def flush_w_stores(keep=0):
            while len(w_stores) > keep:
                w_stores.pop(0)()

        def drain_w_jobs(k):
            for _ in range(k):
                if w_pos[0] >= len(w_jobs):
                    flush_w_stores()
                    return
                kind, mat, src_t, m = w_jobs[w_pos[0]]
                r = w_pos[0] % 4
                w_pos[0] += 1
                if kind == "pair_st":
                    si = src_t
                    st = P.dma(f"pst{si}",
                               lambda s, mat=mat, si=si:
                               s.dma_start(scr_qkvo[mat][:],
                                           wpair[:, si, :, :].rearrange(
                                               "p a b -> p (a b)")),
                               waits=[tr_wpair[si].reads()], eng=BULKQ)
                    tr_wpair[si].add_reader(st)
                    for t in tr_scr[mat]:
                        t.add_write(st)
                    continue
                if kind == "int" or kind == "wdd":
                    extra = [tr_outb[r].begin_write()] if r < 2 else []
                    ld = P.dma(f"wi{r}",
                               lambda s, src_t=src_t, m=m, r=r:
                               s.dma_start(stg_i32[r], src_t[m]),
                               waits=[tr_stgi[r].begin_write()] + extra,
                               eng=WQ)
                    tr_stgi[r].end_write(ld)
                    if r < 2:
                        tr_outb[r].end_write(ld)
                    if kind == "wdd":
                        ct = P.op("vector" if r % 2 == 0 else "gpsimd",
                                  lambda e, r=r, m=m:
                                  e.tensor_copy(wd_buf[:, m, :], stg_i32[r]),
                                  waits=[ld, tr_wd[m].begin_write()])
                        tr_stgi[r].add_reader(ct)
                        if r < 2:
                            tr_outb[r].add_reader(ct)
                        tr_wd[m].end_write(ct)
                        continue
                    ct = P.op("vector",
                              lambda e, r=r: e.tensor_copy(stg_bf[r], stg_i32[r]),
                              waits=[ld, tr_stgb[r].begin_write()])
                    tr_stgi[r].add_reader(ct)
                    if r < 2:
                        tr_outb[r].add_reader(ct)
                    tr_stgb[r].end_write(ct)
                    def mk_store(mat=mat, m=m, r=r, ct=ct):
                        st = P.dma(f"scw{r}",
                                   lambda s, mat=mat, m=m, r=r:
                                   s.dma_start(scr[mat][m], stg_bf[r]),
                                   waits=[ct], eng=BULKQ)
                        tr_stgb[r].add_reader(st)
                        tr_scr[mat][m].add_write(st)
                    w_stores.append(mk_store)
                    flush_w_stores(keep=WKEEP)
                else:
                    rr2 = w_pos[0] % 2
                    ld = P.dma(f"wf{rr2}",
                               lambda s, src_t=src_t, m=m, rr2=rr2:
                               s.dma_start(res[:, rr2, :], src_t[:, m * D:(m + 1) * D]),
                               waits=[tr_res[rr2].begin_write()], eng=WQ)
                    tr_res[rr2].end_write(ld)
                    ct = P.op("scalar",
                              lambda e, r=r, rr2=rr2:
                              e.activation(stg_bf[r], res[:, rr2, :], AF.Copy),
                              waits=[ld, tr_stgb[r].begin_write()])
                    tr_res[rr2].add_reader(ct)
                    tr_stgb[r].end_write(ct)
                    def mk_store2(mat=mat, m=m, r=r, ct=ct):
                        st = P.dma(f"scw{r}",
                                   lambda s, mat=mat, m=m, r=r:
                                   s.dma_start(scr_qkvo[mat][:, m * D:(m + 1) * D],
                                               stg_bf[r]),
                                   waits=[ct], eng=BULKQ)
                        tr_stgb[r].add_reader(st)
                        tr_scr[mat][m].add_write(st)
                    w_stores.append(mk_store2)
                    flush_w_stores(keep=WKEEP)

        def load_pair_direct(m0, m1):
            for si, mat in enumerate((m0, m1)):
                fsrc = w_imgs[mat]
                for j in range(NJ):
                    r = j % 2
                    ld = P.dma(f"wp{si}",
                               lambda s, fsrc=fsrc, j=j, r=r:
                               s.dma_start(x_f32[:, r, :], fsrc[:, j * D:(j + 1) * D]),
                               waits=[tr_x[r].begin_write()], eng=BULKQ)
                    tr_x[r].end_write(ld)
                    ct = P.op("scalar",
                              lambda e, si=si, j=j, r=r:
                              e.activation(wpair[:, si, j, :], x_f32[:, r, :], AF.Copy),
                              waits=[ld, tr_wpair[si].begin_write() if j == 0 else None])
                    tr_x[r].add_reader(ct)
                    if j == 0:
                        tr_wpair[si].end_write(ct)
                    else:
                        tr_wpair[si].add_write(ct)

        def load_pair(m0, m1):
            for si, mat in enumerate((m0, m1)):
                ld = P.dma(f"wp{si}",
                           lambda s, si=si, mat=mat:
                           s.dma_start(wpair[:, si, :, :].rearrange("p a b -> p (a b)"),
                                       scr_qkvo[mat][:]),
                           waits=[tr_wpair[si].begin_write()]
                                 + [t.reads() for t in tr_scr[mat]],
                           eng=BULKQ)
                tr_wpair[si].end_write(ld)

        xring = [0]

        def load_x_ld(tok0):
            xr = xring[0] % 2
            xring[0] += 1
            ld = P.dma(f"x{xr}",
                       lambda s, tok0=tok0, xr=xr:
                       s.dma_start(x_f32[:, xr, :], x_hbm[tok0:tok0 + 128, :]),
                       waits=[tr_x[xr].begin_write()], eng=XQ)
            tr_x[xr].end_write(ld)
            return (ld, xr)

        def load_x_cast(ent, tq):
            ld, xr = ent
            ct = P.op("scalar",
                      lambda e, xr=xr, tq=tq:
                      e.activation(x_bf[:, tq, :], x_f32[:, xr, :], AF.Copy),
                      waits=[ld, tr_xbf[tq].begin_write()])
            tr_x[xr].add_reader(ct)
            tr_xbf[tq].end_write(ct)
            return ct

        def load_x_tp(ct, tq):
            tp = P.dma(f"xt{tq % 2}",
                       lambda s, tq=tq:
                       s.dma_start(xT[:, :, tq * 128:(tq + 1) * 128],
                                   x_bf[:, tq, :], transpose=True),
                       waits=[ct, tr_xT[tq].begin_write()])
            tr_xbf[tq].add_reader(tp)
            tr_xT[tq].end_write(tp)

        def load_x_fin(ent, tq):
            load_x_tp(load_x_cast(ent, tq), tq)

        def dbg_dump(src_ap, n, waits=()):
            tk = P.dma("dbg", lambda sy, src_ap=src_ap, n=n:
                       sy.dma_start(dbg_hbm[:, 0:n], src_ap), waits=waits)
            P.sync_wait([tk])

        def phase1(s, first):
            if first:
                load_pair_direct("wk", "wv")
            NCH = SEQ // 128
            pend = {}

            def emit_kv(u):
                kr, kvr, t_k, t_v = pend.pop(u)
                cbk = ("c0", "c1") if kvr == 0 else ("c2", "c3")
                for h in range(H):
                    last = (h == H - 1)
                    p0 = (h % 2) * 64
                    tok = P.op(
                        "tensor",
                        lambda t, h=h, kr=kr, p0=p0, kvr=kvr:
                        t.matmul(kv_view[p0:p0 + 64, kvr, h // 2, 0:HD + 1],
                                 k_tm[:, kr, h * HD:(h + 1) * HD],
                                 v_tm[:, kr, h, :],
                                 start=True, stop=True,
                                 tile_position=(0, p0)),
                        waits=[t_k, t_v] + (bw(cbk) if h == 0 else []),
                        inc=last)
                bend(cbk, tok)
                tr_ktm[kr].add_reader(tok)
                tr_vtm[kr].add_reader(tok)
                if u == 0:
                    acc = P.op("vector",
                               lambda v, kvr=kvr:
                               v.tensor_copy(kv_acc[:], kv_view[:, kvr, :, 0:HD + 1]),
                               waits=[tok, tr_kvacc.begin_write()])
                else:
                    acc = P.op("vector",
                               lambda v, kvr=kvr:
                               v.tensor_tensor(kv_acc[:], kv_acc[:],
                                               kv_view[:, kvr, :, 0:HD + 1], ALU.add),
                               waits=[tok, tr_kvacc.reads()])
                bread(cbk, acc)
                tr_kvacc.end_write(acc)

            order = [(u + CPT) % NCH for u in range(NCH)]
            cts = {0: load_x_cast(load_x_ld(s * SEQ + order[0] * 128),
                                  order[0] % CPT)}
            load_x_tp(cts.pop(0), order[0] % CPT)
            for u in range(NCH):
                if u + 1 < NCH:
                    cts[u + 1] = load_x_cast(
                        load_x_ld(s * SEQ + order[u + 1] * 128),
                        order[u + 1] % CPT)
                tq = order[u] % CPT
                kr = u % 2
                kvr = u % 2
                if first:
                    drain_w_jobs(4)

                kps = ps_a if u % 2 == 0 else ps_b
                vps = ps_b if u % 2 == 0 else ps_a
                kbk = ("a0", "a1") if u % 2 == 0 else ("b0", "b1")
                vbk = ("b0", "b1") if u % 2 == 0 else ("a0", "a1")
                for j in range(NJ):
                    for hl in range(2):
                        last = (j == NJ - 1 and hl == 1)
                        tok = P.op(
                            "tensor",
                            lambda t, j=j, hl=hl, tq=tq, kps=kps:
                            t.matmul(kps[:, hl * 512:(hl + 1) * 512],
                                     xT[:, j, tq * 128:(tq + 1) * 128],
                                     wpair[:, 0, j, hl * 512:(hl + 1) * 512],
                                     start=(j == 0), stop=(j == NJ - 1)),
                            waits=[tr_xT[tq].reads(), tr_wpair[0].reads()]
                                  + (bw(kbk) if (j == 0 and hl == 0) else []),
                            inc=last)
                bend(kbk, tok)
                for j in range(NJ):
                    for hl in range(2):
                        last = (j == NJ - 1 and hl == 1)
                        tok = P.op(
                            "tensor",
                            lambda t, j=j, hl=hl, tq=tq, vps=vps:
                            t.matmul(vps[:, hl * 512:(hl + 1) * 512],
                                     xT[:, j, tq * 128:(tq + 1) * 128],
                                     wpair[:, 1, j, hl * 512:(hl + 1) * 512],
                                     start=(j == 0), stop=(j == NJ - 1)),
                            waits=[tr_xT[tq].reads(), tr_wpair[1].reads()]
                                  + (bw(vbk) if (j == 0 and hl == 0) else []),
                            inc=last)
                bend(vbk, tok)
                tr_xT[tq].add_reader(tok)
                tr_wpair[0].add_reader(tok)
                tr_wpair[1].add_reader(tok)

                if u > 0:
                    emit_kv(u - 1)

                t_min = P.op("vector",
                             lambda v, kps=kps: v.tensor_scalar_min(t1[:], kps[:], 0.0),
                             waits=brd(kbk) + [tr_t1.begin_write()])
                bread(kbk, t_min)
                tr_t1.end_write(t_min)
                t_v = P.op("scalar",
                           lambda e, kr=kr, vps=vps:
                           e.activation(v_tm[:, kr, :, 0:HD],
                                        vps[:].rearrange("p (h e) -> p h e", h=H),
                                        AF.Copy),
                           waits=brd(vbk) + [tr_vtm[kr].begin_write()])
                bread(vbk, t_v)
                tr_vtm[kr].add_write(t_v)
                t_exp = P.op("scalar",
                             lambda e: e.activation(texp[:], t1[:], AF.Exp),
                             waits=[t_min, tr_texp.begin_write()])
                tr_t1.add_reader(t_exp)
                tr_texp.end_write(t_exp)
                t_k = P.op("vector",
                           lambda v, kr=kr, kps=kps:
                           v.scalar_tensor_tensor(k_tm[:, kr, :], kps[:], 0.0,
                                                  texp[:], ALU.max, ALU.add),
                           waits=[t_exp] + brd(kbk)
                                 + [tr_ktm[kr].begin_write()])
                bread(kbk, t_k)
                tr_texp.add_reader(t_k)
                tr_ktm[kr].end_write(t_k)
                pend[u] = (kr, kvr, t_k, t_v)
                if u + 1 < NCH:
                    load_x_tp(cts.pop(u + 1), order[u + 1] % CPT)

            emit_kv(NCH - 1)
            if first:
                drain_w_jobs(len(w_jobs))
                flush_w_stores()

            z0 = P.op("vector", lambda v: v.memset(kvbd[:], 0.0),
                      waits=[tr_kvbd.begin_write()])
            c0 = P.op("vector",
                      lambda v: v.tensor_copy(kvbd[0:64, :, 0:HD + 1],
                                              kv_acc[0:64, :, :]),
                      waits=[z0, tr_kvacc.reads()])
            c1 = P.op("vector",
                      lambda v: v.tensor_copy(kvbd[64:128, :, HD + 1:2 * (HD + 1)],
                                              kv_acc[64:128, :, :]),
                      waits=[z0, tr_kvacc.reads()])
            tr_kvacc.add_reader(c0)
            tr_kvacc.add_reader(c1)
            tr_kvbd.end_write(c0)
            tr_kvbd.add_write(c1)
            load_pair("wq", "wo")

        def ln_chunk(tc, rr, ln1_out, out_tok0=None, slot=None, lnslot=None):
            sl = tc if slot is None else slot
            lnslot = tc if lnslot is None else lnslot
            t_sq = P.op("scalar",
                        lambda e, rr=rr, sl=sl:
                        e.activation(texp[:], res[:, rr, :], AF.Square,
                                     accum_out=stats[:, sl, 1:2]),
                        waits=[tr_res[rr].reads(), tr_sqs.begin_write(),
                               tr_stats[sl].reads()])
            tr_res[rr].add_reader(t_sq)
            tr_sqs.end_write(t_sq)
            tr_stats[sl].add_write(t_sq)
            t_nm = P.op("vector",
                        lambda v, sl=sl:
                        v.tensor_scalar(stats[:, sl, 2:3], stats[:, sl, 0:1],
                                        -1.0 / D, None, ALU.mult),
                        waits=[tr_stats[sl].reads()])
            tr_stats[sl].add_write(t_nm)
            t_m2 = P.op("vector",
                        lambda v, sl=sl:
                        v.scalar_tensor_tensor(stats[:, sl, 3:4], stats[:, sl, 0:1],
                                               1.0 / (D * D), stats[:, sl, 0:1],
                                               ALU.mult, ALU.mult),
                        waits=[t_nm])
            tr_stats[sl].add_write(t_m2)
            t_var = P.op("vector",
                         lambda v, sl=sl:
                         v.scalar_tensor_tensor(stats[:, sl, 4:5], stats[:, sl, 1:2],
                                                1.0 / D, stats[:, sl, 3:4],
                                                ALU.mult, ALU.subtract),
                         waits=[t_m2, t_sq])
            tr_stats[sl].add_write(t_var)
            t_ln = P.op("scalar",
                        lambda e, sl=sl:
                        e.activation(stats[:, sl, 5:6], stats[:, sl, 4:5], AF.Ln,
                                     bias=consts[:, 0:1]),
                        waits=[t_var, tr_consts.reads()])
            tr_stats[sl].add_write(t_ln)
            t_rs = P.op("scalar",
                        lambda e, sl=sl:
                        e.activation(stats[:, sl, 6:7], stats[:, sl, 5:6], AF.Exp,
                                     scale=consts[:, 1:2]),
                        waits=[t_ln])
            tr_stats[sl].add_write(t_rs)
            t_b = P.op("scalar",
                       lambda e, sl=sl:
                       e.activation(stats[:, sl, 7:8], stats[:, sl, 2:3], AF.Copy,
                                    scale=stats[:, sl, 6:7]),
                       waits=[t_rs, t_nm])
            tr_stats[sl].add_write(t_b)
            if ln1_out:
                t_ap = P.op("scalar",
                            lambda e, rr=rr, sl=sl, lnslot=lnslot:
                            e.activation(ln1_bf[:, lnslot, :], res[:, rr, :],
                                         AF.Identity,
                                         scale=stats[:, sl, 6:7],
                                         bias=stats[:, sl, 7:8]),
                            waits=[t_b, tr_res[rr].reads(),
                                   tr_ln1[lnslot].begin_write()])
                tr_res[rr].add_reader(t_ap)
                tr_stats[sl].add_reader(t_ap)
                tr_ln1[lnslot].end_write(t_ap)
                t_lt = P.dma(f"lt{tc % 2}",
                             lambda s, tc=tc, lnslot=lnslot:
                             s.dma_start(ln1T[:, :, tc * 128:(tc + 1) * 128],
                                         ln1_bf[:, lnslot, :], transpose=True),
                             waits=[t_ap, tr_ln1T.begin_write() if tc == 0 else None])
                tr_ln1[lnslot].add_reader(t_lt)
                if tc == 0:
                    tr_ln1T.end_write(t_lt)
                else:
                    tr_ln1T.add_write(t_lt)
            else:
                orr = tc % 2
                t_ap = P.op("scalar",
                            lambda e, rr=rr, sl=sl, orr=orr:
                            e.activation(outb[:, orr, :], res[:, rr, :], AF.Identity,
                                         scale=stats[:, sl, 6:7],
                                         bias=stats[:, sl, 7:8]),
                            waits=[t_b, tr_res[rr].reads(),
                                   tr_outb[orr].begin_write()])
                tr_res[rr].add_reader(t_ap)
                tr_stats[sl].add_reader(t_ap)
                tr_outb[orr].end_write(t_ap)
                st = P.dma(f"out{orr}",
                           lambda s, orr=orr, out_tok0=out_tok0:
                           s.dma_start(out_hbm[out_tok0:out_tok0 + 128, :],
                                       outb[:, orr, :]),
                           waits=[t_ap], eng=XQ)
                tr_outb[orr].add_reader(st)

        down_toks = {}

        def down_half(sc, hl):
            pr = sc % 2
            bank = ("c0", "c1", "c2", "c3")[pr * 2 + hl]
            for hm in range(NM):
                last = (hm == NM - 1)
                tok = P.op(
                    "tensor",
                    lambda t, hm=hm, hl=hl, sc=sc, pr=pr:
                    t.matmul(d_view[:, pr, hl * 512:(hl + 1) * 512],
                             h_fm[:, hm, sc * 128:(sc + 1) * 128],
                             wd_buf[:, hm, hl * 512:(hl + 1) * 512],
                             start=(hm == 0), stop=(hm == NM - 1)),
                    waits=[tr_h[hm].reads(), tr_wd[hm].reads()]
                          + (bw((bank,)) if hm == 0 else []),
                    inc=last)
            bend((bank,), tok)
            for hm in range(NM):
                tr_h[hm].add_reader(tok)
                tr_wd[hm].add_reader(tok)
            down_toks[(sc, hl)] = tok

        def down_fin(s, g, sc, lnslot):
            base = s * SEQ + g * TILE
            pr = sc % 2
            rr = (sc + 1) % 2
            cbk = ("c0", "c1") if pr == 0 else ("c2", "c3")
            tk0 = down_toks.pop((sc, 0))
            tk1 = down_toks.pop((sc, 1))
            t_res = P.op(
                "vector",
                lambda v, pr=pr, rr=rr, sc=sc, lnslot=lnslot:
                v.scalar_tensor_tensor(res[:, rr, :], d_view[:, pr, :], INV127,
                                       ln1_bf[:, lnslot, :], ALU.mult, ALU.add,
                                       accum_out=stats[:, 4 + sc, 0:1]),
                waits=[tk0, tk1, tr_ln1[lnslot].reads(),
                       tr_res[rr].begin_write(), tr_stats[4 + sc].begin_write()])
            bread(cbk, t_res)
            tr_ln1[lnslot].add_reader(t_res)
            tr_res[rr].end_write(t_res)
            tr_stats[4 + sc].end_write(t_res)
            ln_chunk(sc, rr, ln1_out=False, out_tok0=base + sc * 128, slot=4 + sc)

        def phase2_a(s, g):
            base = s * SEQ + g * TILE
            e0 = load_x_ld(base)
            e1 = load_x_ld(base + 128)
            load_x_fin(e0, 0)
            e2 = load_x_ld(base + 256)
            load_x_fin(e1, 1)
            e3 = load_x_ld(base + 384)
            load_x_fin(e2, 2)
            load_x_fin(e3, 3)

        def phase2(s, g, stop2=None, skip_a=False, pend=None, slot3=3,
                   next_x=None, pre_pend=False, fill_jobs=None):
            base = s * SEQ + g * TILE
            if not skip_a:
                phase2_a(s, g)
            nxt = {}

            def px(step):
                if next_x is None:
                    return
                if step == 0:
                    nxt[0] = load_x_ld(next_x)
                    nxt[1] = load_x_ld(next_x + 128)
                elif step == 1:
                    load_x_fin(nxt[0], 0)
                elif step == 2:
                    nxt[2] = load_x_ld(next_x + 256)
                elif step == 3:
                    load_x_fin(nxt[1], 1)
                elif step == 4:
                    nxt[3] = load_x_ld(next_x + 384)
                elif step == 5:
                    load_x_fin(nxt[2], 2)
                    load_x_fin(nxt[3], 3)

            if pre_pend and pend is not None:
                down_half(0, 0)
                down_half(0, 1)
            q_toks = {}
            for i in range(NJ):
                abk = "a0" if i % 2 == 0 else "a1"
                for j in range(NJ):
                    last = (j == NJ - 1)
                    tok = P.op(
                        "tensor",
                        lambda t, i=i, j=j:
                        t.matmul(ps_a[:, (i % 2) * 512:(i % 2 + 1) * 512],
                                 wpair[:, 0, j, i * 128:(i + 1) * 128],
                                 xT[:, j, :], start=(j == 0), stop=(j == NJ - 1)),
                        waits=[tt.reads() for tt in tr_xT] + [tr_wpair[0].reads()]
                              + (bw((abk,)) if j == 0 else []),
                        inc=last)
                bend((abk,), tok)
                tr_wpair[0].add_reader(tok)
                for tt in tr_xT:
                    tt.add_reader(tok)
                t_min = P.op("vector",
                             lambda v, i=i:
                             v.tensor_scalar_min(t1[:, 0:TILE],
                                                 ps_a[:, (i % 2) * 512:(i % 2 + 1) * 512],
                                                 0.0),
                             waits=[tok, tr_t1.begin_write()])
                bread((abk,), t_min)
                tr_t1.end_write(t_min)
                t_exp = P.op("scalar",
                             lambda e: e.activation(texp[:, 0:TILE], t1[:, 0:TILE],
                                                    AF.Exp),
                             waits=[t_min, tr_texp.begin_write()])
                tr_t1.add_reader(t_exp)
                tr_texp.end_write(t_exp)
                t_q = P.op("vector",
                           lambda v, i=i:
                           v.scalar_tensor_tensor(q_fm[:, i, :],
                                                  ps_a[:, (i % 2) * 512:(i % 2 + 1) * 512],
                                                  0.0, texp[:, 0:TILE], ALU.max, ALU.add),
                           waits=[t_exp, tok,
                                  tr_q.begin_write() if i == 0 else None])
                bread((abk,), t_q)
                tr_texp.add_reader(t_q)
                q_toks[i] = t_q
                if i == 0:
                    tr_q.end_write(t_q)
                else:
                    tr_q.add_write(t_q)

            def emit_c1h(tc, half):
                ar = tc % 2
                for p in range(4):
                    c = half * 4 + p
                    last = (p == 3)
                    tok = P.op(
                        "tensor",
                        lambda t, c=c, p=p, tc=tc:
                        t.matmul(outa_view[:, p, 0:2 * (HD + 1)],
                                 q_fm[:, c, tc * 128:(tc + 1) * 128],
                                 kvbd[:, c, :],
                                 start=True, stop=True),
                        waits=[q_toks[c], tr_kvbd.reads()]
                              + (bw(("a0",)) if p == 0 else
                                 (bw(("a1",)) if p == 2 else [])),
                        inc=last)
                bend(("a0", "a1"), tok)
                tr_q.add_reader(tok)
                tr_kvbd.add_reader(tok)
                if stop2 == "P2o" and tc == 0 and half == 1:
                    cp = P.op("vector",
                              lambda v: v.tensor_copy(
                                  h_fm[:, 0:2, :].rearrange("p a b -> p (a b)"),
                                  ps_a[:]),
                              waits=[tok])
                    dbg_dump(h_fm[:, 0:2, :].rearrange("p a b -> p (a b)"), D,
                             waits=[cp])
                    return True
                pair = outa_view[:, :, 0:2 * (HD + 1)].rearrange(
                    "p c (q e) -> p c q e", q=2)
                hs = slice(half * 4, half * 4 + 4)
                t_z = P.op("vector",
                           lambda v, pair=pair, tc=tc, hs=hs:
                           v.tensor_scalar_add(zr[:, tc, 0, hs, :],
                                               pair[:, :, :, HD], EPS_ATTN),
                           waits=brd(("a0", "a1"))
                                 + [tr_zr[tc].begin_write() if half == 0 else None])
                bread(("a0", "a1"), t_z)
                if half == 0:
                    tr_zr[tc].end_write(t_z)
                else:
                    tr_zr[tc].add_write(t_z)
                t_r = P.op("vector",
                           lambda v, tc=tc, hs=hs:
                           v.reciprocal(zr[:, tc, 1, hs, :], zr[:, tc, 0, hs, :]),
                           waits=[t_z])
                tr_zr[tc].add_write(t_r)
                t_dv = P.op(
                    "vector",
                    lambda v, ar=ar, pair=pair, tc=tc, hs=hs, half=half:
                    v.tensor_tensor(attn_bf[:, ar, half * 512:(half + 1) * 512]
                                    .rearrange("p (c q e) -> p c q e", c=4, q=2),
                                    pair[:, :, :, 0:HD],
                                    zr[:, tc, 1, hs, :].unsqueeze(3).broadcast_to(
                                        [128, 4, 2, HD]),
                                    ALU.mult),
                    waits=[t_r] + brd(("a0", "a1"))
                          + [tr_attn[ar].begin_write() if half == 0 else None])
                bread(("a0", "a1"), t_dv)
                tr_zr[tc].add_reader(t_dv)
                if half == 0:
                    tr_attn[ar].end_write(t_dv)
                    return False
                tr_attn[ar].add_write(t_dv)
                if stop2 == "P2d" and tc == 0:
                    dbg_dump(attn_bf[:, ar, :], D, waits=[t_dv])
                    return True
                t_at = P.dma(f"at{ar}",
                             lambda s, ar=ar, tc=tc:
                             s.dma_start(attnT[:, tc, :, :], attn_bf[:, ar, :],
                                         transpose=True),
                             waits=[t_dv, tr_attnT[tc].begin_write()])
                tr_attn[ar].add_reader(t_at)
                tr_attnT[tc].end_write(t_at)

                return False

            def emit_c2(tc):
                lnsl = tc if tc < CPT - 1 else slot3
                for e_ in range(NJ):
                    for hl in range(2):
                        last = (e_ == NJ - 1 and hl == 1)
                        tok = P.op(
                            "tensor",
                            lambda t, e_=e_, hl=hl, tc=tc:
                            t.matmul(ps_b[:, hl * 512:(hl + 1) * 512],
                                     attnT[:, tc, e_, :],
                                     wpair[:, 1, e_, hl * 512:(hl + 1) * 512],
                                     start=(e_ == 0), stop=(e_ == NJ - 1)),
                            waits=[tr_attnT[tc].reads(), tr_wpair[1].reads()]
                                  + (bw(("b0", "b1")) if (e_ == 0 and hl == 0) else []),
                            inc=last)
                bend(("b0", "b1"), tok)
                tr_attnT[tc].add_reader(tok)
                tr_wpair[1].add_reader(tok)
                if stop2 == "P2w" and tc == 0:
                    cp = P.op("vector",
                              lambda v: v.tensor_copy(
                                  h_fm[:, 0:2, :].rearrange("p a b -> p (a b)"),
                                  ps_b[:]),
                              waits=[tok])
                    dbg_dump(h_fm[:, 0:2, :].rearrange("p a b -> p (a b)"), D,
                             waits=[cp])
                    return True
                rr = tc % 2
                t_res = P.op(
                    "vector",
                    lambda v, rr=rr, tc=tc:
                    v.scalar_tensor_tensor(res[:, rr, :], ps_b[:], 1.0, x_bf[:, tc, :],
                                           ALU.mult, ALU.add,
                                           accum_out=stats[:, tc, 0:1]),
                    waits=brd(("b0", "b1")) + [tr_xbf[tc].reads(),
                                               tr_res[rr].begin_write(),
                                               tr_stats[tc].begin_write()])
                bread(("b0", "b1"), t_res)
                tr_xbf[tc].add_reader(t_res)
                tr_res[rr].end_write(t_res)
                tr_stats[tc].end_write(t_res)
                ln_chunk(tc, rr, ln1_out=True, lnslot=lnsl)
                return False

            def pend_half(sc, hl):
                if pend is not None:
                    down_half(sc, hl)
                elif fill_jobs:
                    fill_jobs.pop(0)()

            def pend_fin(sc):
                if pend is not None:
                    ps_, pg_, psl3 = pend
                    down_fin(ps_, pg_, sc, sc if sc < CPT - 1 else psl3)

            if emit_c1h(0, 0):
                return True
            if emit_c1h(0, 1):
                return True
            for tc in range(1, CPT):
                px((tc - 1) * 2)
                if pre_pend:
                    if tc < CPT - 1:
                        pend_fin(tc - 1)
                        pend_half(tc, 0)
                else:
                    pend_half(tc - 1, 0)
                if emit_c1h(tc, 0):
                    return True
                if pre_pend:
                    if tc < CPT - 1:
                        pend_half(tc, 1)
                else:
                    pend_half(tc - 1, 1)
                if emit_c1h(tc, 1):
                    return True
                if not pre_pend or tc == CPT - 1:
                    pend_fin(tc - 1)
                if emit_c2(tc - 1):
                    return True
                if tc < CPT - 1:
                    px(tc * 2 - 1)
            if emit_c2(CPT - 1):
                return True
            px(5)
            pend_half(CPT - 1, 0)
            pend_half(CPT - 1, 1)
            pend_fin(CPT - 1)
            return False

        slab_state = {"lg": [None] * NM, "lu": [None] * NM, "st": [],
                      "first": True, "pre4": False}

        def load_slab(m, first_tile=False, both_gp=False, defer_vec=None):
            r = m % 4
            lg_toks = slab_state["lg"]
            lu_toks = slab_state["lu"]
            w3_stores = slab_state["st"]
            if first_tile:
                for gi, (mat, img) in enumerate((("wg", wi_imgs["wgi"]),
                                                 ("wu", wi_imgs["wui"]))):
                    sl = (2 * m + gi) % 2
                    ld = P.dma(f"w3i{sl}",
                               lambda sy, img=img, m=m, sl=sl:
                               sy.dma_start(stg_i32[sl], img[m]),
                               waits=[tr_stgi[sl].begin_write(),
                                      tr_outb[sl].begin_write()],
                               eng=BULKQ)
                    tr_stgi[sl].end_write(ld)
                    tr_outb[sl].end_write(ld)

                    def mk_ct(gi=gi, r=r, sl=sl, ld=ld, m=m, mat=mat,
                              both_gp=both_gp):
                        eng = "gpsimd" if (both_gp or gi == 1) else "vector"
                        ct = P.op(eng,
                                  lambda e, gi=gi, r=r, sl=sl:
                                  e.tensor_copy(wgu[:, gi, r, :], stg_i32[sl]),
                                  waits=[ld, tr_wgu[gi][r].begin_write()]
                                        + ([tr_stgi[2 + gi].begin_write()]
                                           if m < 2 else []))
                        tr_stgi[sl].add_reader(ct)
                        tr_outb[sl].add_reader(ct)
                        tr_wgu[gi][r].end_write(ct)
                        if gi == 0:
                            slab_state["lg"][m] = ct
                        else:
                            slab_state["lu"][m] = ct

                        def mk_st(mat=mat, m=m, gi=gi, r=r, ct=ct):
                            st = P.dma(f"scw{(2 * m + gi) % 2}",
                                       lambda sy, mat=mat, m=m, gi=gi, r=r:
                                       sy.dma_start(scr[mat][m], wgu[:, gi, r, :]),
                                       waits=[ct], eng=BULKQ)
                            tr_wgu[gi][r].add_reader(st)
                            tr_scr[mat][m].add_write(st)
                        slab_state["st"].append(mk_st)
                        while len(slab_state["st"]) > 2:
                            slab_state["st"].pop(0)()
                    if gi == 0 and defer_vec is not None:
                        defer_vec.append(mk_ct)
                    else:
                        mk_ct()
                return
            lg = P.dma(f"wg{r % 2}",
                       lambda sy, m=m, r=r:
                       sy.dma_start(wgu[:, 0, r, :], scr["wg"][m]),
                       waits=[tr_wgu[0][r].begin_write(), tr_scr["wg"][m].reads()]
                             + ([tr_stgi[2].begin_write()] if m < 2 else []),
                       eng=BULKQ)
            tr_wgu[0][r].end_write(lg)
            lg_toks[m] = lg
            lu = P.dma(f"wu{r % 2}",
                       lambda sy, m=m, r=r:
                       sy.dma_start(wgu[:, 1, r, :], scr["wu"][m]),
                       waits=[tr_wgu[1][r].begin_write(), tr_scr["wu"][m].reads()]
                             + ([tr_stgi[3].begin_write()] if m < 2 else []),
                       eng=BULKQ)
            tr_wgu[1][r].end_write(lu)
            lu_toks[m] = lu

        def phase3(s, g, next_a=None, next_kv_pair=False, preload_next=False):
            base = s * SEQ + g * TILE
            first_tile = slab_state["first"]
            slab_state["first"] = False
            if next_kv_pair:
                load_pair("wk", "wv")
            lg_toks = slab_state["lg"]
            lu_toks = slab_state["lu"]
            w3_stores = slab_state["st"]

            if first_tile:
                if not slab_state["pre4"]:
                    for m in range(4):
                        load_slab(m, True)
            elif not PRELOAD_SLABS:
                for m in range(4):
                    load_slab(m)
            vc = slab_state.setdefault("vc", [])
            for m in range(NM):
                if vc:
                    vc.pop(0)()
                r = m % 4
                abk = "a0" if m % 2 == 0 else "a1"
                bbk = "b0" if m % 2 == 0 else "b1"
                lg = lg_toks[m]
                lu = lu_toks[m]
                if next_a is not None and m == 12:
                    next_a()
                    next_a = None

                for j in range(NJ):
                    last = (j == NJ - 1)
                    tokg = P.op(
                        "tensor",
                        lambda t, j=j, r=r, m=m:
                        t.matmul(ps_a[:, (m % 2) * 512:(m % 2 + 1) * 512],
                                 wgu[:, 0, r, j * 128:(j + 1) * 128],
                                 ln1T[:, j, :], start=(j == 0), stop=(j == NJ - 1)),
                        waits=[lg, tr_ln1T.reads()] + (bw((abk,)) if j == 0 else []),
                        inc=last)
                bend((abk,), tokg)
                tr_wgu[0][r].add_reader(tokg)
                tr_ln1T.add_reader(tokg)
                for j in range(NJ):
                    last = (j == NJ - 1)
                    toku = P.op(
                        "tensor",
                        lambda t, j=j, r=r, m=m:
                        t.matmul(ps_b[:, (m % 2) * 512:(m % 2 + 1) * 512],
                                 wgu[:, 1, r, j * 128:(j + 1) * 128],
                                 ln1T[:, j, :], start=(j == 0), stop=(j == NJ - 1)),
                        waits=[lu, tr_ln1T.reads()] + (bw((bbk,)) if j == 0 else []),
                        inc=last)
                bend((bbk,), toku)
                tr_wgu[1][r].add_reader(toku)
                tr_ln1T.add_reader(toku)
                if m + 4 < NM:
                    load_slab(m + 4, first_tile,
                              defer_vec=(vc if first_tile else None))
                elif preload_next and m == NM - 1:
                    while w3_stores:
                        w3_stores.pop(0)()
                    for mm in range(4):
                        load_slab(mm)

                t_si = P.op("scalar",
                            lambda e, m=m:
                            e.activation(silu_b[:, m % 2, :],
                                         ps_a[:, (m % 2) * 512:(m % 2 + 1) * 512],
                                         AF.Silu, scale=consts[:, 2:3]),
                            waits=[tokg, tr_silu[m % 2].begin_write(), tr_consts.reads()])
                bread((abk,), t_si)
                tr_silu[m % 2].end_write(t_si)
                t_h = P.op("vector",
                           lambda v, m=m:
                           v.scalar_tensor_tensor(h_fm[:, m, :],
                                                  ps_b[:, (m % 2) * 512:(m % 2 + 1) * 512],
                                                  INV127, silu_b[:, m % 2, :],
                                                  ALU.mult, ALU.mult),
                           waits=[t_si, toku, tr_h[m].begin_write()]
                                 + ([tr_stgb[m // 2].begin_write()] if m < 8 else []))
                bread((bbk,), t_h)
                tr_silu[m % 2].add_reader(t_h)
                tr_h[m].end_write(t_h)
                if m == NM - 1 and PRELOAD:
                    P.op("scalar", lambda e: e.preload_act_table(AF.Ln), waits=[t_si])

            while vc:
                vc.pop(0)()
            while w3_stores:
                w3_stores.pop(0)()

        pend_tile = [None]
        for s in range(n_seq):
            phase1(s, first=(s == 0))
            if stop_after == "P1":
                dbg_dump(kvbd[:].rearrange("p a b -> p (a b)"), NJ * 2 * (HD + 1),
                         waits=[tr_kvbd.reads()])
                P.flush_block()
                return nc
            prefetched = [True]
            for g in range(n_tiles):
                tidx = s * n_tiles + g
                sl3 = CPT - 1 + (tidx % 2)
                nx = (s * SEQ + (g + 1) * TILE) if g + 1 < n_tiles else None
                fj = None
                if tidx == 0:
                    fj = [lambda m=m: load_slab(m, True, both_gp=True)
                          for m in range(4)]
                    slab_state["pre4"] = True
                r2 = phase2(s, g, stop2=stop_after, skip_a=prefetched[0],
                            pend=pend_tile[0], slot3=sl3, next_x=nx,
                            pre_pend=(g == 0), fill_jobs=fj)
                pend_tile[0] = (s, g, sl3)
                prefetched[0] = False
                if r2:
                    P.flush_block()
                    return nc
                if stop_after == "P2":
                    dbg_dump(ln1_bf[:, 0:CPT, :].rearrange("p a b -> p (a b)"),
                             CPT * D, waits=[t.reads() for t in tr_ln1])
                    P.flush_block()
                    return nc
                if g + 1 < n_tiles:
                    prefetched[0] = True
                    phase3(s, g, preload_next=PRELOAD_SLABS)
                else:
                    phase3(s, g, next_kv_pair=(s + 1 < n_seq),
                           preload_next=(PRELOAD_SLABS and s + 1 < n_seq))
        fs, fg, fsl3 = pend_tile[0]
        for sc in range(CPT):
            down_half(sc, 0)
            down_half(sc, 1)
            down_fin(fs, fg, sc, sc if sc < CPT - 1 else fsl3)

        P.sync_wait([(P.dma_sems["out0"], P.dma_count["out0"]),
                     (P.dma_sems["out1"], P.dma_count["out1"])])
        P.flush_block()

    return nc


def _w_img(w_t):
    return np.ascontiguousarray(
        w_t.reshape(NJ, 128, D).transpose(1, 0, 2).reshape(128, NJ * D))


def kernel(L0_attn_x, L0_attn_Wq, L0_attn_Wk, L0_attn_Wv, L0_attn_Wo,
           L0_ffn_w_gate, L0_ffn_w_up, L0_ffn_w_down):
    X = np.ascontiguousarray(np.asarray(L0_attn_x))
    Wq = np.asarray(L0_attn_Wq)
    Wk = np.asarray(L0_attn_Wk)
    Wv = np.asarray(L0_attn_Wv)
    Wo = np.asarray(L0_attn_Wo)
    wg = np.asarray(L0_ffn_w_gate)
    wu = np.asarray(L0_ffn_w_up)
    wd = np.asarray(L0_ffn_w_down)

    imgs = {
        "wq": _w_img(np.ascontiguousarray(Wq.T)),
        "wk": _w_img(np.ascontiguousarray(Wk.T)),
        "wv": _w_img(np.ascontiguousarray(Wv.T)),
        "wo": _w_img(np.ascontiguousarray(Wo.T)),
        "wgi": np.ascontiguousarray(
            wg.T.reshape(NJ, 128, NM, 128).transpose(2, 1, 0, 3).reshape(NM, 128, D)),
        "wui": np.ascontiguousarray(
            wu.T.reshape(NJ, 128, NM, 128).transpose(2, 1, 0, 3).reshape(NM, 128, D)),
        "wdi": np.ascontiguousarray(wd.T.reshape(NM, 128, D)),
    }

    nc = build_program()
    in_maps = []
    for c in range(NCORES):
        im = {"x": np.ascontiguousarray(
            X[NSEQ * c:NSEQ * (c + 1)].reshape(TOKENS, D))}
        im.update(imgs)
        in_maps.append(im)

    resu = run_bass_kernel_spmd(nc, in_maps, core_ids=list(range(NCORES)))
    out = np.empty((16, SEQ, D), dtype=np.float32)
    for c in range(NCORES):
        out[NSEQ * c:NSEQ * (c + 1)] = np.asarray(
            resu.results[c]["out"]).reshape(NSEQ, SEQ, D)
    return out


# version 30
# speedup vs baseline: 1.0258x; 1.0118x over previous
import contextlib
import numpy as np

import concourse.bass as bass
import concourse.mybir as mybir
from concourse.bass_utils import run_bass_kernel_spmd

F32 = mybir.dt.float32
BF16 = mybir.dt.bfloat16
I32 = mybir.dt.int32
AF = mybir.ActivationFunctionType
ALU = mybir.AluOpType

NCORES = 8
SEQ = 2048
NSEQ = 2
D = 1024
NJ = 8
H = 16
HD = 64
FF = 3072
NM = 24
TOKENS = NSEQ * SEQ
TILE = 512
NTILE = TOKENS // TILE
CPT = TILE // 128
INV127 = 1.0 / 127.0
EPS_ATTN = 1e-6
EPS_LN = 1e-5
PRELOAD = False
WKEEP = 3
PRELOAD_SLABS = True
BULKQ = "sync"
WQ = "sync"
XQ = "sync"


def _merge(d, tok):
    if tok is None:
        return
    sem, v = tok
    if d.get(sem, 0) < v:
        d[sem] = v


class Track:
    def __init__(self):
        self.ready = {}
        self.readers = {}

    def begin_write(self):
        w = dict(self.readers)
        for s, v in self.ready.items():
            if w.get(s, 0) < v:
                w[s] = v
        return w

    def end_write(self, tok):
        self.ready = {}
        self.readers = {}
        _merge(self.ready, tok)

    def add_write(self, tok):
        _merge(self.ready, tok)

    def reads(self):
        return dict(self.ready)

    def add_reader(self, tok):
        _merge(self.readers, tok)


class Prog:
    ENGS = ("sync", "tensor", "vector", "scalar", "gpsimd")

    def __init__(self, nc, stack):
        self.nc = nc
        self.sems = {}
        self.count = {}
        for e in ("tensor", "vector", "scalar", "gpsimd"):
            self.sems[e] = stack.enter_context(nc.semaphore(f"sem_{e}"))
            self.count[e] = 0
        self.dma_sems = {}
        self.dma_count = {}
        self.stack = stack
        self._reset_block()

    def _reset_block(self):
        self.ops = {e: [] for e in self.ENGS}
        self.waited = {e: {} for e in self.ENGS}

    def dma_slot(self, name):
        if name not in self.dma_sems:
            self.dma_sems[name] = self.stack.enter_context(
                self.nc.semaphore(f"dsl_{name}"))
            self.dma_count[name] = 0
        return name

    def _gather(self, waits):
        wd = {}
        for w in waits:
            if w is None:
                continue
            if isinstance(w, dict):
                for s, v in w.items():
                    _merge(wd, (s, v))
            else:
                _merge(wd, w)
        return wd

    def _emit_waits(self, eng_name, wd):
        out = []
        seen = self.waited[eng_name]
        for sem, val in sorted(wd.items(), key=lambda kv: kv[0].num):
            if seen.get(sem, 0) < val:
                seen[sem] = val
                out.append((sem, val))
        return out

    def op(self, eng_name, fn, waits=(), inc=True):
        real_waits = self._emit_waits(eng_name, self._gather(waits))
        tok = None
        sem = self.sems[eng_name]
        if inc:
            self.count[eng_name] += 1
            tok = (sem, self.count[eng_name])

        def closure(eng, fn=fn, real_waits=real_waits, tok=tok, sem=sem):
            for s, v in real_waits:
                eng.wait_ge(s, v)
            inst = fn(eng)
            if tok is not None:
                inst.then_inc(sem, 1)

        self.ops[eng_name].append(closure)
        return tok

    def dma(self, slot, fn, waits=(), eng="sync"):
        slot = self.dma_slot(slot)
        wd = self._gather(waits)
        sem = self.dma_sems[slot]
        prev = self.dma_count[slot]
        if prev:
            _merge(wd, (sem, prev))
        real_waits = self._emit_waits(eng, wd)
        self.dma_count[slot] = prev + 16
        tok = (sem, self.dma_count[slot])

        def closure(e, fn=fn, real_waits=real_waits, sem=sem):
            for s, v in real_waits:
                e.wait_ge(s, v)
            fn(e).then_inc(sem, 16)

        self.ops[eng].append(closure)
        return tok

    def sync_wait(self, waits):
        real_waits = self._emit_waits("sync", self._gather(waits))

        def closure(eng, real_waits=real_waits):
            for s, v in real_waits:
                eng.wait_ge(s, v)

        self.ops["sync"].append(closure)

    def flush_block(self):
        nc = self.nc
        ops = self.ops
        with nc.Block() as blk:
            if ops["sync"]:
                blk.sync(lambda e, fns=ops["sync"]: [f(e) for f in fns])
            if ops["tensor"]:
                blk.tensor(lambda e, fns=ops["tensor"]: [f(e) for f in fns])
            if ops["vector"]:
                blk.vector(lambda e, fns=ops["vector"]: [f(e) for f in fns])
            if ops["scalar"]:
                blk.scalar(lambda e, fns=ops["scalar"]: [f(e) for f in fns])
            if ops["gpsimd"]:
                blk.gpsimd(lambda e, fns=ops["gpsimd"]: [f(e) for f in fns])
        self._reset_block()


def build_program(n_seq=NSEQ, n_tiles=NTILE // NSEQ, stop_after=None):
    nc = bass.Bass("TRN2", target_bir_lowering=False,
                   detect_race_conditions=False, dynamic_dma_scratch_size=2048)

    x_hbm = nc.dram_tensor("x", [TOKENS, D], F32, kind="ExternalInput")
    w_imgs = {n: nc.dram_tensor(n, [128, NJ * D], F32, kind="ExternalInput")
              for n in ("wq", "wk", "wv", "wo")}
    wi_imgs = {n: nc.dram_tensor(n, [NM, 128, D], I32, kind="ExternalInput")
               for n in ("wgi", "wui", "wdi")}
    out_hbm = nc.dram_tensor("out", [TOKENS, D], F32, kind="ExternalOutput")
    scr = {n: nc.dram_tensor(f"scr_{n}", [NM, 128, D], BF16)
           for n in ("wg", "wu", "wd")}
    scr_qkvo = {n: nc.dram_tensor(f"scr_{n}", [128, NJ * D], BF16)
                for n in ("wq", "wk", "wv", "wo")}
    dbg_hbm = None
    if stop_after is not None:
        dbg_hbm = nc.dram_tensor("dbg", [128, 8192], BF16, kind="ExternalOutput")

    with contextlib.ExitStack() as ctx:
        E = ctx.enter_context
        wpair = E(nc.sbuf_tensor("wpair", [128, 2, NJ, D], BF16))
        wd_buf = E(nc.sbuf_tensor("wd_buf", [128, NM, D], BF16))
        wgu = E(nc.sbuf_tensor("wgu", [128, 2, 4, D], BF16))
        h_fm = E(nc.sbuf_tensor("h_fm", [128, NM, TILE], BF16))
        ln1_bf = E(nc.sbuf_tensor("ln1_bf", [128, CPT + 1, D], BF16))
        ln1T = E(nc.sbuf_tensor("ln1T", [128, NJ, TILE], BF16))
        x_f32 = E(nc.sbuf_tensor("x_f32", [128, 2, D], F32))
        x_bf = E(nc.sbuf_tensor("x_bf", [128, CPT, D], BF16))
        xT = E(nc.sbuf_tensor("xT", [128, NJ, TILE], BF16))
        q_fm = E(nc.sbuf_tensor("q_fm", [128, NJ, TILE], BF16))
        k_tm = E(nc.sbuf_tensor("k_tm", [128, 2, D], BF16))
        v_tm = E(nc.sbuf_tensor("v_tm", [128, 2, H, HD + 1], BF16))
        kv_acc = E(nc.sbuf_tensor("kv_acc", [128, NJ, HD + 1], F32))
        kvbd = E(nc.sbuf_tensor("kvbd", [128, NJ, 2 * (HD + 1)], BF16))
        attn_bf = E(nc.sbuf_tensor("attn_bf", [128, 2, D], BF16))
        attnT = E(nc.sbuf_tensor("attnT", [128, CPT, NJ, 128], BF16))
        t1 = E(nc.sbuf_tensor("t1", [128, D], BF16))
        texp = E(nc.sbuf_tensor("texp", [128, D], BF16))
        res = E(nc.sbuf_tensor("res", [128, 2, D], F32))
        stats = E(nc.sbuf_tensor("stats", [128, 2 * CPT, 8], F32))
        zr = E(nc.sbuf_tensor("zr", [128, CPT, 2, NJ, 2], F32))
        outb = E(nc.sbuf_tensor("outb", [128, 2, D], F32))
        silu_b = E(nc.sbuf_tensor("silu_b", [128, 2, TILE], BF16))
        consts = E(nc.sbuf_tensor("consts", [128, 4], F32))
        ps_a = E(nc.psum_tensor("ps_a", [128, 1024], F32))
        ps_b = E(nc.psum_tensor("ps_b", [128, 1024], F32))
        ps_c = E(nc.psum_tensor("ps_c", [128, 2048], F32))

        P = Prog(nc, ctx)

        BK = {k: Track() for k in ("a0", "a1", "b0", "b1", "c0", "c1", "c2", "c3")}

        def bw(keys):
            return [BK[k].begin_write() for k in keys]

        def brd(keys):
            return [BK[k].reads() for k in keys]

        def bend(keys, tok):
            for k in keys:
                BK[k].end_write(tok)

        def bread(keys, tok):
            for k in keys:
                BK[k].add_reader(tok)

        kv_view = ps_c[:].rearrange("p (r c e) -> p r c e", r=2, c=NJ)
        out_view = ps_c[:].rearrange("p (c e) -> p c e", c=NJ)
        outa_view = ps_a[:].rearrange("p (c e) -> p c e", c=CPT)
        d_view = ps_c[:].rearrange("p (r e) -> p r e", r=2)

        tr_x = [Track() for _ in range(2)]
        tr_xbf = [Track() for _ in range(CPT)]
        tr_xT = [Track() for _ in range(CPT)]
        tr_wp = [[Track() for _ in range(NJ)] for _ in range(2)]
        tr_ktm = [Track() for _ in range(2)]
        tr_vtm = [Track() for _ in range(2)]
        tr_kvacc = Track()
        tr_kvbd = Track()
        tr_q = Track()
        tr_attn = [Track() for _ in range(2)]
        tr_attnT = [Track() for _ in range(CPT)]
        tr_t1 = Track()
        tr_texp = Track()
        tr_sqs = tr_texp
        tr_res = [Track() for _ in range(2)]
        tr_stats = [Track() for _ in range(2 * CPT)]
        tr_zr = [Track() for _ in range(CPT)]
        tr_outb = [Track() for _ in range(2)]
        tr_ln1 = [Track() for _ in range(CPT + 1)]
        tr_ln1T = Track()
        tr_h = [Track() for _ in range(NM)]
        tr_wgu = [[Track() for _ in range(4)] for _ in range(2)]
        tr_wd = [Track() for _ in range(NM)]
        tr_silu = [Track() for _ in range(2)]
        tr_stgi = [Track() for _ in range(4)]
        tr_stgb = [Track() for _ in range(4)]
        tr_scr = {n: [Track() for _ in range(NM)] for n in ("wg", "wu")}
        tr_scr.update({n: [Track() for _ in range(NJ)]
                       for n in ("wq", "wk", "wv", "wo")})
        tr_consts = Track()

        tok = P.op("vector", lambda v: v.memset(consts[:, 0:1], EPS_LN))
        tr_consts.add_write(tok)
        tok = P.op("vector", lambda v: v.memset(consts[:, 1:2], -0.5))
        tr_consts.add_write(tok)
        tok = P.op("vector", lambda v: v.memset(consts[:, 2:3], INV127))
        tr_consts.add_write(tok)
        for r in range(2):
            tok = P.op("vector", lambda v, r=r: v.memset(v_tm[:, r, :, HD:HD + 1], 1.0))
            tr_vtm[r].add_write(tok)

        stg_i32 = [outb[:, 0, :].bitcast(I32), outb[:, 1, :].bitcast(I32),
                   wgu[:, 0, :, :].rearrange("p a b -> p (a b)").bitcast(I32)[:, 0:D],
                   wgu[:, 1, :, :].rearrange("p a b -> p (a b)").bitcast(I32)[:, 0:D]]
        stg_bf = [h_fm[:, 2 * i:2 * i + 2, :].rearrange("p a b -> p (a b)")
                  for i in range(4)]

        w_jobs = []
        for mat in ("wq", "wo"):
            for j in range(NJ):
                w_jobs.append(("f32", mat, w_imgs[mat], j))
        for m in range(NM):
            w_jobs.append(("wdd", "wd", wi_imgs["wdi"], m))
        for mat, si in (("wk", 0), ("wv", 1)):
            w_jobs.append(("pair_st", mat, si, 0))
        w_pos = [0]
        w_stores = []

        def flush_w_stores(keep=0):
            while len(w_stores) > keep:
                w_stores.pop(0)()

        def drain_w_jobs(k):
            for _ in range(k):
                if w_pos[0] >= len(w_jobs):
                    flush_w_stores()
                    return
                kind, mat, src_t, m = w_jobs[w_pos[0]]
                r = w_pos[0] % 4
                w_pos[0] += 1
                if kind == "pair_st":
                    si = src_t
                    st = P.dma(f"pst{si}",
                               lambda s, mat=mat, si=si:
                               s.dma_start(scr_qkvo[mat][:],
                                           wpair[:, si, :, :].rearrange(
                                               "p a b -> p (a b)")),
                               waits=[t.reads() for t in tr_wp[si]],
                               eng=BULKQ)
                    for t in tr_wp[si]:
                        t.add_reader(st)
                    for t in tr_scr[mat]:
                        t.add_write(st)
                    continue
                if kind == "int" or kind == "wdd":
                    extra = [tr_outb[r].begin_write()] if r < 2 else []
                    ld = P.dma(f"wi{r}",
                               lambda s, src_t=src_t, m=m, r=r:
                               s.dma_start(stg_i32[r], src_t[m]),
                               waits=[tr_stgi[r].begin_write()] + extra,
                               eng=WQ)
                    tr_stgi[r].end_write(ld)
                    if r < 2:
                        tr_outb[r].end_write(ld)
                    if kind == "wdd":
                        ct = P.op("vector" if r % 2 == 0 else "gpsimd",
                                  lambda e, r=r, m=m:
                                  e.tensor_copy(wd_buf[:, m, :], stg_i32[r]),
                                  waits=[ld, tr_wd[m].begin_write()])
                        tr_stgi[r].add_reader(ct)
                        if r < 2:
                            tr_outb[r].add_reader(ct)
                        tr_wd[m].end_write(ct)
                        continue
                    ct = P.op("vector",
                              lambda e, r=r: e.tensor_copy(stg_bf[r], stg_i32[r]),
                              waits=[ld, tr_stgb[r].begin_write()])
                    tr_stgi[r].add_reader(ct)
                    if r < 2:
                        tr_outb[r].add_reader(ct)
                    tr_stgb[r].end_write(ct)
                    def mk_store(mat=mat, m=m, r=r, ct=ct):
                        st = P.dma(f"scw{r}",
                                   lambda s, mat=mat, m=m, r=r:
                                   s.dma_start(scr[mat][m], stg_bf[r]),
                                   waits=[ct], eng=BULKQ)
                        tr_stgb[r].add_reader(st)
                        tr_scr[mat][m].add_write(st)
                    w_stores.append(mk_store)
                    flush_w_stores(keep=WKEEP)
                else:
                    rr2 = w_pos[0] % 2
                    ld = P.dma(f"wf{rr2}",
                               lambda s, src_t=src_t, m=m, rr2=rr2:
                               s.dma_start(res[:, rr2, :], src_t[:, m * D:(m + 1) * D]),
                               waits=[tr_res[rr2].begin_write()], eng=WQ)
                    tr_res[rr2].end_write(ld)
                    ct = P.op("scalar",
                              lambda e, r=r, rr2=rr2:
                              e.activation(stg_bf[r], res[:, rr2, :], AF.Copy),
                              waits=[ld, tr_stgb[r].begin_write()])
                    tr_res[rr2].add_reader(ct)
                    tr_stgb[r].end_write(ct)
                    def mk_store2(mat=mat, m=m, r=r, ct=ct):
                        st = P.dma(f"scw{r}",
                                   lambda s, mat=mat, m=m, r=r:
                                   s.dma_start(scr_qkvo[mat][:, m * D:(m + 1) * D],
                                               stg_bf[r]),
                                   waits=[ct], eng=BULKQ)
                        tr_stgb[r].add_reader(st)
                        tr_scr[mat][m].add_write(st)
                    w_stores.append(mk_store2)
                    flush_w_stores(keep=WKEEP)

        def load_pair_direct(m0, m1):
            stg4 = [x_f32[:, 0, :], x_f32[:, 1, :], outb[:, 0, :], outb[:, 1, :]]
            trs4 = [tr_x[0], tr_x[1], tr_outb[0], tr_outb[1]]
            for j in range(NJ):
                for si, mat in enumerate((m0, m1)):
                    idx = 2 * j + si
                    r = idx % 4
                    fsrc = w_imgs[mat]
                    ld = P.dma(f"wp{r}",
                               lambda s, fsrc=fsrc, j=j, r=r:
                               s.dma_start(stg4[r], fsrc[:, j * D:(j + 1) * D]),
                               waits=[trs4[r].begin_write()], eng=BULKQ)
                    trs4[r].end_write(ld)
                    ct = P.op("scalar" if idx % 2 == 0 else "vector",
                              lambda e, si=si, j=j, r=r:
                              (e.activation(wpair[:, si, j, :], stg4[r], AF.Copy)
                               if hasattr(e, "activation") else
                               e.tensor_copy(wpair[:, si, j, :], stg4[r])),
                              waits=[ld, tr_wp[si][j].begin_write()])
                    trs4[r].add_reader(ct)
                    tr_wp[si][j].end_write(ct)

        def load_pair(m0, m1):
            for si, mat in enumerate((m0, m1)):
                ld = P.dma(f"wp{si}",
                           lambda s, si=si, mat=mat:
                           s.dma_start(wpair[:, si, :, :].rearrange("p a b -> p (a b)"),
                                       scr_qkvo[mat][:]),
                           waits=[t.begin_write() for t in tr_wp[si]]
                                 + [t.reads() for t in tr_scr[mat]],
                           eng=BULKQ)
                for t in tr_wp[si]:
                    t.end_write(ld)

        xring = [0]

        def load_x_ld(tok0):
            xr = xring[0] % 2
            xring[0] += 1
            ld = P.dma(f"x{xr}",
                       lambda s, tok0=tok0, xr=xr:
                       s.dma_start(x_f32[:, xr, :], x_hbm[tok0:tok0 + 128, :]),
                       waits=[tr_x[xr].begin_write()], eng=XQ)
            tr_x[xr].end_write(ld)
            return (ld, xr)

        def load_x_cast(ent, tq):
            ld, xr = ent
            ct = P.op("scalar",
                      lambda e, xr=xr, tq=tq:
                      e.activation(x_bf[:, tq, :], x_f32[:, xr, :], AF.Copy),
                      waits=[ld, tr_xbf[tq].begin_write()])
            tr_x[xr].add_reader(ct)
            tr_xbf[tq].end_write(ct)
            return ct

        def load_x_tp(ct, tq):
            tp = P.dma(f"xt{tq % 2}",
                       lambda s, tq=tq:
                       s.dma_start(xT[:, :, tq * 128:(tq + 1) * 128],
                                   x_bf[:, tq, :], transpose=True),
                       waits=[ct, tr_xT[tq].begin_write()])
            tr_xbf[tq].add_reader(tp)
            tr_xT[tq].end_write(tp)

        def load_x_fin(ent, tq):
            load_x_tp(load_x_cast(ent, tq), tq)

        def dbg_dump(src_ap, n, waits=()):
            tk = P.dma("dbg", lambda sy, src_ap=src_ap, n=n:
                       sy.dma_start(dbg_hbm[:, 0:n], src_ap), waits=waits)
            P.sync_wait([tk])

        def phase1(s, first):
            NCH = SEQ // 128
            pend = {}

            def emit_kv(u):
                kr, kvr, t_k, t_v = pend.pop(u)
                cbk = ("c0", "c1") if kvr == 0 else ("c2", "c3")
                for h in range(H):
                    last = (h == H - 1)
                    p0 = (h % 2) * 64
                    tok = P.op(
                        "tensor",
                        lambda t, h=h, kr=kr, p0=p0, kvr=kvr:
                        t.matmul(kv_view[p0:p0 + 64, kvr, h // 2, 0:HD + 1],
                                 k_tm[:, kr, h * HD:(h + 1) * HD],
                                 v_tm[:, kr, h, :],
                                 start=True, stop=True,
                                 tile_position=(0, p0)),
                        waits=[t_k, t_v] + (bw(cbk) if h == 0 else []),
                        inc=last)
                bend(cbk, tok)
                tr_ktm[kr].add_reader(tok)
                tr_vtm[kr].add_reader(tok)
                if u == 0:
                    acc = P.op("vector",
                               lambda v, kvr=kvr:
                               v.tensor_copy(kv_acc[:], kv_view[:, kvr, :, 0:HD + 1]),
                               waits=[tok, tr_kvacc.begin_write()])
                else:
                    acc = P.op("vector",
                               lambda v, kvr=kvr:
                               v.tensor_tensor(kv_acc[:], kv_acc[:],
                                               kv_view[:, kvr, :, 0:HD + 1], ALU.add),
                               waits=[tok, tr_kvacc.reads()])
                bread(cbk, acc)
                tr_kvacc.end_write(acc)

            order = [(u + CPT) % NCH for u in range(NCH)]
            cts = {0: load_x_cast(load_x_ld(s * SEQ + order[0] * 128),
                                  order[0] % CPT)}
            load_x_tp(cts.pop(0), order[0] % CPT)
            if first:
                load_pair_direct("wk", "wv")
            for u in range(NCH):
                if u + 1 < NCH:
                    cts[u + 1] = load_x_cast(
                        load_x_ld(s * SEQ + order[u + 1] * 128),
                        order[u + 1] % CPT)
                tq = order[u] % CPT
                kr = u % 2
                kvr = u % 2
                if first:
                    drain_w_jobs(4)

                kps = ps_a if u % 2 == 0 else ps_b
                vps = ps_b if u % 2 == 0 else ps_a
                kbk = ("a0", "a1") if u % 2 == 0 else ("b0", "b1")
                vbk = ("b0", "b1") if u % 2 == 0 else ("a0", "a1")
                for j in range(NJ):
                    for hl in range(2):
                        last = (j == NJ - 1 and hl == 1)
                        tok = P.op(
                            "tensor",
                            lambda t, j=j, hl=hl, tq=tq, kps=kps:
                            t.matmul(kps[:, hl * 512:(hl + 1) * 512],
                                     xT[:, j, tq * 128:(tq + 1) * 128],
                                     wpair[:, 0, j, hl * 512:(hl + 1) * 512],
                                     start=(j == 0), stop=(j == NJ - 1)),
                            waits=[tr_xT[tq].reads(), tr_wp[0][j].reads()]
                                  + (bw(kbk) if (j == 0 and hl == 0) else []),
                            inc=last)
                bend(kbk, tok)
                for j in range(NJ):
                    for hl in range(2):
                        last = (j == NJ - 1 and hl == 1)
                        tok = P.op(
                            "tensor",
                            lambda t, j=j, hl=hl, tq=tq, vps=vps:
                            t.matmul(vps[:, hl * 512:(hl + 1) * 512],
                                     xT[:, j, tq * 128:(tq + 1) * 128],
                                     wpair[:, 1, j, hl * 512:(hl + 1) * 512],
                                     start=(j == 0), stop=(j == NJ - 1)),
                            waits=[tr_xT[tq].reads(), tr_wp[1][j].reads()]
                                  + (bw(vbk) if (j == 0 and hl == 0) else []),
                            inc=last)
                bend(vbk, tok)
                tr_xT[tq].add_reader(tok)
                for j in range(NJ):
                    tr_wp[0][j].add_reader(tok)
                    tr_wp[1][j].add_reader(tok)

                if u > 0:
                    emit_kv(u - 1)

                t_min = P.op("vector",
                             lambda v, kps=kps: v.tensor_scalar_min(t1[:], kps[:], 0.0),
                             waits=brd(kbk) + [tr_t1.begin_write()])
                bread(kbk, t_min)
                tr_t1.end_write(t_min)
                t_v = P.op("scalar",
                           lambda e, kr=kr, vps=vps:
                           e.activation(v_tm[:, kr, :, 0:HD],
                                        vps[:].rearrange("p (h e) -> p h e", h=H),
                                        AF.Copy),
                           waits=brd(vbk) + [tr_vtm[kr].begin_write()])
                bread(vbk, t_v)
                tr_vtm[kr].add_write(t_v)
                t_exp = P.op("scalar",
                             lambda e: e.activation(texp[:], t1[:], AF.Exp),
                             waits=[t_min, tr_texp.begin_write()])
                tr_t1.add_reader(t_exp)
                tr_texp.end_write(t_exp)
                t_k = P.op("vector",
                           lambda v, kr=kr, kps=kps:
                           v.scalar_tensor_tensor(k_tm[:, kr, :], kps[:], 0.0,
                                                  texp[:], ALU.max, ALU.add),
                           waits=[t_exp] + brd(kbk)
                                 + [tr_ktm[kr].begin_write()])
                bread(kbk, t_k)
                tr_texp.add_reader(t_k)
                tr_ktm[kr].end_write(t_k)
                pend[u] = (kr, kvr, t_k, t_v)
                if u + 1 < NCH:
                    load_x_tp(cts.pop(u + 1), order[u + 1] % CPT)

            emit_kv(NCH - 1)
            if first:
                drain_w_jobs(len(w_jobs))
                flush_w_stores()

            z0 = P.op("vector", lambda v: v.memset(kvbd[:], 0.0),
                      waits=[tr_kvbd.begin_write()])
            c0 = P.op("vector",
                      lambda v: v.tensor_copy(kvbd[0:64, :, 0:HD + 1],
                                              kv_acc[0:64, :, :]),
                      waits=[z0, tr_kvacc.reads()])
            c1 = P.op("vector",
                      lambda v: v.tensor_copy(kvbd[64:128, :, HD + 1:2 * (HD + 1)],
                                              kv_acc[64:128, :, :]),
                      waits=[z0, tr_kvacc.reads()])
            tr_kvacc.add_reader(c0)
            tr_kvacc.add_reader(c1)
            tr_kvbd.end_write(c0)
            tr_kvbd.add_write(c1)
            load_pair("wq", "wo")

        def ln_chunk(tc, rr, ln1_out, out_tok0=None, slot=None, lnslot=None):
            sl = tc if slot is None else slot
            lnslot = tc if lnslot is None else lnslot
            t_sq = P.op("scalar",
                        lambda e, rr=rr, sl=sl:
                        e.activation(texp[:], res[:, rr, :], AF.Square,
                                     accum_out=stats[:, sl, 1:2]),
                        waits=[tr_res[rr].reads(), tr_sqs.begin_write(),
                               tr_stats[sl].reads()])
            tr_res[rr].add_reader(t_sq)
            tr_sqs.end_write(t_sq)
            tr_stats[sl].add_write(t_sq)
            t_nm = P.op("vector",
                        lambda v, sl=sl:
                        v.tensor_scalar(stats[:, sl, 2:3], stats[:, sl, 0:1],
                                        -1.0 / D, None, ALU.mult),
                        waits=[tr_stats[sl].reads()])
            tr_stats[sl].add_write(t_nm)
            t_m2 = P.op("vector",
                        lambda v, sl=sl:
                        v.scalar_tensor_tensor(stats[:, sl, 3:4], stats[:, sl, 0:1],
                                               1.0 / (D * D), stats[:, sl, 0:1],
                                               ALU.mult, ALU.mult),
                        waits=[t_nm])
            tr_stats[sl].add_write(t_m2)
            t_var = P.op("vector",
                         lambda v, sl=sl:
                         v.scalar_tensor_tensor(stats[:, sl, 4:5], stats[:, sl, 1:2],
                                                1.0 / D, stats[:, sl, 3:4],
                                                ALU.mult, ALU.subtract),
                         waits=[t_m2, t_sq])
            tr_stats[sl].add_write(t_var)
            t_ln = P.op("scalar",
                        lambda e, sl=sl:
                        e.activation(stats[:, sl, 5:6], stats[:, sl, 4:5], AF.Ln,
                                     bias=consts[:, 0:1]),
                        waits=[t_var, tr_consts.reads()])
            tr_stats[sl].add_write(t_ln)
            t_rs = P.op("scalar",
                        lambda e, sl=sl:
                        e.activation(stats[:, sl, 6:7], stats[:, sl, 5:6], AF.Exp,
                                     scale=consts[:, 1:2]),
                        waits=[t_ln])
            tr_stats[sl].add_write(t_rs)
            t_b = P.op("scalar",
                       lambda e, sl=sl:
                       e.activation(stats[:, sl, 7:8], stats[:, sl, 2:3], AF.Copy,
                                    scale=stats[:, sl, 6:7]),
                       waits=[t_rs, t_nm])
            tr_stats[sl].add_write(t_b)
            if ln1_out:
                t_ap = P.op("scalar",
                            lambda e, rr=rr, sl=sl, lnslot=lnslot:
                            e.activation(ln1_bf[:, lnslot, :], res[:, rr, :],
                                         AF.Identity,
                                         scale=stats[:, sl, 6:7],
                                         bias=stats[:, sl, 7:8]),
                            waits=[t_b, tr_res[rr].reads(),
                                   tr_ln1[lnslot].begin_write()])
                tr_res[rr].add_reader(t_ap)
                tr_stats[sl].add_reader(t_ap)
                tr_ln1[lnslot].end_write(t_ap)
                t_lt = P.dma(f"lt{tc % 2}",
                             lambda s, tc=tc, lnslot=lnslot:
                             s.dma_start(ln1T[:, :, tc * 128:(tc + 1) * 128],
                                         ln1_bf[:, lnslot, :], transpose=True),
                             waits=[t_ap, tr_ln1T.begin_write() if tc == 0 else None])
                tr_ln1[lnslot].add_reader(t_lt)
                if tc == 0:
                    tr_ln1T.end_write(t_lt)
                else:
                    tr_ln1T.add_write(t_lt)
            else:
                orr = tc % 2
                t_ap = P.op("scalar",
                            lambda e, rr=rr, sl=sl, orr=orr:
                            e.activation(outb[:, orr, :], res[:, rr, :], AF.Identity,
                                         scale=stats[:, sl, 6:7],
                                         bias=stats[:, sl, 7:8]),
                            waits=[t_b, tr_res[rr].reads(),
                                   tr_outb[orr].begin_write()])
                tr_res[rr].add_reader(t_ap)
                tr_stats[sl].add_reader(t_ap)
                tr_outb[orr].end_write(t_ap)
                st = P.dma(f"out{orr}",
                           lambda s, orr=orr, out_tok0=out_tok0:
                           s.dma_start(out_hbm[out_tok0:out_tok0 + 128, :],
                                       outb[:, orr, :]),
                           waits=[t_ap], eng=XQ)
                tr_outb[orr].add_reader(st)

        down_toks = {}

        def down_half(sc, hl):
            pr = sc % 2
            bank = ("c0", "c1", "c2", "c3")[pr * 2 + hl]
            for hm in range(NM):
                last = (hm == NM - 1)
                tok = P.op(
                    "tensor",
                    lambda t, hm=hm, hl=hl, sc=sc, pr=pr:
                    t.matmul(d_view[:, pr, hl * 512:(hl + 1) * 512],
                             h_fm[:, hm, sc * 128:(sc + 1) * 128],
                             wd_buf[:, hm, hl * 512:(hl + 1) * 512],
                             start=(hm == 0), stop=(hm == NM - 1)),
                    waits=[tr_h[hm].reads(), tr_wd[hm].reads()]
                          + (bw((bank,)) if hm == 0 else []),
                    inc=last)
            bend((bank,), tok)
            for hm in range(NM):
                tr_h[hm].add_reader(tok)
                tr_wd[hm].add_reader(tok)
            down_toks[(sc, hl)] = tok

        def down_fin(s, g, sc, lnslot):
            base = s * SEQ + g * TILE
            pr = sc % 2
            rr = (sc + 1) % 2
            cbk = ("c0", "c1") if pr == 0 else ("c2", "c3")
            tk0 = down_toks.pop((sc, 0))
            tk1 = down_toks.pop((sc, 1))
            t_res = P.op(
                "vector",
                lambda v, pr=pr, rr=rr, sc=sc, lnslot=lnslot:
                v.scalar_tensor_tensor(res[:, rr, :], d_view[:, pr, :], INV127,
                                       ln1_bf[:, lnslot, :], ALU.mult, ALU.add,
                                       accum_out=stats[:, 4 + sc, 0:1]),
                waits=[tk0, tk1, tr_ln1[lnslot].reads(),
                       tr_res[rr].begin_write(), tr_stats[4 + sc].begin_write()])
            bread(cbk, t_res)
            tr_ln1[lnslot].add_reader(t_res)
            tr_res[rr].end_write(t_res)
            tr_stats[4 + sc].end_write(t_res)
            ln_chunk(sc, rr, ln1_out=False, out_tok0=base + sc * 128, slot=4 + sc)

        def phase2_a(s, g):
            base = s * SEQ + g * TILE
            e0 = load_x_ld(base)
            e1 = load_x_ld(base + 128)
            load_x_fin(e0, 0)
            e2 = load_x_ld(base + 256)
            load_x_fin(e1, 1)
            e3 = load_x_ld(base + 384)
            load_x_fin(e2, 2)
            load_x_fin(e3, 3)

        def phase2(s, g, stop2=None, skip_a=False, pend=None, slot3=3,
                   next_x=None, pre_pend=False, fill_jobs=None):
            base = s * SEQ + g * TILE
            if not skip_a:
                phase2_a(s, g)
            nxt = {}

            def px(step):
                if next_x is None:
                    return
                if step == 0:
                    nxt[0] = load_x_ld(next_x)
                    nxt[1] = load_x_ld(next_x + 128)
                elif step == 1:
                    load_x_fin(nxt[0], 0)
                elif step == 2:
                    nxt[2] = load_x_ld(next_x + 256)
                elif step == 3:
                    load_x_fin(nxt[1], 1)
                elif step == 4:
                    nxt[3] = load_x_ld(next_x + 384)
                elif step == 5:
                    load_x_fin(nxt[2], 2)
                    load_x_fin(nxt[3], 3)

            if pre_pend and pend is not None:
                down_half(0, 0)
                down_half(0, 1)
            q_toks = {}
            for i in range(NJ):
                abk = "a0" if i % 2 == 0 else "a1"
                for j in range(NJ):
                    last = (j == NJ - 1)
                    tok = P.op(
                        "tensor",
                        lambda t, i=i, j=j:
                        t.matmul(ps_a[:, (i % 2) * 512:(i % 2 + 1) * 512],
                                 wpair[:, 0, j, i * 128:(i + 1) * 128],
                                 xT[:, j, :], start=(j == 0), stop=(j == NJ - 1)),
                        waits=[tt.reads() for tt in tr_xT] + [tr_wp[0][j].reads()]
                              + (bw((abk,)) if j == 0 else []),
                        inc=last)
                bend((abk,), tok)
                for j in range(NJ):
                    tr_wp[0][j].add_reader(tok)
                for tt in tr_xT:
                    tt.add_reader(tok)
                t_min = P.op("vector",
                             lambda v, i=i:
                             v.tensor_scalar_min(t1[:, 0:TILE],
                                                 ps_a[:, (i % 2) * 512:(i % 2 + 1) * 512],
                                                 0.0),
                             waits=[tok, tr_t1.begin_write()])
                bread((abk,), t_min)
                tr_t1.end_write(t_min)
                t_exp = P.op("scalar",
                             lambda e: e.activation(texp[:, 0:TILE], t1[:, 0:TILE],
                                                    AF.Exp),
                             waits=[t_min, tr_texp.begin_write()])
                tr_t1.add_reader(t_exp)
                tr_texp.end_write(t_exp)
                t_q = P.op("vector",
                           lambda v, i=i:
                           v.scalar_tensor_tensor(q_fm[:, i, :],
                                                  ps_a[:, (i % 2) * 512:(i % 2 + 1) * 512],
                                                  0.0, texp[:, 0:TILE], ALU.max, ALU.add),
                           waits=[t_exp, tok,
                                  tr_q.begin_write() if i == 0 else None])
                bread((abk,), t_q)
                tr_texp.add_reader(t_q)
                q_toks[i] = t_q
                if i == 0:
                    tr_q.end_write(t_q)
                else:
                    tr_q.add_write(t_q)

            def emit_c1h(tc, half):
                ar = tc % 2
                for p in range(4):
                    c = half * 4 + p
                    last = (p == 3)
                    tok = P.op(
                        "tensor",
                        lambda t, c=c, p=p, tc=tc:
                        t.matmul(outa_view[:, p, 0:2 * (HD + 1)],
                                 q_fm[:, c, tc * 128:(tc + 1) * 128],
                                 kvbd[:, c, :],
                                 start=True, stop=True),
                        waits=[q_toks[c], tr_kvbd.reads()]
                              + (bw(("a0",)) if p == 0 else
                                 (bw(("a1",)) if p == 2 else [])),
                        inc=last)
                bend(("a0", "a1"), tok)
                tr_q.add_reader(tok)
                tr_kvbd.add_reader(tok)
                if stop2 == "P2o" and tc == 0 and half == 1:
                    cp = P.op("vector",
                              lambda v: v.tensor_copy(
                                  h_fm[:, 0:2, :].rearrange("p a b -> p (a b)"),
                                  ps_a[:]),
                              waits=[tok])
                    dbg_dump(h_fm[:, 0:2, :].rearrange("p a b -> p (a b)"), D,
                             waits=[cp])
                    return True
                pair = outa_view[:, :, 0:2 * (HD + 1)].rearrange(
                    "p c (q e) -> p c q e", q=2)
                hs = slice(half * 4, half * 4 + 4)
                t_z = P.op("vector",
                           lambda v, pair=pair, tc=tc, hs=hs:
                           v.tensor_scalar_add(zr[:, tc, 0, hs, :],
                                               pair[:, :, :, HD], EPS_ATTN),
                           waits=brd(("a0", "a1"))
                                 + [tr_zr[tc].begin_write() if half == 0 else None])
                bread(("a0", "a1"), t_z)
                if half == 0:
                    tr_zr[tc].end_write(t_z)
                else:
                    tr_zr[tc].add_write(t_z)
                t_r = P.op("vector",
                           lambda v, tc=tc, hs=hs:
                           v.reciprocal(zr[:, tc, 1, hs, :], zr[:, tc, 0, hs, :]),
                           waits=[t_z])
                tr_zr[tc].add_write(t_r)
                t_dv = P.op(
                    "vector",
                    lambda v, ar=ar, pair=pair, tc=tc, hs=hs, half=half:
                    v.tensor_tensor(attn_bf[:, ar, half * 512:(half + 1) * 512]
                                    .rearrange("p (c q e) -> p c q e", c=4, q=2),
                                    pair[:, :, :, 0:HD],
                                    zr[:, tc, 1, hs, :].unsqueeze(3).broadcast_to(
                                        [128, 4, 2, HD]),
                                    ALU.mult),
                    waits=[t_r] + brd(("a0", "a1"))
                          + [tr_attn[ar].begin_write() if half == 0 else None])
                bread(("a0", "a1"), t_dv)
                tr_zr[tc].add_reader(t_dv)
                if half == 0:
                    tr_attn[ar].end_write(t_dv)
                    return False
                tr_attn[ar].add_write(t_dv)
                if stop2 == "P2d" and tc == 0:
                    dbg_dump(attn_bf[:, ar, :], D, waits=[t_dv])
                    return True
                t_at = P.dma(f"at{ar}",
                             lambda s, ar=ar, tc=tc:
                             s.dma_start(attnT[:, tc, :, :], attn_bf[:, ar, :],
                                         transpose=True),
                             waits=[t_dv, tr_attnT[tc].begin_write()])
                tr_attn[ar].add_reader(t_at)
                tr_attnT[tc].end_write(t_at)

                return False

            def emit_c2(tc):
                lnsl = tc if tc < CPT - 1 else slot3
                for e_ in range(NJ):
                    for hl in range(2):
                        last = (e_ == NJ - 1 and hl == 1)
                        tok = P.op(
                            "tensor",
                            lambda t, e_=e_, hl=hl, tc=tc:
                            t.matmul(ps_b[:, hl * 512:(hl + 1) * 512],
                                     attnT[:, tc, e_, :],
                                     wpair[:, 1, e_, hl * 512:(hl + 1) * 512],
                                     start=(e_ == 0), stop=(e_ == NJ - 1)),
                            waits=[tr_attnT[tc].reads(), tr_wp[1][e_].reads()]
                                  + (bw(("b0", "b1")) if (e_ == 0 and hl == 0) else []),
                            inc=last)
                bend(("b0", "b1"), tok)
                tr_attnT[tc].add_reader(tok)
                for j in range(NJ):
                    tr_wp[1][j].add_reader(tok)
                if stop2 == "P2w" and tc == 0:
                    cp = P.op("vector",
                              lambda v: v.tensor_copy(
                                  h_fm[:, 0:2, :].rearrange("p a b -> p (a b)"),
                                  ps_b[:]),
                              waits=[tok])
                    dbg_dump(h_fm[:, 0:2, :].rearrange("p a b -> p (a b)"), D,
                             waits=[cp])
                    return True
                rr = tc % 2
                t_res = P.op(
                    "vector",
                    lambda v, rr=rr, tc=tc:
                    v.scalar_tensor_tensor(res[:, rr, :], ps_b[:], 1.0, x_bf[:, tc, :],
                                           ALU.mult, ALU.add,
                                           accum_out=stats[:, tc, 0:1]),
                    waits=brd(("b0", "b1")) + [tr_xbf[tc].reads(),
                                               tr_res[rr].begin_write(),
                                               tr_stats[tc].begin_write()])
                bread(("b0", "b1"), t_res)
                tr_xbf[tc].add_reader(t_res)
                tr_res[rr].end_write(t_res)
                tr_stats[tc].end_write(t_res)
                ln_chunk(tc, rr, ln1_out=True, lnslot=lnsl)
                return False

            def pend_half(sc, hl):
                if pend is not None:
                    down_half(sc, hl)
                elif fill_jobs:
                    fill_jobs.pop(0)()

            def pend_fin(sc):
                if pend is not None:
                    ps_, pg_, psl3 = pend
                    down_fin(ps_, pg_, sc, sc if sc < CPT - 1 else psl3)

            if emit_c1h(0, 0):
                return True
            if emit_c1h(0, 1):
                return True
            for tc in range(1, CPT):
                px((tc - 1) * 2)
                if pre_pend:
                    if tc < CPT - 1:
                        pend_fin(tc - 1)
                        pend_half(tc, 0)
                else:
                    pend_half(tc - 1, 0)
                if emit_c1h(tc, 0):
                    return True
                if pre_pend:
                    if tc < CPT - 1:
                        pend_half(tc, 1)
                else:
                    pend_half(tc - 1, 1)
                if emit_c1h(tc, 1):
                    return True
                if not pre_pend or tc == CPT - 1:
                    pend_fin(tc - 1)
                if emit_c2(tc - 1):
                    return True
                if tc < CPT - 1:
                    px(tc * 2 - 1)
            if emit_c2(CPT - 1):
                return True
            px(5)
            pend_half(CPT - 1, 0)
            pend_half(CPT - 1, 1)
            pend_fin(CPT - 1)
            return False

        slab_state = {"lg": [None] * NM, "lu": [None] * NM, "st": [],
                      "first": True, "pre4": False}

        def load_slab(m, first_tile=False, both_gp=False, defer_vec=None):
            r = m % 4
            lg_toks = slab_state["lg"]
            lu_toks = slab_state["lu"]
            w3_stores = slab_state["st"]
            if first_tile:
                for gi, (mat, img) in enumerate((("wg", wi_imgs["wgi"]),
                                                 ("wu", wi_imgs["wui"]))):
                    sl = (2 * m + gi) % 2
                    ld = P.dma(f"w3i{sl}",
                               lambda sy, img=img, m=m, sl=sl:
                               sy.dma_start(stg_i32[sl], img[m]),
                               waits=[tr_stgi[sl].begin_write(),
                                      tr_outb[sl].begin_write()],
                               eng=BULKQ)
                    tr_stgi[sl].end_write(ld)
                    tr_outb[sl].end_write(ld)

                    def mk_ct(gi=gi, r=r, sl=sl, ld=ld, m=m, mat=mat,
                              both_gp=both_gp):
                        eng = "gpsimd" if (both_gp or gi == 1) else "vector"
                        ct = P.op(eng,
                                  lambda e, gi=gi, r=r, sl=sl:
                                  e.tensor_copy(wgu[:, gi, r, :], stg_i32[sl]),
                                  waits=[ld, tr_wgu[gi][r].begin_write()]
                                        + ([tr_stgi[2 + gi].begin_write()]
                                           if m < 2 else []))
                        tr_stgi[sl].add_reader(ct)
                        tr_outb[sl].add_reader(ct)
                        tr_wgu[gi][r].end_write(ct)
                        if gi == 0:
                            slab_state["lg"][m] = ct
                        else:
                            slab_state["lu"][m] = ct

                        def mk_st(mat=mat, m=m, gi=gi, r=r, ct=ct):
                            st = P.dma(f"scw{(2 * m + gi) % 2}",
                                       lambda sy, mat=mat, m=m, gi=gi, r=r:
                                       sy.dma_start(scr[mat][m], wgu[:, gi, r, :]),
                                       waits=[ct], eng=BULKQ)
                            tr_wgu[gi][r].add_reader(st)
                            tr_scr[mat][m].add_write(st)
                        slab_state["st"].append(mk_st)
                        while len(slab_state["st"]) > 2:
                            slab_state["st"].pop(0)()
                    if gi == 0 and defer_vec is not None:
                        defer_vec.append(mk_ct)
                    else:
                        mk_ct()
                return
            lg = P.dma(f"wg{r % 2}",
                       lambda sy, m=m, r=r:
                       sy.dma_start(wgu[:, 0, r, :], scr["wg"][m]),
                       waits=[tr_wgu[0][r].begin_write(), tr_scr["wg"][m].reads()]
                             + ([tr_stgi[2].begin_write()] if m < 2 else []),
                       eng=BULKQ)
            tr_wgu[0][r].end_write(lg)
            lg_toks[m] = lg
            lu = P.dma(f"wu{r % 2}",
                       lambda sy, m=m, r=r:
                       sy.dma_start(wgu[:, 1, r, :], scr["wu"][m]),
                       waits=[tr_wgu[1][r].begin_write(), tr_scr["wu"][m].reads()]
                             + ([tr_stgi[3].begin_write()] if m < 2 else []),
                       eng=BULKQ)
            tr_wgu[1][r].end_write(lu)
            lu_toks[m] = lu

        def phase3(s, g, next_a=None, next_kv_pair=False, preload_next=False):
            base = s * SEQ + g * TILE
            first_tile = slab_state["first"]
            slab_state["first"] = False
            if next_kv_pair:
                load_pair("wk", "wv")
            lg_toks = slab_state["lg"]
            lu_toks = slab_state["lu"]
            w3_stores = slab_state["st"]

            if first_tile:
                if not slab_state["pre4"]:
                    for m in range(4):
                        load_slab(m, True)
            elif not PRELOAD_SLABS:
                for m in range(4):
                    load_slab(m)
            vc = slab_state.setdefault("vc", [])
            for m in range(NM):
                if vc:
                    vc.pop(0)()
                r = m % 4
                abk = "a0" if m % 2 == 0 else "a1"
                bbk = "b0" if m % 2 == 0 else "b1"
                lg = lg_toks[m]
                lu = lu_toks[m]
                if next_a is not None and m == 12:
                    next_a()
                    next_a = None

                for j in range(NJ):
                    last = (j == NJ - 1)
                    tokg = P.op(
                        "tensor",
                        lambda t, j=j, r=r, m=m:
                        t.matmul(ps_a[:, (m % 2) * 512:(m % 2 + 1) * 512],
                                 wgu[:, 0, r, j * 128:(j + 1) * 128],
                                 ln1T[:, j, :], start=(j == 0), stop=(j == NJ - 1)),
                        waits=[lg, tr_ln1T.reads()] + (bw((abk,)) if j == 0 else []),
                        inc=last)
                bend((abk,), tokg)
                tr_wgu[0][r].add_reader(tokg)
                tr_ln1T.add_reader(tokg)
                for j in range(NJ):
                    last = (j == NJ - 1)
                    toku = P.op(
                        "tensor",
                        lambda t, j=j, r=r, m=m:
                        t.matmul(ps_b[:, (m % 2) * 512:(m % 2 + 1) * 512],
                                 wgu[:, 1, r, j * 128:(j + 1) * 128],
                                 ln1T[:, j, :], start=(j == 0), stop=(j == NJ - 1)),
                        waits=[lu, tr_ln1T.reads()] + (bw((bbk,)) if j == 0 else []),
                        inc=last)
                bend((bbk,), toku)
                tr_wgu[1][r].add_reader(toku)
                tr_ln1T.add_reader(toku)
                if m + 4 < NM:
                    load_slab(m + 4, first_tile,
                              defer_vec=(vc if first_tile else None))
                elif preload_next and m == NM - 1:
                    while w3_stores:
                        w3_stores.pop(0)()
                    for mm in range(4):
                        load_slab(mm)

                t_si = P.op("scalar",
                            lambda e, m=m:
                            e.activation(silu_b[:, m % 2, :],
                                         ps_a[:, (m % 2) * 512:(m % 2 + 1) * 512],
                                         AF.Silu, scale=consts[:, 2:3]),
                            waits=[tokg, tr_silu[m % 2].begin_write(), tr_consts.reads()])
                bread((abk,), t_si)
                tr_silu[m % 2].end_write(t_si)
                t_h = P.op("vector",
                           lambda v, m=m:
                           v.scalar_tensor_tensor(h_fm[:, m, :],
                                                  ps_b[:, (m % 2) * 512:(m % 2 + 1) * 512],
                                                  INV127, silu_b[:, m % 2, :],
                                                  ALU.mult, ALU.mult),
                           waits=[t_si, toku, tr_h[m].begin_write()]
                                 + ([tr_stgb[m // 2].begin_write()] if m < 8 else []))
                bread((bbk,), t_h)
                tr_silu[m % 2].add_reader(t_h)
                tr_h[m].end_write(t_h)
                if m == NM - 1 and PRELOAD:
                    P.op("scalar", lambda e: e.preload_act_table(AF.Ln), waits=[t_si])

            while vc:
                vc.pop(0)()
            while w3_stores:
                w3_stores.pop(0)()

        pend_tile = [None]
        for s in range(n_seq):
            phase1(s, first=(s == 0))
            if stop_after == "P1":
                dbg_dump(kvbd[:].rearrange("p a b -> p (a b)"), NJ * 2 * (HD + 1),
                         waits=[tr_kvbd.reads()])
                P.flush_block()
                return nc
            prefetched = [True]
            for g in range(n_tiles):
                tidx = s * n_tiles + g
                sl3 = CPT - 1 + (tidx % 2)
                nx = (s * SEQ + (g + 1) * TILE) if g + 1 < n_tiles else None
                fj = None
                if tidx == 0:
                    fj = [lambda m=m: load_slab(m, True, both_gp=True)
                          for m in range(4)]
                    slab_state["pre4"] = True
                r2 = phase2(s, g, stop2=stop_after, skip_a=prefetched[0],
                            pend=pend_tile[0], slot3=sl3, next_x=nx,
                            pre_pend=(g == 0), fill_jobs=fj)
                pend_tile[0] = (s, g, sl3)
                prefetched[0] = False
                if r2:
                    P.flush_block()
                    return nc
                if stop_after == "P2":
                    dbg_dump(ln1_bf[:, 0:CPT, :].rearrange("p a b -> p (a b)"),
                             CPT * D, waits=[t.reads() for t in tr_ln1])
                    P.flush_block()
                    return nc
                if g + 1 < n_tiles:
                    prefetched[0] = True
                    phase3(s, g, preload_next=PRELOAD_SLABS)
                else:
                    phase3(s, g, next_kv_pair=(s + 1 < n_seq),
                           preload_next=(PRELOAD_SLABS and s + 1 < n_seq))
        fs, fg, fsl3 = pend_tile[0]
        for sc in range(CPT):
            down_half(sc, 0)
            down_half(sc, 1)
            down_fin(fs, fg, sc, sc if sc < CPT - 1 else fsl3)

        P.sync_wait([(P.dma_sems["out0"], P.dma_count["out0"]),
                     (P.dma_sems["out1"], P.dma_count["out1"])])
        P.flush_block()

    return nc


def _w_img(w_t):
    return np.ascontiguousarray(
        w_t.reshape(NJ, 128, D).transpose(1, 0, 2).reshape(128, NJ * D))


def kernel(L0_attn_x, L0_attn_Wq, L0_attn_Wk, L0_attn_Wv, L0_attn_Wo,
           L0_ffn_w_gate, L0_ffn_w_up, L0_ffn_w_down):
    X = np.ascontiguousarray(np.asarray(L0_attn_x))
    Wq = np.asarray(L0_attn_Wq)
    Wk = np.asarray(L0_attn_Wk)
    Wv = np.asarray(L0_attn_Wv)
    Wo = np.asarray(L0_attn_Wo)
    wg = np.asarray(L0_ffn_w_gate)
    wu = np.asarray(L0_ffn_w_up)
    wd = np.asarray(L0_ffn_w_down)

    imgs = {
        "wq": _w_img(np.ascontiguousarray(Wq.T)),
        "wk": _w_img(np.ascontiguousarray(Wk.T)),
        "wv": _w_img(np.ascontiguousarray(Wv.T)),
        "wo": _w_img(np.ascontiguousarray(Wo.T)),
        "wgi": np.ascontiguousarray(
            wg.T.reshape(NJ, 128, NM, 128).transpose(2, 1, 0, 3).reshape(NM, 128, D)),
        "wui": np.ascontiguousarray(
            wu.T.reshape(NJ, 128, NM, 128).transpose(2, 1, 0, 3).reshape(NM, 128, D)),
        "wdi": np.ascontiguousarray(wd.T.reshape(NM, 128, D)),
    }

    nc = build_program()
    in_maps = []
    for c in range(NCORES):
        im = {"x": np.ascontiguousarray(
            X[NSEQ * c:NSEQ * (c + 1)].reshape(TOKENS, D))}
        im.update(imgs)
        in_maps.append(im)

    resu = run_bass_kernel_spmd(nc, in_maps, core_ids=list(range(NCORES)))
    out = np.empty((16, SEQ, D), dtype=np.float32)
    for c in range(NCORES):
        out[NSEQ * c:NSEQ * (c + 1)] = np.asarray(
            resu.results[c]["out"]).reshape(NSEQ, SEQ, D)
    return out


# version 32
# speedup vs baseline: 1.0648x; 1.0381x over previous
import contextlib
import numpy as np

import concourse.bass as bass
import concourse.mybir as mybir
from concourse.bass_utils import run_bass_kernel_spmd

F32 = mybir.dt.float32
BF16 = mybir.dt.bfloat16
I32 = mybir.dt.int32
AF = mybir.ActivationFunctionType
ALU = mybir.AluOpType

NCORES = 8
SEQ = 2048
NSEQ = 2
D = 1024
NJ = 8
H = 16
HD = 64
FF = 3072
NM = 24
TOKENS = NSEQ * SEQ
TILE = 512
NTILE = TOKENS // TILE
CPT = TILE // 128
INV127 = 1.0 / 127.0
EPS_ATTN = 1e-6
EPS_LN = 1e-5
PRELOAD = False
WKEEP = 3
PRELOAD_SLABS = True
BULKQ = "sync"
WQ = "sync"
XQ = "sync"


def _merge(d, tok):
    if tok is None:
        return
    sem, v = tok
    if d.get(sem, 0) < v:
        d[sem] = v


class Track:
    def __init__(self):
        self.ready = {}
        self.readers = {}

    def begin_write(self):
        w = dict(self.readers)
        for s, v in self.ready.items():
            if w.get(s, 0) < v:
                w[s] = v
        return w

    def end_write(self, tok):
        self.ready = {}
        self.readers = {}
        _merge(self.ready, tok)

    def add_write(self, tok):
        _merge(self.ready, tok)

    def reads(self):
        return dict(self.ready)

    def add_reader(self, tok):
        _merge(self.readers, tok)


class Prog:
    ENGS = ("sync", "tensor", "vector", "scalar", "gpsimd")

    def __init__(self, nc, stack):
        self.nc = nc
        self.sems = {}
        self.count = {}
        for e in ("tensor", "vector", "scalar", "gpsimd"):
            self.sems[e] = stack.enter_context(nc.semaphore(f"sem_{e}"))
            self.count[e] = 0
        self.dma_sems = {}
        self.dma_count = {}
        self.stack = stack
        self._reset_block()

    def _reset_block(self):
        self.ops = {e: [] for e in self.ENGS}
        self.waited = {e: {} for e in self.ENGS}

    def dma_slot(self, name):
        if name not in self.dma_sems:
            self.dma_sems[name] = self.stack.enter_context(
                self.nc.semaphore(f"dsl_{name}"))
            self.dma_count[name] = 0
        return name

    def _gather(self, waits):
        wd = {}
        for w in waits:
            if w is None:
                continue
            if isinstance(w, dict):
                for s, v in w.items():
                    _merge(wd, (s, v))
            else:
                _merge(wd, w)
        return wd

    def _emit_waits(self, eng_name, wd):
        out = []
        seen = self.waited[eng_name]
        for sem, val in sorted(wd.items(), key=lambda kv: kv[0].num):
            if seen.get(sem, 0) < val:
                seen[sem] = val
                out.append((sem, val))
        return out

    def op(self, eng_name, fn, waits=(), inc=True):
        real_waits = self._emit_waits(eng_name, self._gather(waits))
        tok = None
        sem = self.sems[eng_name]
        if inc:
            self.count[eng_name] += 1
            tok = (sem, self.count[eng_name])

        def closure(eng, fn=fn, real_waits=real_waits, tok=tok, sem=sem):
            for s, v in real_waits:
                eng.wait_ge(s, v)
            inst = fn(eng)
            if tok is not None:
                inst.then_inc(sem, 1)

        self.ops[eng_name].append(closure)
        return tok

    def dma(self, slot, fn, waits=(), eng="sync"):
        slot = self.dma_slot(slot)
        wd = self._gather(waits)
        sem = self.dma_sems[slot]
        prev = self.dma_count[slot]
        if prev:
            _merge(wd, (sem, prev))
        real_waits = self._emit_waits(eng, wd)
        self.dma_count[slot] = prev + 16
        tok = (sem, self.dma_count[slot])

        def closure(e, fn=fn, real_waits=real_waits, sem=sem):
            for s, v in real_waits:
                e.wait_ge(s, v)
            fn(e).then_inc(sem, 16)

        self.ops[eng].append(closure)
        return tok

    def sync_wait(self, waits):
        real_waits = self._emit_waits("sync", self._gather(waits))

        def closure(eng, real_waits=real_waits):
            for s, v in real_waits:
                eng.wait_ge(s, v)

        self.ops["sync"].append(closure)

    def flush_block(self):
        nc = self.nc
        ops = self.ops
        with nc.Block() as blk:
            if ops["sync"]:
                blk.sync(lambda e, fns=ops["sync"]: [f(e) for f in fns])
            if ops["tensor"]:
                blk.tensor(lambda e, fns=ops["tensor"]: [f(e) for f in fns])
            if ops["vector"]:
                blk.vector(lambda e, fns=ops["vector"]: [f(e) for f in fns])
            if ops["scalar"]:
                blk.scalar(lambda e, fns=ops["scalar"]: [f(e) for f in fns])
            if ops["gpsimd"]:
                blk.gpsimd(lambda e, fns=ops["gpsimd"]: [f(e) for f in fns])
        self._reset_block()


def build_program(n_seq=NSEQ, n_tiles=NTILE // NSEQ, stop_after=None):
    nc = bass.Bass("TRN2", target_bir_lowering=False,
                   detect_race_conditions=False, dynamic_dma_scratch_size=2048)

    x_hbm = nc.dram_tensor("x", [TOKENS, D], F32, kind="ExternalInput")
    w_imgs = {n: nc.dram_tensor(n, [128, NJ * D], F32, kind="ExternalInput")
              for n in ("wq", "wk", "wv", "wo")}
    wi_imgs = {n: nc.dram_tensor(n, [NM, 128, D], I32, kind="ExternalInput")
               for n in ("wgi", "wui", "wdi")}
    out_hbm = nc.dram_tensor("out", [TOKENS, D], F32, kind="ExternalOutput")
    scr = {n: nc.dram_tensor(f"scr_{n}", [NM, 128, D], BF16)
           for n in ("wg", "wu", "wd")}
    scr_qkvo = {n: nc.dram_tensor(f"scr_{n}", [128, NJ * D], BF16)
                for n in ("wq", "wk", "wv", "wo")}
    dbg_hbm = None
    if stop_after is not None:
        dbg_hbm = nc.dram_tensor("dbg", [128, 8192], BF16, kind="ExternalOutput")

    with contextlib.ExitStack() as ctx:
        E = ctx.enter_context
        wpair = E(nc.sbuf_tensor("wpair", [128, 2, NJ, D], BF16))
        wd_buf = E(nc.sbuf_tensor("wd_buf", [128, NM, D], BF16))
        wgu = E(nc.sbuf_tensor("wgu", [128, 2, 4, D], BF16))
        h_fm = E(nc.sbuf_tensor("h_fm", [128, NM, TILE], BF16))
        ln1_bf = E(nc.sbuf_tensor("ln1_bf", [128, CPT + 1, D], BF16))
        ln1T = E(nc.sbuf_tensor("ln1T", [128, NJ, TILE], BF16))
        x_f32 = E(nc.sbuf_tensor("x_f32", [128, 2, D], F32))
        x_bf = E(nc.sbuf_tensor("x_bf", [128, CPT, D], BF16))
        xT = E(nc.sbuf_tensor("xT", [128, NJ, TILE], BF16))
        q_fm = E(nc.sbuf_tensor("q_fm", [128, NJ, TILE], BF16))
        k_tm = E(nc.sbuf_tensor("k_tm", [128, 2, D], BF16))
        v_tm = E(nc.sbuf_tensor("v_tm", [128, 2, H, HD + 1], BF16))
        kv_acc = E(nc.sbuf_tensor("kv_acc", [128, NJ, HD + 1], F32))
        kvbd = E(nc.sbuf_tensor("kvbd", [128, NJ, 2 * (HD + 1)], BF16))
        attn_bf = E(nc.sbuf_tensor("attn_bf", [128, 2, D], BF16))
        attnT = E(nc.sbuf_tensor("attnT", [128, CPT, NJ, 128], BF16))
        t1 = E(nc.sbuf_tensor("t1", [128, D], BF16))
        texp = E(nc.sbuf_tensor("texp", [128, D], BF16))
        res = E(nc.sbuf_tensor("res", [128, 2, D], F32))
        stats = E(nc.sbuf_tensor("stats", [128, 2 * CPT, 8], F32))
        zr = E(nc.sbuf_tensor("zr", [128, CPT, 2, NJ, 2], F32))
        outb = E(nc.sbuf_tensor("outb", [128, 2, D], F32))
        silu_b = E(nc.sbuf_tensor("silu_b", [128, 2, TILE], BF16))
        consts = E(nc.sbuf_tensor("consts", [128, 4], F32))
        ps_a = E(nc.psum_tensor("ps_a", [128, 1024], F32))
        ps_b = E(nc.psum_tensor("ps_b", [128, 1024], F32))
        ps_c = E(nc.psum_tensor("ps_c", [128, 2048], F32))

        P = Prog(nc, ctx)

        BK = {k: Track() for k in ("a0", "a1", "b0", "b1", "c0", "c1", "c2", "c3")}

        def bw(keys):
            return [BK[k].begin_write() for k in keys]

        def brd(keys):
            return [BK[k].reads() for k in keys]

        def bend(keys, tok):
            for k in keys:
                BK[k].end_write(tok)

        def bread(keys, tok):
            for k in keys:
                BK[k].add_reader(tok)

        kv_view = ps_c[:].rearrange("p (r c e) -> p r c e", r=2, c=NJ)
        out_view = ps_c[:].rearrange("p (c e) -> p c e", c=NJ)
        outa_view = ps_a[:].rearrange("p (c e) -> p c e", c=CPT)
        d_view = ps_c[:].rearrange("p (r e) -> p r e", r=2)

        tr_x = [Track() for _ in range(2)]
        tr_xbf = [Track() for _ in range(CPT)]
        tr_xT = [Track() for _ in range(CPT)]
        tr_wp = [[Track() for _ in range(NJ)] for _ in range(2)]
        tr_ktm = [Track() for _ in range(2)]
        tr_vtm = [Track() for _ in range(2)]
        tr_kvacc = Track()
        tr_kvbd = Track()
        tr_q = Track()
        tr_attn = [Track() for _ in range(2)]
        tr_attnT = [Track() for _ in range(CPT)]
        tr_t1 = Track()
        tr_texp = Track()
        tr_sqs = tr_texp
        tr_res = [Track() for _ in range(2)]
        tr_stats = [Track() for _ in range(2 * CPT)]
        tr_zr = [Track() for _ in range(CPT)]
        tr_outb = [Track() for _ in range(2)]
        tr_ln1 = [Track() for _ in range(CPT + 1)]
        tr_ln1T = Track()
        tr_h = [Track() for _ in range(NM)]
        tr_wgu = [[Track() for _ in range(4)] for _ in range(2)]
        tr_wd = [Track() for _ in range(NM)]
        tr_silu = [Track() for _ in range(2)]
        tr_stgi = [Track() for _ in range(6)]
        tr_stgb = [Track() for _ in range(4)]
        tr_scr = {n: [Track() for _ in range(NM)] for n in ("wg", "wu")}
        tr_scr.update({n: [Track() for _ in range(NJ)]
                       for n in ("wq", "wk", "wv", "wo")})
        tr_consts = Track()

        tok = P.op("vector", lambda v: v.memset(consts[:, 0:1], EPS_LN))
        tr_consts.add_write(tok)
        tok = P.op("vector", lambda v: v.memset(consts[:, 1:2], -0.5))
        tr_consts.add_write(tok)
        tok = P.op("vector", lambda v: v.memset(consts[:, 2:3], INV127))
        tr_consts.add_write(tok)
        for r in range(2):
            tok = P.op("vector", lambda v, r=r: v.memset(v_tm[:, r, :, HD:HD + 1], 1.0))
            tr_vtm[r].add_write(tok)

        stg_i32 = [outb[:, 0, :].bitcast(I32), outb[:, 1, :].bitcast(I32),
                   wgu[:, 0, :, :].rearrange("p a b -> p (a b)").bitcast(I32)[:, 0:D],
                   wgu[:, 1, :, :].rearrange("p a b -> p (a b)").bitcast(I32)[:, 0:D],
                   attnT[:, 0:2, :, :].rearrange("p a b c -> p (a b c)").bitcast(I32),
                   attnT[:, 2:4, :, :].rearrange("p a b c -> p (a b c)").bitcast(I32)]
        stg_bf = [h_fm[:, 2 * i:2 * i + 2, :].rearrange("p a b -> p (a b)")
                  for i in range(4)]

        w_jobs = []
        for mat in ("wq", "wo"):
            for j in range(NJ):
                w_jobs.append(("f32", mat, w_imgs[mat], j))
        for m in range(NM):
            w_jobs.append(("wdd", "wd", wi_imgs["wdi"], m))
        for mat, si in (("wk", 0), ("wv", 1)):
            w_jobs.append(("pair_st", mat, si, 0))
        w_pos = [0]
        w_stores = []

        def flush_w_stores(keep=0):
            while len(w_stores) > keep:
                w_stores.pop(0)()

        def drain_w_jobs(k):
            for _ in range(k):
                if w_pos[0] >= len(w_jobs):
                    flush_w_stores()
                    return
                kind, mat, src_t, m = w_jobs[w_pos[0]]
                r = w_pos[0] % 4
                w_pos[0] += 1
                if kind == "pair_st":
                    si = src_t
                    st = P.dma(f"pst{si}",
                               lambda s, mat=mat, si=si:
                               s.dma_start(scr_qkvo[mat][:],
                                           wpair[:, si, :, :].rearrange(
                                               "p a b -> p (a b)")),
                               waits=[t.reads() for t in tr_wp[si]],
                               eng=BULKQ)
                    for t in tr_wp[si]:
                        t.add_reader(st)
                    for t in tr_scr[mat]:
                        t.add_write(st)
                    continue
                if kind == "int" or kind == "wdd":
                    extra = [tr_outb[r].begin_write()] if r < 2 else []
                    ld = P.dma(f"wi{r}",
                               lambda s, src_t=src_t, m=m, r=r:
                               s.dma_start(stg_i32[r], src_t[m]),
                               waits=[tr_stgi[r].begin_write()] + extra,
                               eng=WQ)
                    tr_stgi[r].end_write(ld)
                    if r < 2:
                        tr_outb[r].end_write(ld)
                    if kind == "wdd":
                        ct = P.op("vector" if r % 2 == 0 else "gpsimd",
                                  lambda e, r=r, m=m:
                                  e.tensor_copy(wd_buf[:, m, :], stg_i32[r]),
                                  waits=[ld, tr_wd[m].begin_write()])
                        tr_stgi[r].add_reader(ct)
                        if r < 2:
                            tr_outb[r].add_reader(ct)
                        tr_wd[m].end_write(ct)
                        continue
                    ct = P.op("vector",
                              lambda e, r=r: e.tensor_copy(stg_bf[r], stg_i32[r]),
                              waits=[ld, tr_stgb[r].begin_write()])
                    tr_stgi[r].add_reader(ct)
                    if r < 2:
                        tr_outb[r].add_reader(ct)
                    tr_stgb[r].end_write(ct)
                    def mk_store(mat=mat, m=m, r=r, ct=ct):
                        st = P.dma(f"scw{r}",
                                   lambda s, mat=mat, m=m, r=r:
                                   s.dma_start(scr[mat][m], stg_bf[r]),
                                   waits=[ct], eng=BULKQ)
                        tr_stgb[r].add_reader(st)
                        tr_scr[mat][m].add_write(st)
                    w_stores.append(mk_store)
                    flush_w_stores(keep=WKEEP)
                else:
                    rr2 = w_pos[0] % 2
                    ld = P.dma(f"wf{rr2}",
                               lambda s, src_t=src_t, m=m, rr2=rr2:
                               s.dma_start(res[:, rr2, :], src_t[:, m * D:(m + 1) * D]),
                               waits=[tr_res[rr2].begin_write()], eng=WQ)
                    tr_res[rr2].end_write(ld)
                    ct = P.op("scalar",
                              lambda e, r=r, rr2=rr2:
                              e.activation(stg_bf[r], res[:, rr2, :], AF.Copy),
                              waits=[ld, tr_stgb[r].begin_write()])
                    tr_res[rr2].add_reader(ct)
                    tr_stgb[r].end_write(ct)
                    def mk_store2(mat=mat, m=m, r=r, ct=ct):
                        st = P.dma(f"scw{r}",
                                   lambda s, mat=mat, m=m, r=r:
                                   s.dma_start(scr_qkvo[mat][:, m * D:(m + 1) * D],
                                               stg_bf[r]),
                                   waits=[ct], eng=BULKQ)
                        tr_stgb[r].add_reader(st)
                        tr_scr[mat][m].add_write(st)
                    w_stores.append(mk_store2)
                    flush_w_stores(keep=WKEEP)

        def load_pair_direct(m0, m1):
            stg4 = [x_f32[:, 0, :], x_f32[:, 1, :], outb[:, 0, :], outb[:, 1, :]]
            trs4 = [tr_x[0], tr_x[1], tr_outb[0], tr_outb[1]]
            for j in range(NJ):
                for si, mat in enumerate((m0, m1)):
                    idx = 2 * j + si
                    r = idx % 4
                    fsrc = w_imgs[mat]
                    ld = P.dma(f"wp{r}",
                               lambda s, fsrc=fsrc, j=j, r=r:
                               s.dma_start(stg4[r], fsrc[:, j * D:(j + 1) * D]),
                               waits=[trs4[r].begin_write()], eng=BULKQ)
                    trs4[r].end_write(ld)
                    ct = P.op("scalar" if idx % 2 == 0 else "vector",
                              lambda e, si=si, j=j, r=r:
                              (e.activation(wpair[:, si, j, :], stg4[r], AF.Copy)
                               if hasattr(e, "activation") else
                               e.tensor_copy(wpair[:, si, j, :], stg4[r])),
                              waits=[ld, tr_wp[si][j].begin_write()])
                    trs4[r].add_reader(ct)
                    tr_wp[si][j].end_write(ct)

        def load_pair(m0, m1):
            for si, mat in enumerate((m0, m1)):
                ld = P.dma(f"wp{si}",
                           lambda s, si=si, mat=mat:
                           s.dma_start(wpair[:, si, :, :].rearrange("p a b -> p (a b)"),
                                       scr_qkvo[mat][:]),
                           waits=[t.begin_write() for t in tr_wp[si]]
                                 + [t.reads() for t in tr_scr[mat]],
                           eng=BULKQ)
                for t in tr_wp[si]:
                    t.end_write(ld)

        xring = [0]

        def load_x_ld(tok0):
            xr = xring[0] % 2
            xring[0] += 1
            ld = P.dma(f"x{xr}",
                       lambda s, tok0=tok0, xr=xr:
                       s.dma_start(x_f32[:, xr, :], x_hbm[tok0:tok0 + 128, :]),
                       waits=[tr_x[xr].begin_write()], eng=XQ)
            tr_x[xr].end_write(ld)
            return (ld, xr)

        def load_x_cast(ent, tq):
            ld, xr = ent
            ct = P.op("scalar",
                      lambda e, xr=xr, tq=tq:
                      e.activation(x_bf[:, tq, :], x_f32[:, xr, :], AF.Copy),
                      waits=[ld, tr_xbf[tq].begin_write()])
            tr_x[xr].add_reader(ct)
            tr_xbf[tq].end_write(ct)
            return ct

        def load_x_tp(ct, tq):
            tp = P.dma(f"xt{tq % 2}",
                       lambda s, tq=tq:
                       s.dma_start(xT[:, :, tq * 128:(tq + 1) * 128],
                                   x_bf[:, tq, :], transpose=True),
                       waits=[ct, tr_xT[tq].begin_write()])
            tr_xbf[tq].add_reader(tp)
            tr_xT[tq].end_write(tp)

        def load_x_fin(ent, tq):
            load_x_tp(load_x_cast(ent, tq), tq)

        def dbg_dump(src_ap, n, waits=()):
            tk = P.dma("dbg", lambda sy, src_ap=src_ap, n=n:
                       sy.dma_start(dbg_hbm[:, 0:n], src_ap), waits=waits)
            P.sync_wait([tk])

        def phase1(s, first, pre_x=False):
            NCH = SEQ // 128
            pend = {}

            def emit_kv(u):
                kr, kvr, t_k, t_v = pend.pop(u)
                cbk = ("c0", "c1") if kvr == 0 else ("c2", "c3")
                for h in range(H):
                    last = (h == H - 1)
                    p0 = (h % 2) * 64
                    tok = P.op(
                        "tensor",
                        lambda t, h=h, kr=kr, p0=p0, kvr=kvr:
                        t.matmul(kv_view[p0:p0 + 64, kvr, h // 2, 0:HD + 1],
                                 k_tm[:, kr, h * HD:(h + 1) * HD],
                                 v_tm[:, kr, h, :],
                                 start=True, stop=True,
                                 tile_position=(0, p0)),
                        waits=[t_k, t_v] + (bw(cbk) if h == 0 else []),
                        inc=last)
                bend(cbk, tok)
                tr_ktm[kr].add_reader(tok)
                tr_vtm[kr].add_reader(tok)
                if u == 0:
                    acc = P.op("vector",
                               lambda v, kvr=kvr:
                               v.tensor_copy(kv_acc[:], kv_view[:, kvr, :, 0:HD + 1]),
                               waits=[tok, tr_kvacc.begin_write()])
                else:
                    acc = P.op("vector",
                               lambda v, kvr=kvr:
                               v.tensor_tensor(kv_acc[:], kv_acc[:],
                                               kv_view[:, kvr, :, 0:HD + 1], ALU.add),
                               waits=[tok, tr_kvacc.reads()])
                bread(cbk, acc)
                tr_kvacc.end_write(acc)

            order = [(u + CPT) % NCH for u in range(NCH)]
            start_i = 4 if pre_x else 0
            cts = {}
            lds = {}

            def p1_ld(i):
                if start_i <= i < NCH:
                    lds[i] = load_x_ld(s * SEQ + order[i] * 128)

            def p1_cast(i):
                if i in lds:
                    cts[i] = load_x_cast(lds.pop(i), order[i] % CPT)

            def p1_tp(i):
                if i in cts:
                    load_x_tp(cts.pop(i), order[i] % CPT)

            p1_ld(0)
            p1_ld(1)
            p1_cast(0)
            p1_tp(0)
            if first:
                p1_cast(1)
                load_pair_direct("wk", "wv")
            for u in range(NCH):
                p1_cast(u + 1)
                tq = order[u] % CPT
                kr = u % 2
                kvr = u % 2
                if first:
                    drain_w_jobs(4)

                kps = ps_a if u % 2 == 0 else ps_b
                vps = ps_b if u % 2 == 0 else ps_a
                kbk = ("a0", "a1") if u % 2 == 0 else ("b0", "b1")
                vbk = ("b0", "b1") if u % 2 == 0 else ("a0", "a1")
                for j in range(NJ):
                    for hl in range(2):
                        last = (j == NJ - 1 and hl == 1)
                        tok = P.op(
                            "tensor",
                            lambda t, j=j, hl=hl, tq=tq, kps=kps:
                            t.matmul(kps[:, hl * 512:(hl + 1) * 512],
                                     xT[:, j, tq * 128:(tq + 1) * 128],
                                     wpair[:, 0, j, hl * 512:(hl + 1) * 512],
                                     start=(j == 0), stop=(j == NJ - 1)),
                            waits=[tr_xT[tq].reads(), tr_wp[0][j].reads()]
                                  + (bw(kbk) if (j == 0 and hl == 0) else []),
                            inc=last)
                bend(kbk, tok)
                p1_ld(u + 2)
                p1_tp(u + 1)
                for j in range(NJ):
                    for hl in range(2):
                        last = (j == NJ - 1 and hl == 1)
                        tok = P.op(
                            "tensor",
                            lambda t, j=j, hl=hl, tq=tq, vps=vps:
                            t.matmul(vps[:, hl * 512:(hl + 1) * 512],
                                     xT[:, j, tq * 128:(tq + 1) * 128],
                                     wpair[:, 1, j, hl * 512:(hl + 1) * 512],
                                     start=(j == 0), stop=(j == NJ - 1)),
                            waits=[tr_xT[tq].reads(), tr_wp[1][j].reads()]
                                  + (bw(vbk) if (j == 0 and hl == 0) else []),
                            inc=last)
                bend(vbk, tok)
                tr_xT[tq].add_reader(tok)
                for j in range(NJ):
                    tr_wp[0][j].add_reader(tok)
                    tr_wp[1][j].add_reader(tok)

                if u > 0:
                    emit_kv(u - 1)

                t_min = P.op("vector",
                             lambda v, kps=kps: v.tensor_scalar_min(t1[:], kps[:], 0.0),
                             waits=brd(kbk) + [tr_t1.begin_write()])
                bread(kbk, t_min)
                tr_t1.end_write(t_min)
                t_v = P.op("scalar",
                           lambda e, kr=kr, vps=vps:
                           e.activation(v_tm[:, kr, :, 0:HD],
                                        vps[:].rearrange("p (h e) -> p h e", h=H),
                                        AF.Copy),
                           waits=brd(vbk) + [tr_vtm[kr].begin_write()])
                bread(vbk, t_v)
                tr_vtm[kr].add_write(t_v)
                t_exp = P.op("scalar",
                             lambda e: e.activation(texp[:], t1[:], AF.Exp),
                             waits=[t_min, tr_texp.begin_write()])
                tr_t1.add_reader(t_exp)
                tr_texp.end_write(t_exp)
                t_k = P.op("vector",
                           lambda v, kr=kr, kps=kps:
                           v.scalar_tensor_tensor(k_tm[:, kr, :], kps[:], 0.0,
                                                  texp[:], ALU.max, ALU.add),
                           waits=[t_exp] + brd(kbk)
                                 + [tr_ktm[kr].begin_write()])
                bread(kbk, t_k)
                tr_texp.add_reader(t_k)
                tr_ktm[kr].end_write(t_k)
                pend[u] = (kr, kvr, t_k, t_v)

            emit_kv(NCH - 1)
            if first:
                drain_w_jobs(len(w_jobs))
                flush_w_stores()

            z0 = P.op("vector", lambda v: v.memset(kvbd[:], 0.0),
                      waits=[tr_kvbd.begin_write()])
            c0 = P.op("vector",
                      lambda v: v.tensor_copy(kvbd[0:64, :, 0:HD + 1],
                                              kv_acc[0:64, :, :]),
                      waits=[z0, tr_kvacc.reads()])
            c1 = P.op("vector",
                      lambda v: v.tensor_copy(kvbd[64:128, :, HD + 1:2 * (HD + 1)],
                                              kv_acc[64:128, :, :]),
                      waits=[z0, tr_kvacc.reads()])
            tr_kvacc.add_reader(c0)
            tr_kvacc.add_reader(c1)
            tr_kvbd.end_write(c0)
            tr_kvbd.add_write(c1)
            load_pair("wq", "wo")

        def ln_chunk(tc, rr, ln1_out, out_tok0=None, slot=None, lnslot=None):
            sl = tc if slot is None else slot
            lnslot = tc if lnslot is None else lnslot
            t_sq = P.op("scalar",
                        lambda e, rr=rr, sl=sl:
                        e.activation(texp[:], res[:, rr, :], AF.Square,
                                     accum_out=stats[:, sl, 1:2]),
                        waits=[tr_res[rr].reads(), tr_sqs.begin_write(),
                               tr_stats[sl].reads()])
            tr_res[rr].add_reader(t_sq)
            tr_sqs.end_write(t_sq)
            tr_stats[sl].add_write(t_sq)
            t_nm = P.op("vector",
                        lambda v, sl=sl:
                        v.tensor_scalar(stats[:, sl, 2:3], stats[:, sl, 0:1],
                                        -1.0 / D, None, ALU.mult),
                        waits=[tr_stats[sl].reads()])
            tr_stats[sl].add_write(t_nm)
            t_m2 = P.op("vector",
                        lambda v, sl=sl:
                        v.scalar_tensor_tensor(stats[:, sl, 3:4], stats[:, sl, 0:1],
                                               1.0 / (D * D), stats[:, sl, 0:1],
                                               ALU.mult, ALU.mult),
                        waits=[t_nm])
            tr_stats[sl].add_write(t_m2)
            t_var = P.op("vector",
                         lambda v, sl=sl:
                         v.scalar_tensor_tensor(stats[:, sl, 4:5], stats[:, sl, 1:2],
                                                1.0 / D, stats[:, sl, 3:4],
                                                ALU.mult, ALU.subtract),
                         waits=[t_m2, t_sq])
            tr_stats[sl].add_write(t_var)
            t_ln = P.op("scalar",
                        lambda e, sl=sl:
                        e.activation(stats[:, sl, 5:6], stats[:, sl, 4:5], AF.Ln,
                                     bias=consts[:, 0:1]),
                        waits=[t_var, tr_consts.reads()])
            tr_stats[sl].add_write(t_ln)
            t_rs = P.op("scalar",
                        lambda e, sl=sl:
                        e.activation(stats[:, sl, 6:7], stats[:, sl, 5:6], AF.Exp,
                                     scale=consts[:, 1:2]),
                        waits=[t_ln])
            tr_stats[sl].add_write(t_rs)
            t_b = P.op("scalar",
                       lambda e, sl=sl:
                       e.activation(stats[:, sl, 7:8], stats[:, sl, 2:3], AF.Copy,
                                    scale=stats[:, sl, 6:7]),
                       waits=[t_rs, t_nm])
            tr_stats[sl].add_write(t_b)
            if ln1_out:
                t_ap = P.op("scalar",
                            lambda e, rr=rr, sl=sl, lnslot=lnslot:
                            e.activation(ln1_bf[:, lnslot, :], res[:, rr, :],
                                         AF.Identity,
                                         scale=stats[:, sl, 6:7],
                                         bias=stats[:, sl, 7:8]),
                            waits=[t_b, tr_res[rr].reads(),
                                   tr_ln1[lnslot].begin_write()])
                tr_res[rr].add_reader(t_ap)
                tr_stats[sl].add_reader(t_ap)
                tr_ln1[lnslot].end_write(t_ap)
                t_lt = P.dma(f"lt{tc % 2}",
                             lambda s, tc=tc, lnslot=lnslot:
                             s.dma_start(ln1T[:, :, tc * 128:(tc + 1) * 128],
                                         ln1_bf[:, lnslot, :], transpose=True),
                             waits=[t_ap, tr_ln1T.begin_write() if tc == 0 else None])
                tr_ln1[lnslot].add_reader(t_lt)
                if tc == 0:
                    tr_ln1T.end_write(t_lt)
                else:
                    tr_ln1T.add_write(t_lt)
            else:
                orr = tc % 2
                t_ap = P.op("scalar",
                            lambda e, rr=rr, sl=sl, orr=orr:
                            e.activation(outb[:, orr, :], res[:, rr, :], AF.Identity,
                                         scale=stats[:, sl, 6:7],
                                         bias=stats[:, sl, 7:8]),
                            waits=[t_b, tr_res[rr].reads(),
                                   tr_outb[orr].begin_write()])
                tr_res[rr].add_reader(t_ap)
                tr_stats[sl].add_reader(t_ap)
                tr_outb[orr].end_write(t_ap)
                st = P.dma(f"out{orr}",
                           lambda s, orr=orr, out_tok0=out_tok0:
                           s.dma_start(out_hbm[out_tok0:out_tok0 + 128, :],
                                       outb[:, orr, :]),
                           waits=[t_ap], eng=XQ)
                tr_outb[orr].add_reader(st)

        down_toks = {}

        def down_half(sc, hl):
            pr = sc % 2
            bank = ("c0", "c1", "c2", "c3")[pr * 2 + hl]
            for hm in range(NM):
                last = (hm == NM - 1)
                tok = P.op(
                    "tensor",
                    lambda t, hm=hm, hl=hl, sc=sc, pr=pr:
                    t.matmul(d_view[:, pr, hl * 512:(hl + 1) * 512],
                             h_fm[:, hm, sc * 128:(sc + 1) * 128],
                             wd_buf[:, hm, hl * 512:(hl + 1) * 512],
                             start=(hm == 0), stop=(hm == NM - 1)),
                    waits=[tr_h[hm].reads(), tr_wd[hm].reads()]
                          + (bw((bank,)) if hm == 0 else []),
                    inc=last)
            bend((bank,), tok)
            for hm in range(NM):
                tr_h[hm].add_reader(tok)
                tr_wd[hm].add_reader(tok)
            down_toks[(sc, hl)] = tok

        def down_fin(s, g, sc, lnslot):
            base = s * SEQ + g * TILE
            pr = sc % 2
            rr = (sc + 1) % 2
            cbk = ("c0", "c1") if pr == 0 else ("c2", "c3")
            tk0 = down_toks.pop((sc, 0))
            tk1 = down_toks.pop((sc, 1))
            t_res = P.op(
                "vector",
                lambda v, pr=pr, rr=rr, sc=sc, lnslot=lnslot:
                v.scalar_tensor_tensor(res[:, rr, :], d_view[:, pr, :], INV127,
                                       ln1_bf[:, lnslot, :], ALU.mult, ALU.add,
                                       accum_out=stats[:, 4 + sc, 0:1]),
                waits=[tk0, tk1, tr_ln1[lnslot].reads(),
                       tr_res[rr].begin_write(), tr_stats[4 + sc].begin_write()])
            bread(cbk, t_res)
            tr_ln1[lnslot].add_reader(t_res)
            tr_res[rr].end_write(t_res)
            tr_stats[4 + sc].end_write(t_res)
            ln_chunk(sc, rr, ln1_out=False, out_tok0=base + sc * 128, slot=4 + sc)

        def phase2_a(s, g):
            base = s * SEQ + g * TILE
            e0 = load_x_ld(base)
            e1 = load_x_ld(base + 128)
            load_x_fin(e0, 0)
            e2 = load_x_ld(base + 256)
            load_x_fin(e1, 1)
            e3 = load_x_ld(base + 384)
            load_x_fin(e2, 2)
            load_x_fin(e3, 3)

        def phase2(s, g, stop2=None, skip_a=False, pend=None, slot3=3,
                   next_x=None, pre_pend=False, fill_jobs=None):
            base = s * SEQ + g * TILE
            if not skip_a:
                phase2_a(s, g)
            nxt = {}

            def px(step):
                if next_x is None:
                    return
                if step == 0:
                    nxt[0] = load_x_ld(next_x)
                    nxt[1] = load_x_ld(next_x + 128)
                elif step == 1:
                    load_x_fin(nxt[0], 0)
                elif step == 2:
                    nxt[2] = load_x_ld(next_x + 256)
                elif step == 3:
                    load_x_fin(nxt[1], 1)
                elif step == 4:
                    nxt[3] = load_x_ld(next_x + 384)
                elif step == 5:
                    load_x_fin(nxt[2], 2)
                    load_x_fin(nxt[3], 3)

            if pre_pend and pend is not None:
                down_half(0, 0)
                down_half(0, 1)
            q_toks = {}
            for i in range(NJ):
                abk = "a0" if i % 2 == 0 else "a1"
                for j in range(NJ):
                    last = (j == NJ - 1)
                    tok = P.op(
                        "tensor",
                        lambda t, i=i, j=j:
                        t.matmul(ps_a[:, (i % 2) * 512:(i % 2 + 1) * 512],
                                 wpair[:, 0, j, i * 128:(i + 1) * 128],
                                 xT[:, j, :], start=(j == 0), stop=(j == NJ - 1)),
                        waits=[tt.reads() for tt in tr_xT] + [tr_wp[0][j].reads()]
                              + (bw((abk,)) if j == 0 else []),
                        inc=last)
                bend((abk,), tok)
                for j in range(NJ):
                    tr_wp[0][j].add_reader(tok)
                for tt in tr_xT:
                    tt.add_reader(tok)
                t_min = P.op("vector",
                             lambda v, i=i:
                             v.tensor_scalar_min(t1[:, 0:TILE],
                                                 ps_a[:, (i % 2) * 512:(i % 2 + 1) * 512],
                                                 0.0),
                             waits=[tok, tr_t1.begin_write()])
                bread((abk,), t_min)
                tr_t1.end_write(t_min)
                t_exp = P.op("scalar",
                             lambda e: e.activation(texp[:, 0:TILE], t1[:, 0:TILE],
                                                    AF.Exp),
                             waits=[t_min, tr_texp.begin_write()])
                tr_t1.add_reader(t_exp)
                tr_texp.end_write(t_exp)
                t_q = P.op("vector",
                           lambda v, i=i:
                           v.scalar_tensor_tensor(q_fm[:, i, :],
                                                  ps_a[:, (i % 2) * 512:(i % 2 + 1) * 512],
                                                  0.0, texp[:, 0:TILE], ALU.max, ALU.add),
                           waits=[t_exp, tok,
                                  tr_q.begin_write() if i == 0 else None])
                bread((abk,), t_q)
                tr_texp.add_reader(t_q)
                q_toks[i] = t_q
                if i == 0:
                    tr_q.end_write(t_q)
                else:
                    tr_q.add_write(t_q)

            def emit_c1h(tc, half):
                ar = tc % 2
                for p in range(4):
                    c = half * 4 + p
                    last = (p == 3)
                    tok = P.op(
                        "tensor",
                        lambda t, c=c, p=p, tc=tc:
                        t.matmul(outa_view[:, p, 0:2 * (HD + 1)],
                                 q_fm[:, c, tc * 128:(tc + 1) * 128],
                                 kvbd[:, c, :],
                                 start=True, stop=True),
                        waits=[q_toks[c], tr_kvbd.reads()]
                              + (bw(("a0",)) if p == 0 else
                                 (bw(("a1",)) if p == 2 else [])),
                        inc=last)
                bend(("a0", "a1"), tok)
                tr_q.add_reader(tok)
                tr_kvbd.add_reader(tok)
                if stop2 == "P2o" and tc == 0 and half == 1:
                    cp = P.op("vector",
                              lambda v: v.tensor_copy(
                                  h_fm[:, 0:2, :].rearrange("p a b -> p (a b)"),
                                  ps_a[:]),
                              waits=[tok])
                    dbg_dump(h_fm[:, 0:2, :].rearrange("p a b -> p (a b)"), D,
                             waits=[cp])
                    return True
                pair = outa_view[:, :, 0:2 * (HD + 1)].rearrange(
                    "p c (q e) -> p c q e", q=2)
                hs = slice(half * 4, half * 4 + 4)
                t_z = P.op("vector",
                           lambda v, pair=pair, tc=tc, hs=hs:
                           v.tensor_scalar_add(zr[:, tc, 0, hs, :],
                                               pair[:, :, :, HD], EPS_ATTN),
                           waits=brd(("a0", "a1"))
                                 + [tr_zr[tc].begin_write() if half == 0 else None])
                bread(("a0", "a1"), t_z)
                if half == 0:
                    tr_zr[tc].end_write(t_z)
                else:
                    tr_zr[tc].add_write(t_z)
                t_r = P.op("vector",
                           lambda v, tc=tc, hs=hs:
                           v.reciprocal(zr[:, tc, 1, hs, :], zr[:, tc, 0, hs, :]),
                           waits=[t_z])
                tr_zr[tc].add_write(t_r)
                t_dv = P.op(
                    "vector",
                    lambda v, ar=ar, pair=pair, tc=tc, hs=hs, half=half:
                    v.tensor_tensor(attn_bf[:, ar, half * 512:(half + 1) * 512]
                                    .rearrange("p (c q e) -> p c q e", c=4, q=2),
                                    pair[:, :, :, 0:HD],
                                    zr[:, tc, 1, hs, :].unsqueeze(3).broadcast_to(
                                        [128, 4, 2, HD]),
                                    ALU.mult),
                    waits=[t_r] + brd(("a0", "a1"))
                          + [tr_attn[ar].begin_write() if half == 0 else None])
                bread(("a0", "a1"), t_dv)
                tr_zr[tc].add_reader(t_dv)
                if half == 0:
                    tr_attn[ar].end_write(t_dv)
                    return False
                tr_attn[ar].add_write(t_dv)
                if stop2 == "P2d" and tc == 0:
                    dbg_dump(attn_bf[:, ar, :], D, waits=[t_dv])
                    return True
                t_at = P.dma(f"at{ar}",
                             lambda s, ar=ar, tc=tc:
                             s.dma_start(attnT[:, tc, :, :], attn_bf[:, ar, :],
                                         transpose=True),
                             waits=[t_dv, tr_attnT[tc].begin_write()])
                tr_attn[ar].add_reader(t_at)
                tr_attnT[tc].end_write(t_at)

                return False

            def emit_c2(tc):
                lnsl = tc if tc < CPT - 1 else slot3
                for e_ in range(NJ):
                    for hl in range(2):
                        last = (e_ == NJ - 1 and hl == 1)
                        tok = P.op(
                            "tensor",
                            lambda t, e_=e_, hl=hl, tc=tc:
                            t.matmul(ps_b[:, hl * 512:(hl + 1) * 512],
                                     attnT[:, tc, e_, :],
                                     wpair[:, 1, e_, hl * 512:(hl + 1) * 512],
                                     start=(e_ == 0), stop=(e_ == NJ - 1)),
                            waits=[tr_attnT[tc].reads(), tr_wp[1][e_].reads()]
                                  + (bw(("b0", "b1")) if (e_ == 0 and hl == 0) else []),
                            inc=last)
                bend(("b0", "b1"), tok)
                tr_attnT[tc].add_reader(tok)
                for j in range(NJ):
                    tr_wp[1][j].add_reader(tok)
                if stop2 == "P2w" and tc == 0:
                    cp = P.op("vector",
                              lambda v: v.tensor_copy(
                                  h_fm[:, 0:2, :].rearrange("p a b -> p (a b)"),
                                  ps_b[:]),
                              waits=[tok])
                    dbg_dump(h_fm[:, 0:2, :].rearrange("p a b -> p (a b)"), D,
                             waits=[cp])
                    return True
                rr = tc % 2
                t_res = P.op(
                    "vector",
                    lambda v, rr=rr, tc=tc:
                    v.scalar_tensor_tensor(res[:, rr, :], ps_b[:], 1.0, x_bf[:, tc, :],
                                           ALU.mult, ALU.add,
                                           accum_out=stats[:, tc, 0:1]),
                    waits=brd(("b0", "b1")) + [tr_xbf[tc].reads(),
                                               tr_res[rr].begin_write(),
                                               tr_stats[tc].begin_write()])
                bread(("b0", "b1"), t_res)
                tr_xbf[tc].add_reader(t_res)
                tr_res[rr].end_write(t_res)
                tr_stats[tc].end_write(t_res)
                ln_chunk(tc, rr, ln1_out=True, lnslot=lnsl)
                return False

            def pend_half(sc, hl):
                if pend is not None:
                    down_half(sc, hl)
                elif fill_jobs:
                    fill_jobs.pop(0)()

            def pend_fin(sc):
                if pend is not None:
                    ps_, pg_, psl3 = pend
                    down_fin(ps_, pg_, sc, sc if sc < CPT - 1 else psl3)

            if emit_c1h(0, 0):
                return True
            if emit_c1h(0, 1):
                return True
            for tc in range(1, CPT):
                px((tc - 1) * 2)
                if pre_pend:
                    if tc < CPT - 1:
                        pend_fin(tc - 1)
                        pend_half(tc, 0)
                else:
                    pend_half(tc - 1, 0)
                if emit_c1h(tc, 0):
                    return True
                if pre_pend:
                    if tc < CPT - 1:
                        pend_half(tc, 1)
                else:
                    pend_half(tc - 1, 1)
                if emit_c1h(tc, 1):
                    return True
                if not pre_pend or tc == CPT - 1:
                    pend_fin(tc - 1)
                if emit_c2(tc - 1):
                    return True
                if tc < CPT - 1:
                    px(tc * 2 - 1)
            if emit_c2(CPT - 1):
                return True
            px(5)
            pend_half(CPT - 1, 0)
            pend_half(CPT - 1, 1)
            pend_fin(CPT - 1)
            return False

        slab_state = {"lg": [None] * NM, "lu": [None] * NM, "st": [],
                      "first": True, "pre4": False}

        def load_slab(m, first_tile=False, both_gp=False, defer_vec=None):
            r = m % 4
            lg_toks = slab_state["lg"]
            lu_toks = slab_state["lu"]
            w3_stores = slab_state["st"]
            if first_tile:
                for gi, (mat, img) in enumerate((("wg", wi_imgs["wgi"]),
                                                 ("wu", wi_imgs["wui"]))):
                    sl = gi + (4 if (m >= 4 and m % 2 == 1) else 0)
                    if sl >= 4:
                        alias = [tr_attnT[(sl - 4) * 2], tr_attnT[(sl - 4) * 2 + 1]]
                    else:
                        alias = [tr_outb[sl]]
                    ld = P.dma(f"w3i{sl}",
                               lambda sy, img=img, m=m, sl=sl:
                               sy.dma_start(stg_i32[sl], img[m]),
                               waits=[tr_stgi[sl].begin_write()]
                                     + [t.begin_write() for t in alias],
                               eng=BULKQ)
                    tr_stgi[sl].end_write(ld)
                    for t in alias:
                        t.end_write(ld)

                    def mk_ct(gi=gi, r=r, sl=sl, ld=ld, m=m, mat=mat,
                              both_gp=both_gp, alias=alias):
                        eng = "gpsimd" if (both_gp or gi == 1) else "vector"
                        ct = P.op(eng,
                                  lambda e, gi=gi, r=r, sl=sl:
                                  e.tensor_copy(wgu[:, gi, r, :], stg_i32[sl]),
                                  waits=[ld, tr_wgu[gi][r].begin_write()]
                                        + ([tr_stgi[2 + gi].begin_write()]
                                           if m < 2 else []))
                        tr_stgi[sl].add_reader(ct)
                        for t in alias:
                            t.add_reader(ct)
                        tr_wgu[gi][r].end_write(ct)
                        if gi == 0:
                            slab_state["lg"][m] = ct
                        else:
                            slab_state["lu"][m] = ct

                        def mk_st(mat=mat, m=m, gi=gi, r=r, ct=ct):
                            st = P.dma(f"scw{(2 * m + gi) % 2}",
                                       lambda sy, mat=mat, m=m, gi=gi, r=r:
                                       sy.dma_start(scr[mat][m], wgu[:, gi, r, :]),
                                       waits=[ct], eng=BULKQ)
                            tr_wgu[gi][r].add_reader(st)
                            tr_scr[mat][m].add_write(st)
                        slab_state["st"].append(mk_st)
                        while len(slab_state["st"]) > 2:
                            slab_state["st"].pop(0)()
                    if gi == 0 and defer_vec is not None:
                        defer_vec.append(mk_ct)
                    else:
                        mk_ct()
                return
            lg = P.dma(f"wg{r % 2}",
                       lambda sy, m=m, r=r:
                       sy.dma_start(wgu[:, 0, r, :], scr["wg"][m]),
                       waits=[tr_wgu[0][r].begin_write(), tr_scr["wg"][m].reads()]
                             + ([tr_stgi[2].begin_write()] if m < 2 else []),
                       eng=BULKQ)
            tr_wgu[0][r].end_write(lg)
            lg_toks[m] = lg
            lu = P.dma(f"wu{r % 2}",
                       lambda sy, m=m, r=r:
                       sy.dma_start(wgu[:, 1, r, :], scr["wu"][m]),
                       waits=[tr_wgu[1][r].begin_write(), tr_scr["wu"][m].reads()]
                             + ([tr_stgi[3].begin_write()] if m < 2 else []),
                       eng=BULKQ)
            tr_wgu[1][r].end_write(lu)
            lu_toks[m] = lu

        def phase3(s, g, next_a=None, next_kv_pair=False, preload_next=False):
            base = s * SEQ + g * TILE
            first_tile = slab_state["first"]
            slab_state["first"] = False
            if next_kv_pair:
                load_pair("wk", "wv")
            lg_toks = slab_state["lg"]
            lu_toks = slab_state["lu"]
            w3_stores = slab_state["st"]

            if first_tile:
                if not slab_state["pre4"]:
                    for m in range(4):
                        load_slab(m, True)
            elif not PRELOAD_SLABS:
                for m in range(4):
                    load_slab(m)
            vc = slab_state.setdefault("vc", [])
            for m in range(NM):
                if vc:
                    vc.pop(0)()
                r = m % 4
                abk = "a0" if m % 2 == 0 else "a1"
                bbk = "b0" if m % 2 == 0 else "b1"
                lg = lg_toks[m]
                lu = lu_toks[m]
                if next_a is not None and m == 12:
                    next_a()
                    next_a = None

                for j in range(NJ):
                    last = (j == NJ - 1)
                    tokg = P.op(
                        "tensor",
                        lambda t, j=j, r=r, m=m:
                        t.matmul(ps_a[:, (m % 2) * 512:(m % 2 + 1) * 512],
                                 wgu[:, 0, r, j * 128:(j + 1) * 128],
                                 ln1T[:, j, :], start=(j == 0), stop=(j == NJ - 1)),
                        waits=[lg, tr_ln1T.reads()] + (bw((abk,)) if j == 0 else []),
                        inc=last)
                bend((abk,), tokg)
                tr_wgu[0][r].add_reader(tokg)
                tr_ln1T.add_reader(tokg)
                for j in range(NJ):
                    last = (j == NJ - 1)
                    toku = P.op(
                        "tensor",
                        lambda t, j=j, r=r, m=m:
                        t.matmul(ps_b[:, (m % 2) * 512:(m % 2 + 1) * 512],
                                 wgu[:, 1, r, j * 128:(j + 1) * 128],
                                 ln1T[:, j, :], start=(j == 0), stop=(j == NJ - 1)),
                        waits=[lu, tr_ln1T.reads()] + (bw((bbk,)) if j == 0 else []),
                        inc=last)
                bend((bbk,), toku)
                tr_wgu[1][r].add_reader(toku)
                tr_ln1T.add_reader(toku)
                if m + 4 < NM:
                    load_slab(m + 4, first_tile,
                              defer_vec=(vc if first_tile else None))
                elif preload_next and m == NM - 1:
                    while w3_stores:
                        w3_stores.pop(0)()
                    for mm in range(4):
                        load_slab(mm)

                t_si = P.op("scalar",
                            lambda e, m=m:
                            e.activation(silu_b[:, m % 2, :],
                                         ps_a[:, (m % 2) * 512:(m % 2 + 1) * 512],
                                         AF.Silu, scale=consts[:, 2:3]),
                            waits=[tokg, tr_silu[m % 2].begin_write(), tr_consts.reads()])
                bread((abk,), t_si)
                tr_silu[m % 2].end_write(t_si)
                t_h = P.op("vector",
                           lambda v, m=m:
                           v.scalar_tensor_tensor(h_fm[:, m, :],
                                                  ps_b[:, (m % 2) * 512:(m % 2 + 1) * 512],
                                                  INV127, silu_b[:, m % 2, :],
                                                  ALU.mult, ALU.mult),
                           waits=[t_si, toku, tr_h[m].begin_write()]
                                 + ([tr_stgb[m // 2].begin_write()] if m < 8 else []))
                bread((bbk,), t_h)
                tr_silu[m % 2].add_reader(t_h)
                tr_h[m].end_write(t_h)
                if m == NM - 1 and PRELOAD:
                    P.op("scalar", lambda e: e.preload_act_table(AF.Ln), waits=[t_si])

            while vc:
                vc.pop(0)()
            while w3_stores:
                w3_stores.pop(0)()

        pend_tile = [None]
        for s in range(n_seq):
            phase1(s, first=(s == 0), pre_x=(s > 0))
            if stop_after == "P1":
                dbg_dump(kvbd[:].rearrange("p a b -> p (a b)"), NJ * 2 * (HD + 1),
                         waits=[tr_kvbd.reads()])
                P.flush_block()
                return nc
            prefetched = [True]
            for g in range(n_tiles):
                tidx = s * n_tiles + g
                sl3 = CPT - 1 + (tidx % 2)
                if g + 1 < n_tiles:
                    nx = s * SEQ + (g + 1) * TILE
                elif s + 1 < n_seq:
                    nx = (s + 1) * SEQ + CPT * 128
                else:
                    nx = None
                fj = None
                if tidx == 0:
                    fj = [lambda m=m: load_slab(m, True, both_gp=True)
                          for m in range(4)]
                    slab_state["pre4"] = True
                r2 = phase2(s, g, stop2=stop_after, skip_a=prefetched[0],
                            pend=pend_tile[0], slot3=sl3, next_x=nx,
                            pre_pend=(g == 0), fill_jobs=fj)
                pend_tile[0] = (s, g, sl3)
                prefetched[0] = False
                if r2:
                    P.flush_block()
                    return nc
                if stop_after == "P2":
                    dbg_dump(ln1_bf[:, 0:CPT, :].rearrange("p a b -> p (a b)"),
                             CPT * D, waits=[t.reads() for t in tr_ln1])
                    P.flush_block()
                    return nc
                if g + 1 < n_tiles:
                    prefetched[0] = True
                    phase3(s, g, preload_next=PRELOAD_SLABS)
                else:
                    phase3(s, g, next_kv_pair=(s + 1 < n_seq),
                           preload_next=(PRELOAD_SLABS and s + 1 < n_seq))
        fs, fg, fsl3 = pend_tile[0]
        for sc in range(CPT):
            down_half(sc, 0)
            down_half(sc, 1)
            down_fin(fs, fg, sc, sc if sc < CPT - 1 else fsl3)

        P.sync_wait([(P.dma_sems["out0"], P.dma_count["out0"]),
                     (P.dma_sems["out1"], P.dma_count["out1"])])
        P.flush_block()

    return nc


def _w_img(w_t):
    return np.ascontiguousarray(
        w_t.reshape(NJ, 128, D).transpose(1, 0, 2).reshape(128, NJ * D))


def kernel(L0_attn_x, L0_attn_Wq, L0_attn_Wk, L0_attn_Wv, L0_attn_Wo,
           L0_ffn_w_gate, L0_ffn_w_up, L0_ffn_w_down):
    X = np.ascontiguousarray(np.asarray(L0_attn_x))
    Wq = np.asarray(L0_attn_Wq)
    Wk = np.asarray(L0_attn_Wk)
    Wv = np.asarray(L0_attn_Wv)
    Wo = np.asarray(L0_attn_Wo)
    wg = np.asarray(L0_ffn_w_gate)
    wu = np.asarray(L0_ffn_w_up)
    wd = np.asarray(L0_ffn_w_down)

    imgs = {
        "wq": _w_img(np.ascontiguousarray(Wq.T)),
        "wk": _w_img(np.ascontiguousarray(Wk.T)),
        "wv": _w_img(np.ascontiguousarray(Wv.T)),
        "wo": _w_img(np.ascontiguousarray(Wo.T)),
        "wgi": np.ascontiguousarray(
            wg.T.reshape(NJ, 128, NM, 128).transpose(2, 1, 0, 3).reshape(NM, 128, D)),
        "wui": np.ascontiguousarray(
            wu.T.reshape(NJ, 128, NM, 128).transpose(2, 1, 0, 3).reshape(NM, 128, D)),
        "wdi": np.ascontiguousarray(wd.T.reshape(NM, 128, D)),
    }

    nc = build_program()
    in_maps = []
    for c in range(NCORES):
        im = {"x": np.ascontiguousarray(
            X[NSEQ * c:NSEQ * (c + 1)].reshape(TOKENS, D))}
        im.update(imgs)
        in_maps.append(im)

    resu = run_bass_kernel_spmd(nc, in_maps, core_ids=list(range(NCORES)))
    out = np.empty((16, SEQ, D), dtype=np.float32)
    for c in range(NCORES):
        out[NSEQ * c:NSEQ * (c + 1)] = np.asarray(
            resu.results[c]["out"]).reshape(NSEQ, SEQ, D)
    return out
